# version 11
# baseline (speedup 1.0000x reference)
import os
import numpy as np
import ml_dtypes
import concourse.bass as bass
import concourse.mybir as mybir
from concourse.bass_utils import run_bass_kernel_spmd

F32 = mybir.dt.float32
BF16 = mybir.dt.bfloat16
AF = mybir.ActivationFunctionType
ALU = mybir.AluOpType
AX = mybir.AxisListType
AP = bass.AP

C = 1024
NCORES = 8
SEG = 2048
WARM = 128
TS = SEG + WARM
L = 32
FCH = 8
TT = 256
TILES = [WARM] + [TT] * (SEG // TT)
EPS = 1e-5
SLOTS = 64
HFF = 512
GDIM = 3 * C

DBG = os.environ.get("KDBG", "") == "1"
KMAXT = int(os.environ.get("KMAXT", "99"))
KSKIP2 = os.environ.get("KSKIP2", "") == "1"
KSTAGE = int(os.environ.get("KSTAGE", "99"))
KREV = int(os.environ.get("KREV", "7"))
KSEL = int(os.environ.get("KSEL", "15"))
KCH = int(os.environ.get("KCH", "99"))
KBRK = int(os.environ.get("KBRK", "1"))
KLAG = int(os.environ.get("KLAG", "0"))


def _host_prepare(inputs):
    g = lambda n: np.ascontiguousarray(np.asarray(inputs["L0_attn_" + n], np.float32))
    f = lambda n: np.ascontiguousarray(np.asarray(inputs["L0_ffn_" + n], np.float32))
    x = np.ascontiguousarray(np.asarray(inputs["L0_attn_hidden_states"], np.float32)[0])

    shared = {}
    for n in ("W_r", "W_k", "W_v", "W_o"):
        shared[n] = g(n)
    for n in ("w_lora_a", "w_lora_b", "a_lora_a", "a_lora_b", "g_lora_a", "g_lora_b"):
        shared[n] = g(n)

    def fmcol(v):
        return np.ascontiguousarray(v.reshape(FCH, 128).T)

    for n in ("x_r", "x_w", "x_k", "x_v", "x_a", "k_k", "k_a",
              "w_lora_bias", "a_lora_bias", "g_lora_bias"):
        shared["v_" + n] = fmcol(g(n))
    shared["v_r_k"] = fmcol(g("r_k").reshape(-1))
    shared["v_bo"] = fmcol(f("bo"))
    shared["gnw_rep"] = np.ascontiguousarray(np.broadcast_to(g("gn_weight").reshape(1, C), (128, C)))
    shared["gnb_rep"] = np.ascontiguousarray(np.broadcast_to(g("gn_bias").reshape(1, C), (128, C)))

    w1 = f("w1")
    shared["w1aT"] = np.ascontiguousarray(w1[:, :C].T)
    shared["w1bT"] = np.ascontiguousarray(w1[:, C:].T)
    shared["w2T"] = np.ascontiguousarray(f("w2").T)
    shared["gru_wihT"] = np.ascontiguousarray(f("gru_wih").T)
    shared["gru_whhT"] = np.ascontiguousarray(f("gru_whh").T)
    shared["wvT"] = np.ascontiguousarray(f("wv").T)
    shared["woT"] = np.ascontiguousarray(f("wo").T)
    shared["slot_init"] = f("slot_init")
    shared["v_slot_scale"] = f("slot_scale").reshape(SLOTS, 1)
    shared["v_b1"] = np.ascontiguousarray(f("b1").reshape(4, 128).T)
    shared["b2_rep"] = np.ascontiguousarray(np.broadcast_to(f("b2").reshape(1, SLOTS), (128, SLOTS)))
    shared["bv_rep"] = np.ascontiguousarray(np.broadcast_to(f("bv").reshape(1, C), (SLOTS, C)))
    shared["bih_rep"] = np.ascontiguousarray(np.broadcast_to(f("gru_bih").reshape(1, GDIM), (SLOTS, GDIM)))
    shared["bhh_rep"] = np.ascontiguousarray(np.broadcast_to(f("gru_bhh").reshape(1, GDIM), (SLOTS, GDIM)))
    shared["tau"] = f("tau").reshape(1, 1)

    shared["identity128"] = np.eye(128, dtype=np.float32)
    shared["ones_col"] = np.ones((128, 1), np.float32)
    shared["ones_row"] = np.ones((1, 128), np.float32)
    shared["ones_rowf"] = np.ones((1, 128), np.float32)
    shared["ones_f32"] = np.ones((128, TT), np.float32)
    onespat = np.zeros((128, FCH, 16), np.float32)
    for ch in range(FCH):
        onespat[0:64, ch, 2 * ch] = 1.0
        onespat[64:128, ch, 2 * ch + 1] = 1.0
    shared["onespat"] = np.ascontiguousarray(onespat.reshape(128, FCH * 16))
    sel2 = np.zeros((2, 128), np.float32)
    sel2[0, 0:64] = 1.0
    sel2[1, 64:128] = 1.0
    shared["sel2"] = sel2
    mtrb = np.zeros((32, 64), np.float32)
    for s in range(32):
        for t in range(32):
            mtrb[s, t] = 1.0 if s < t else 0.0
            mtrb[s, 32 + t] = 1.0 if s <= t else 0.0
    shared["mask_trb"] = np.ascontiguousarray(np.tile(np.tile(mtrb, (4, 1)), (1, 8)))
    mw0 = np.tril(np.ones((32, 32), np.float32), -1)
    shared["mask_w0"] = np.ascontiguousarray(np.tile(np.tile(mw0, (4, 1)), (1, 16)))

    per_core = []
    for cid in range(NCORES):
        lo = cid * SEG
        ext = WARM + 1
        if lo - ext >= 0:
            xs = x[lo - ext:lo + SEG]
        else:
            xs = np.concatenate([np.zeros((ext - lo, C), np.float32), x[:lo + SEG]], 0)
        m = dict(shared)
        m["xT"] = np.ascontiguousarray(xs.T)
        per_core.append(m)
    return per_core


class KB:
    ENGS = ("te", "ve", "se", "ge", "sy")
    SEM_OF = {"te": "pe", "ve": "ve", "se": "se", "ge": "geC", "sy": "syC"}

    class Mark:
        __slots__ = ("sem", "val")

        def __init__(self, sem, val):
            self.sem = sem
            self.val = val

    def __init__(self, nc, es):
        self.nc = nc
        self.recs = {k: [] for k in self.ENGS}
        self.semh = {}
        self.semn = {}
        for name in ("pe", "ve", "se", "geC", "syC", "syQ", "geQ", "scQ",
                     "cc", "outQ", "wA", "wB", "oA", "oB", "x2s"):
            self.semh[name] = es.enter_context(nc.semaphore("s_" + name))
            self.semn[name] = 0

    DMA_SEMS = ("syQ", "geQ", "scQ", "outQ", "wA", "wB", "oA", "oB", "x2s")

    def _rec(self, eng, fn, inc, k, deps):
        own = self.SEM_OF.get(eng)
        if eng != "te" and own is not None and self.semn[own] > 0:
            deps = tuple(deps) + (self.Mark(own, self.semn[own]),)
        if inc in self.DMA_SEMS:
            deps = tuple(deps) + (self.Mark(inc, self.semn[inc]),)
        tgt = None
        if inc is not None:
            self.semn[inc] += k
            tgt = self.Mark(inc, self.semn[inc])
        self.recs[eng].append((fn, inc, k, tuple(d for d in deps
                                                 if d is not None)))
        return tgt

    def te(self, fn, deps=()):
        return self._rec("te", fn, "pe", 1, deps)

    def ve(self, fn, deps=()):
        return self._rec("ve", fn, "ve", 1, deps)

    def se(self, fn, deps=()):
        return self._rec("se", fn, "se", 1, deps)

    def ge(self, fn, deps=()):
        return self._rec("ge", fn, "geC", 1, deps)

    def dma_sy(self, fn, deps=(), sem="syQ"):
        return self._rec("sy", fn, sem, 16, deps)

    def dma_ge(self, fn, deps=(), sem="geQ"):
        return self._rec("ge", fn, sem, 16, deps)

    def dma_sc(self, fn, deps=(), sem="scQ"):
        return self._rec("se", fn, sem, 16, deps)

    def dma_out(self, fn, deps=(), sem=None):
        return self._rec("sy", fn, sem or "outQ", 16, deps)

    def barrier(self):
        marks = [self.te(lambda e: e.nop()),
                 self.ve(lambda e: e.nop()),
                 self.se(lambda e: e.nop()),
                 self.ge(lambda e: e.nop())]
        marks += [self.Mark(s, self.semn[s]) for s in
                  ("syQ", "geQ", "scQ", "wA", "wB", "oA", "oB", "x2s")]
        for eng in ("te", "ve", "se", "ge", "sy"):
            self._rec(eng, lambda e: None, None, 0, marks)
        return marks

    @staticmethod
    def _norm(d):
        if isinstance(d, KB.Mark):
            return (d.sem, d.val)
        return d

    def finalize(self, block):
        te_waits = []
        te_real = []
        waited = {}
        pe_seen = 0
        run_len = 0
        prev_bnd = 0
        for fn, inc, k, deps in self.recs["te"]:
            w = []
            for d in deps:
                sem, val = self._norm(d)
                if sem == "pe":
                    continue
                if waited.get(sem, -1) < val:
                    waited[sem] = val
                    w.append((sem, val))
            te_real.append(bool(w))
            if w:
                prev_bnd = pe_seen
                run_len = 0
            else:
                run_len += 1
                if KBRK and prev_bnd > 0 and run_len >= KBRK:
                    w.append(("pe", prev_bnd))
                    run_len = 0
            te_waits.append(w)
            pe_seen += k if inc else 0
        n_te = len(self.recs["te"])
        run_end = [0] * n_te
        cnt = 0
        ends = []
        starts = []
        pe_cnt = [0] * n_te
        c = 0
        for i in range(n_te):
            c += self.recs["te"][i][2] if self.recs["te"][i][1] else 0
            pe_cnt[i] = c
        i = 0
        while i < n_te:
            j = i + 1
            while j < n_te and not te_real[j]:
                j += 1
            for q in range(i, j):
                run_end[q] = pe_cnt[j - 1]
            i = j
        val2idx = {}
        for i in range(n_te):
            val2idx[pe_cnt[i]] = i

        def resolve(sem, val):
            if sem == "pe" and val in val2idx:
                return (sem, run_end[val2idx[val]])
            return (sem, val)

        decs = {"te": block.tensor, "ve": block.vector, "se": block.scalar,
                "ge": block.gpsimd, "sy": block.sync}
        semh = self.semh
        for eng in self.ENGS:
            if eng == "te":
                oplist = []
                for idx, (fn, inc, k, deps) in enumerate(self.recs["te"]):
                    oplist.append((fn, inc, k, tuple(te_waits[idx])))
            else:
                oplist = []
                waited = {}
                for fn, inc, k, deps in self.recs[eng]:
                    w = []
                    for d in deps:
                        sem, val = resolve(*self._norm(d))
                        if waited.get(sem, -1) < val:
                            waited[sem] = val
                            w.append((sem, val))
                    oplist.append((fn, inc, k, tuple(w)))

            def body(e, oplist=oplist):
                for fn, inc, k, waits in oplist:
                    for sname, val in waits:
                        e.wait_ge(semh[sname], val)
                    ins = fn(e)
                    if inc is not None and ins is not None:
                        ins.then_inc(semh[inc], k)
            decs[eng](body)


def sap(buf, p0, npart, col, free=()):
    pitch = int(np.prod(buf.shape[1:]))
    dims = [[pitch, npart]] + [list(d) for d in free]
    return AP(buf, p0 * pitch + col, dims)


def build_program():
    import contextlib
    nc = bass.Bass("TRN2", target_bir_lowering=False)
    es = contextlib.ExitStack()

    din = {}

    def dram_in(name, shape, dt=F32):
        t = nc.dram_tensor(name, list(shape), dt, kind="ExternalInput")
        din[name] = t

    dram_in("xT", (C, TS + 1))
    for n in ("W_r", "W_k", "W_v", "W_o", "wvT", "woT"):
        dram_in(n, (C, C))
    dram_in("w_lora_a", (C, 64)); dram_in("w_lora_b", (64, C))
    dram_in("a_lora_a", (C, 64)); dram_in("a_lora_b", (64, C))
    dram_in("g_lora_a", (C, 128)); dram_in("g_lora_b", (128, C))
    for n in ("x_r", "x_w", "x_k", "x_v", "x_a", "k_k", "k_a",
              "w_lora_bias", "a_lora_bias", "g_lora_bias", "r_k", "bo"):
        dram_in("v_" + n, (128, FCH))
    dram_in("gnw_rep", (128, C)); dram_in("gnb_rep", (128, C))
    dram_in("w1aT", (C, HFF)); dram_in("w1bT", (C, HFF)); dram_in("w2T", (HFF, SLOTS))
    dram_in("gru_wihT", (C, GDIM)); dram_in("gru_whhT", (C, GDIM))
    dram_in("slot_init", (SLOTS, C)); dram_in("v_slot_scale", (SLOTS, 1))
    dram_in("v_b1", (128, 4)); dram_in("b2_rep", (128, SLOTS))
    dram_in("bv_rep", (SLOTS, C))
    dram_in("bih_rep", (SLOTS, GDIM)); dram_in("bhh_rep", (SLOTS, GDIM))
    dram_in("tau", (1, 1))
    dram_in("identity128", (128, 128)); dram_in("ones_col", (128, 1))
    dram_in("ones_row", (1, 128)); dram_in("ones_rowf", (1, 128))
    dram_in("ones_f32", (128, TT))
    dram_in("onespat", (128, FCH * 16)); dram_in("sel2", (2, 128))
    dram_in("mask_trb", (128, 512)); dram_in("mask_w0", (128, 512))

    def dap(name, col, dims):
        return AP(din[name], col, dims)

    out_fm = nc.dram_tensor("out_fm", [C, SEG], F32, kind="ExternalOutput")
    dbg = {}

    def dbg_t(name, shape, dt=F32):
        if DBG and name not in dbg:
            dbg[name] = nc.dram_tensor("dbg_" + name, list(shape), dt,
                                       kind="ExternalOutput")
        return dbg.get(name)

    wsc = {n: nc.dram_tensor("wsc_" + n, [C, C], BF16, kind="Internal")
           for n in ("W_r", "W_k", "W_v")}
    cc_in = nc.dram_tensor("cc_in", [SLOTS, C + 16], F32, kind="Internal").ap().opt()
    cc_out = nc.dram_tensor("cc_out", [SLOTS, C + 16], F32, kind="Internal").ap().opt()
    x2_hbm = nc.dram_tensor("x2_hbm", [C, SEG], F32, kind="Internal")

    with es:
        T = KB(nc, es)

        def sbuf(name, shape, dt):
            return es.enter_context(nc.sbuf_tensor(name, list(shape), dt))

        k_id = sbuf("k_id", (128, 128), BF16)
        k_onescol = sbuf("k_onescol", (128, 1), BF16)
        k_onescolf = sbuf("k_onescolf", (128, 1), F32)
        k_onesrow = sbuf("k_onesrow", (1, 128), BF16)
        k_onesrowf = sbuf("k_onesrowf", (1, 128), F32)
        k_onesf32 = sbuf("k_onesf32", (128, TT), F32)
        k_onespat = sbuf("k_onespat", (128, FCH * 16), BF16)
        k_sel2 = sbuf("k_sel2", (2, 128), BF16)
        k_mtrb = sbuf("k_mtrb", (128, 512), BF16)
        k_mw0 = sbuf("k_mw0", (128, 512), BF16)
        vec = {}
        for n in ("x_r", "x_w", "x_k", "x_v", "x_a", "k_k", "k_a",
                  "w_lora_bias", "a_lora_bias", "g_lora_bias", "r_k", "bo"):
            vec[n] = sbuf("vec_" + n, (128, FCH), F32)
        v_omka = sbuf("v_omka", (128, FCH), F32)
        v_negwb = sbuf("v_negwb", (128, FCH), F32)
        gw_tm = sbuf("gw_tm", (128, C), BF16)
        gb_tm = sbuf("gb_tm", (128, C), BF16)
        k_consts = sbuf("k_consts", (128, 4), F32)
        wo_in = sbuf("wo_in", (128, FCH * SEG), BF16)

        ps = [es.enter_context(nc.psum_tensor("ps%d" % i, [128, 512], F32))
              for i in range(8)]

        dep_c = []
        dep_c.append(T.dma_sy(lambda e: e.dma_start(k_onesf32[:], din["ones_f32"].ap())))
        dep_c.append(T.dma_sy(lambda e: e.dma_start(k_onesrowf[:], din["ones_rowf"].ap())))
        dep_c.append(T.dma_sy(lambda e: e.dma_start(k_onescolf[:],
                                                    AP(din["ones_col"], 0, [[1, 128], [1, 1]]))))
        vdeps = {}
        for n in vec:
            vdeps[n] = T.dma_sy(lambda e, n=n: e.dma_start(vec[n][:], din["v_" + n].ap()))
        for name, buf in (("identity128", k_id), ("ones_col", k_onescol),
                          ("ones_row", k_onesrow), ("onespat", k_onespat),
                          ("sel2", k_sel2), ("mask_trb", k_mtrb), ("mask_w0", k_mw0)):
            dep_c.append(T.dma_ge(lambda e, b=buf, nm=name: e.dma_start(b[:], din[nm].ap())))
        T.ve(lambda e: e.memset(k_consts[:, 0:1], EPS))
        T.ve(lambda e: e.memset(k_consts[:, 1:2], -1.0))
        T.ve(lambda e: e.memset(k_consts[:, 2:3], -0.5))
        nc.const_aps.register(F32, EPS, k_consts[:, 0:1])
        nc.const_aps.register(F32, -1.0, k_consts[:, 1:2])
        nc.const_aps.register(F32, -0.5, k_consts[:, 2:3])
        m_omka = T.ve(lambda e: e.tensor_scalar(v_omka[:], vec["k_a"][:], -1.0, 1.0,
                                                ALU.mult, ALU.add), deps=(vdeps["k_a"],))
        m_negwb = T.ve(lambda e: e.tensor_scalar_mul(v_negwb[:], vec["w_lora_bias"][:],
                                                     -1.0), deps=(vdeps["w_lora_bias"],))
        T.barrier()

        p1 = contextlib.ExitStack()
        with p1:
            def sb1(name, shape, dt):
                return p1.enter_context(nc.sbuf_tensor(name, list(shape), dt))

            w_wla = sb1("w_wla", (128, FCH * 64), BF16)
            w_wlb = sb1("w_wlb", (64, C), BF16)
            w_ala = sb1("w_ala", (128, FCH * 64), BF16)
            w_alb = sb1("w_alb", (64, C), BF16)
            w_gla = sb1("w_gla", (128, FCH * 128), BF16)
            w_glb = sb1("w_glb", (128, C), BF16)
            w_st = sb1("w_st", (128, 2 * C), BF16)

            z_f32 = sb1("z_f32", (128, FCH * 64), F32)
            z_bf = sb1("z_bf", (128, FCH * 64), BF16)
            g_carry = sb1("g_carry", (128, FCH), F32)

            NW1 = TT + 1
            xf = sb1("xf", (128, FCH * NW1), F32)
            ln_f = sb1("ln_f", (128, FCH * NW1), F32)
            cb_f = sb1("cb_f", (128, 2 * NW1), F32)
            dd_b = sb1("dd_b", (128, FCH * TT), BF16)
            lrp = sb1("lrp", (128, FCH * TT), BF16)
            r_b = sb1("r_b", (128, FCH * TT), BF16)
            v_b = sb1("v_b", (128, FCH * TT), BF16)
            ah_f = sb1("ah_f", (128, FCH * TT), F32)
            kk_f = sb1("kk_f", (128, FCH * TT), F32)
            sp_f = sb1("sp_f", (128, FCH * TT), F32)
            g_f = sb1("g_f", (128, FCH * TT), F32)
            gbase = sb1("gbase", (128, FCH * 8), F32)
            clc_f = sb1("clc_f", (128, FCH * 8), F32)
            d1_f = sb1("d1_f", (128, FCH * TT), F32)
            bk_b = sb1("bk_b", (128, 2 * FCH * TT), BF16)
            ar_b = sb1("ar_b", (128, 2 * FCH * TT), BF16)
            bh_b = sb1("bh_b", (128, FCH * TT), BF16)
            kh_b = sb1("kh_b", (128, FCH * TT), BF16)
            gate_b = sb1("gate_b", (128, FCH * TT), BF16)
            bonus_b = sb1("bonus_b", (128, FCH * TT), BF16)
            scr_b = sb1("scr_b", (128, FCH * NW1), BF16)
            mid_b = sb1("mid_b", (128, TT), BF16)
            row_f = sb1("row_f", (1, 4 * NW1), F32)
            row_c0 = sb1("row_c0", (1, NW1), F32)
            row_c1 = sb1("row_c1", (1, NW1), F32)
            nrm_b = sb1("nrm_b", (2, FCH * TT), BF16)
            stm = sb1("stm", (128, 3 * 1024), BF16)
            strb = sb1("strb", (128, 1024), BF16)
            smrk = sb1("smrk", (128, 1024), BF16)
            sw0 = sb1("sw0", (128, 512), BF16)
            ssq = sb1("ssq", (128, 7 * 512), BF16)
            sxp_f = [sb1("sxpf0", (128, 1024), F32), sb1("sxpf1", (128, 1024), F32)]
            sxp_b = sb1("sxp_b", (128, 1024), BF16)
            o_tm = xf
            gnst = sb1("gnst", (128, 6 * 16), F32)

            def fmS(buf, ch, col, n, width=TT):
                return sap(buf, 0, 128, ch * width + col, [[1, n]])

            def fm3(buf, col, n, width=TT, nch=FCH, base=0):
                return sap(buf, 0, 128, base + col, [[width, nch], [1, n]])

            def ldw_small(dst, nm, mcols):
                return T.dma_ge(lambda e, d=dst, nm=nm, m=mcols: e.dma_start(
                    sap(d, 0, 128, 0, [[m, FCH], [1, m]]),
                    dap(nm, 0, [[m, 128], [128 * m, FCH], [1, m]])))

            ldw_small(w_wla, "w_lora_a", 64)
            ldw_small(w_ala, "a_lora_a", 64)
            ldw_small(w_gla, "g_lora_a", 128)
            T.dma_ge(lambda e: e.dma_start(w_wlb[:], din["w_lora_b"].ap()))
            T.dma_ge(lambda e: e.dma_start(w_alb[:], din["a_lora_b"].ap()))
            T.dma_ge(lambda e: e.dma_start(w_glb[:], din["g_lora_b"].ap()))
            d_lora = ("geQ", T.semn["geQ"])
            d_wla = d_ala = d_gla = d_wlb = d_alb = d_glb = d_lora

            T.ve(lambda e: e.memset(z_f32[:], 0.0))
            m_zb0 = T.ve(lambda e: e.memset(z_bf[:], 0.0))
            m_zi = T.ve(lambda e: e.memset(g_carry[:], 0.0))

            wl_marks = [None, None]
            
            wc_marks = [None, None]
            ws_marks = [None, None]
            blocks = [(nm, k) for nm in ("W_k", "W_r", "W_v") for k in range(FCH)]
            for i, (nm, k) in enumerate(blocks):
                h = i % 2
                ldep = (wc_marks[h],) if wc_marks[h] else ()
                wl_marks[h] = T.dma_sy(
                    lambda e, nm=nm, k=k, h=h: e.dma_start(
                        sap(xf, 0, 128, h * 1024, [[1, C]]),
                        dap(nm, k * 128 * C, [[C, 128], [1, C]])),
                    deps=ldep, sem=("wA", "wB")[h])
                cdep = (wl_marks[h],) + ((ws_marks[h],) if ws_marks[h] else ())
                wc_marks[h] = T.se(lambda e, h=h: e.copy(
                    sap(w_st, 0, 128, h * 1024, [[1, C]]),
                    sap(xf, 0, 128, h * 1024, [[1, C]])), deps=cdep)
                ws_marks[h] = T.dma_sc(
                    lambda e, nm=nm, k=k, h=h: e.dma_start(
                        AP(wsc[nm], k * 128 * C, [[C, 128], [1, C]]),
                        sap(w_st, 0, 128, h * 1024, [[1, C]])),
                    deps=(wc_marks[h],), sem=("oA", "oB")[h])
            m_wsc = (("oA", T.semn["oA"]), ("oB", T.semn["oB"]),
                     ("wA", T.semn["wA"]), ("wB", T.semn["wB"]))
            d_gnw = T.dma_sy(lambda e: e.dma_start(
                sap(xf, 0, 128, 0, [[1, C]]), din["gnw_rep"].ap()),
                deps=(wc_marks[0],), sem="wA")
            d_gnb = T.dma_sc(lambda e: e.dma_start(
                sap(xf, 0, 128, 1024, [[1, C]]), din["gnb_rep"].ap()),
                deps=(wc_marks[1],), sem="wB")
            T.se(lambda e: e.copy(gw_tm[:], sap(xf, 0, 128, 0, [[1, C]])),
                 deps=(d_gnw, ws_marks[0]))
            m_gnbc = T.se(lambda e: e.copy(gb_tm[:], sap(xf, 0, 128, 1024, [[1, C]])),
                          deps=(d_gnb, ws_marks[1]))

            scan_t0 = 0
            prev_zb = m_zb0
            prev_gn_add = None
            for ti, TTi in enumerate(TILES[:KMAXT]):
                is_warm = (ti == 0)
                ncht = TTi // L
                NW = TTi + 1

                m_xf = T.dma_sc(lambda e, t0=scan_t0, n=NW: e.dma_start(
                    fm3(xf, 0, n, width=NW1),
                    dap("xT", t0, [[TS + 1, 128], [128 * (TS + 1), FCH], [1, n]])),
                    deps=(m_gnbc,) if ti == 0 else ())

                for ch in range(FCH):
                    mm = T.te(lambda e, ch=ch, n=NW: e.matmul(
                        ps[4][0:1, 0:n], k_onescolf[:], fmS(xf, ch, 0, n, width=NW1),
                        start=(ch == 0), stop=(ch == FCH - 1)),
                        deps=(m_xf,) if ch == 0 else ())
                m_sq = T.se(lambda e, n=NW: e.activation(
                    sap(scr_b, 0, 128, 0, [[1, FCH * n]]) if n == NW1 else
                    fm3(scr_b, 0, n, width=NW1),
                    sap(xf, 0, 128, 0, [[1, FCH * n]]) if n == NW1 else
                    fm3(xf, 0, n, width=NW1),
                    AF.Square), deps=(m_xf,))
                for ch in range(FCH):
                    mm = T.te(lambda e, ch=ch, n=NW: e.matmul(
                        ps[5][0:1, 0:n], k_onescol[:], fmS(scr_b, ch, 0, n, width=NW1),
                        start=(ch == 0), stop=(ch == FCH - 1)),
                        deps=(m_sq,) if ch == 0 else ())
                m_stat = mm
                T.ve(lambda e, n=NW: e.tensor_scalar_mul(AP(row_f, 0 * NW1, [[4 * NW1, 1], [1, n]]),
                                                         ps[4][0:1, 0:n], 1.0 / C),
                     deps=(m_stat,))
                T.ve(lambda e, n=NW: e.tensor_mul(AP(row_f, 1 * NW1, [[4 * NW1, 1], [1, n]]), AP(row_f, 0 * NW1, [[4 * NW1, 1], [1, n]]),
                                                  AP(row_f, 0 * NW1, [[4 * NW1, 1], [1, n]])))
                m_var = T.ve(lambda e, n=NW: e.scalar_tensor_tensor(
                    AP(row_f, 2 * NW1, [[4 * NW1, 1], [1, n]]), ps[5][0:1, 0:n], 1.0 / C, AP(row_f, 1 * NW1, [[4 * NW1, 1], [1, n]]),
                    ALU.mult, ALU.subtract))
                m_sd = T.se(lambda e, n=NW: e.activation(AP(row_f, 3 * NW1, [[4 * NW1, 1], [1, n]]),
                                                         AP(row_f, 2 * NW1, [[4 * NW1, 1], [1, n]]), AF.Sqrt,
                                                         bias=EPS), deps=(m_var,))
                T.ve(lambda e, n=NW: e.reciprocal(AP(row_f, 2 * NW1, [[4 * NW1, 1], [1, n]]), AP(row_f, 3 * NW1, [[4 * NW1, 1], [1, n]])),
                     deps=(m_sd,))
                T.ve(lambda e, n=NW: e.tensor_copy(row_c1[0:1, 0:n], AP(row_f, 2 * NW1, [[4 * NW1, 1], [1, n]])))
                m_rows = T.ve(lambda e, n=NW: e.scalar_tensor_tensor(
                    row_c0[0:1, 0:n], AP(row_f, 0 * NW1, [[4 * NW1, 1], [1, n]]), -1.0, AP(row_f, 2 * NW1, [[4 * NW1, 1], [1, n]]),
                    ALU.mult, ALU.mult))
                T.te(lambda e, n=NW: e.matmul(ps[6][:, 0:n], k_onesrowf[0:1, :],
                                              row_c0[0:1, 0:n], start=True, stop=True),
                     deps=(m_rows,))
                m_bc = T.te(lambda e, n=NW: e.matmul(ps[7][:, 0:n], k_onesrowf[0:1, :],
                                                     row_c1[0:1, 0:n],
                                                     start=True, stop=True))
                T.se(lambda e, n=NW: e.copy(sap(cb_f, 0, 128, 0, [[1, n]]),
                                            ps[6][:, 0:n]), deps=(m_bc,))
                m_cb = T.se(lambda e, n=NW: e.copy(sap(cb_f, 0, 128, NW1, [[1, n]]),
                                                   ps[7][:, 0:n]))
                T.ve(lambda e, n=NW: e.tensor_mul(
                    fm3(ln_f, 0, n, width=NW1), fm3(xf, 0, n, width=NW1),
                    sap(cb_f, 0, 128, NW1, [[0, FCH], [1, n]])), deps=(m_cb,))
                m_ln = T.ve(lambda e, n=NW: e.tensor_add(
                    fm3(ln_f, 0, n, width=NW1), fm3(ln_f, 0, n, width=NW1),
                    sap(cb_f, 0, 128, 0, [[0, FCH], [1, n]])))
                m_dd = T.ve(lambda e, n=TTi: e.tensor_sub(
                    fm3(dd_b, 0, n), fm3(ln_f, 0, n, width=NW1),
                    fm3(ln_f, 1, n, width=NW1)))

                def lerp(nm, extra=()):
                    mk = None
                    for ch in range(FCH):
                        mk = T.ve(lambda e, ch=ch, n=TTi, nm=nm: e.scalar_tensor_tensor(
                            fmS(lrp, ch, 0, n), fmS(dd_b, ch, 0, n),
                            sap(vec[nm], 0, 128, ch, [[1, 1]]),
                            sap(ln_f, 0, 128, ch * NW1 + 1, [[1, n]]),
                            ALU.mult, ALU.add),
                            deps=((m_dd, vdeps[nm]) + tuple(extra)) if ch == 0 else ())
                    return mk

                def proj(wname, m_lrp, evac_fn, extra=()):
                    d_w = [None, None]
                    d_w[0] = T.dma_sy(lambda e, nm=wname: e.dma_start(
                        sap(w_st, 0, 128, 0, [[1, C]]),
                        AP(wsc[nm], 0, [[C, 128], [1, C]])),
                        deps=(m_lrp,) + m_wsc, sem="wA")
                    mmk = None
                    for kch in range(FCH):
                        if kch + 1 < FCH:
                            dmaf = T.dma_sy if (kch + 1) % 2 == 0 else T.dma_sc
                            d_w[(kch + 1) % 2] = dmaf(
                                lambda e, nm=wname, k=kch + 1: e.dma_start(
                                    sap(w_st, 0, 128, (k % 2) * C, [[1, C]]),
                                    AP(wsc[nm], k * 128 * C, [[C, 128], [1, C]])),
                                deps=((mmk,) if kch >= 1 else m_wsc),
                                sem=("wA", "wB")[(kch + 1) % 2])
                        for mch in range(FCH):
                            bank = ps[mch]
                            dep = ()
                            if mch == 0:
                                dep = (d_w[kch % 2], m_lrp)
                                if kch == 0:
                                    dep = dep + tuple(extra)
                            mmk = T.te(lambda e, m=mch, k=kch, b=bank, n=TTi:
                                       e.matmul(
                                b[:, 0:n],
                                sap(w_st, 0, 128, (k % 2) * C + m * 128, [[1, 128]]),
                                fmS(lrp, k, 0, n),
                                start=(k == 0), stop=(k == FCH - 1)), deps=dep)
                    marks = []
                    for mch in range(FCH):
                        marks.append(evac_fn(mch, ps[mch],
                                             (mmk,) if mch == 0 else ()))
                    return marks

                if KSTAGE >= 12:
                    m_lrp = lerp("x_w")
                    mmk = None
                    for kch in range(FCH):
                        dep = (m_lrp, d_wla) if kch == 0 else ()
                        mmk = T.te(lambda e, k=kch, n=TTi: e.matmul(
                            ps[6][0:64, 0:n], sap(w_wla, 0, 128, k * 64, [[1, 64]]),
                            fmS(lrp, k, 0, n), start=(k == 0), stop=(k == FCH - 1)), deps=dep)
                    m_wmid = T.se(lambda e, n=TTi: e.activation(mid_b[0:64, 0:n],
                                                                ps[6][0:64, 0:n], AF.Tanh),
                                  deps=(mmk,))
                    ev_w = None
                    evl = []
                    for mch in range(FCH):
                        bank = ps[6 + mch % 2]
                        dep = (m_wmid, d_wlb, m_negwb) if mch == 0 else ()
                        if mch >= 2:
                            dep = dep + (evl[mch - 2],)
                        mmk = T.te(lambda e, m=mch, b=bank, n=TTi: e.matmul(
                            b[:, 0:n], sap(w_wlb, 0, 64, m * 128, [[1, 128]]),
                            mid_b[0:64, 0:n], start=True, stop=True), deps=dep)
                        ev_x = T.se(lambda e, m=mch, b=bank, n=TTi: e.activation(
                            fmS(g_f, m, 0, n), b[:, 0:n], AF.Exp,
                            bias=sap(v_negwb, 0, 128, m, [[1, 1]]), scale=-1.0),
                            deps=(mmk,))
                        evl.append(ev_x)
                        ev_w = T.se(lambda e, m=mch, n=TTi: e.activation(
                            fmS(sp_f, m, 0, n), fmS(g_f, m, 0, n),
                            AF.Ln, bias=1.0))
                    m_w = T.ve(lambda e, n=TTi: e.tensor_scalar(
                        fm3(sp_f, 0, n), fm3(sp_f, 0, n), -1.0, -0.5, ALU.mult, ALU.add),
                        deps=(ev_w,))


                if KSTAGE >= 13:
                    mG = None
                    for ch in range(FCH):
                        dep = (m_w, m_zi) if ch == 0 else ()
                        mG = T.ve(lambda e, ch=ch, n=TTi: e.tensor_tensor_scan(
                            fmS(g_f, ch, 0, n), k_onesf32[:, 0:n], fmS(sp_f, ch, 0, n),
                            sap(g_carry, 0, 128, ch, [[1, 1]]), ALU.mult, ALU.add), deps=dep)
                    T.ve(lambda e: e.tensor_sub(
                        sap(gbase, 0, 128, 0, [[8, FCH], [1, 1]]),
                        sap(g_f, 0, 128, 0, [[TT, FCH], [1, 1]]),
                        sap(sp_f, 0, 128, 0, [[TT, FCH], [1, 1]])))
                    if ncht > 1:
                        m_gb = T.ve(lambda e, nj=ncht: e.tensor_copy(
                            sap(gbase, 0, 128, 1, [[8, FCH], [1, nj - 1]]),
                            sap(g_f, 0, 128, L - 1, [[TT, FCH], [L, nj - 1]])))
                    else:
                        m_gb = mG
                    m_gc = T.ve(lambda e, n=TTi: e.tensor_copy(
                        g_carry[:, 0:FCH],
                        sap(g_f, 0, 128, n - 1, [[TT, FCH], [1, 1]])))
                    m_a1 = T.ve(lambda e, nj=ncht: e.tensor_sub(
                        sap(sp_f, 0, 128, 0, [[TT, FCH], [L, nj], [1, L]]),
                        sap(g_f, 0, 128, 0, [[TT, FCH], [L, nj], [1, L]]),
                        sap(gbase, 0, 128, 0, [[8, FCH], [1, nj], [0, L]])))
                    m_d1 = T.se(lambda e, n=TTi: e.activation(fm3(d1_f, 0, n),
                                                              fm3(sp_f, 0, n), AF.Exp),
                                deps=(m_a1,))
                    m_clc = T.se(lambda e, nj=ncht: e.activation(
                        sap(clc_f, 0, 128, 0, [[8, FCH], [1, nj]]),
                        sap(sp_f, 0, 128, L - 1, [[TT, FCH], [L, nj]]), AF.Exp))
                    m_d3 = T.ve(lambda e, n=TTi: e.reciprocal(fm3(sp_f, 0, n),
                                                              fm3(d1_f, 0, n)),
                                deps=(m_d1, m_clc))


                if KSTAGE >= 14:
                    m_lrp = lerp("x_a", extra=(m_w,))
                    mmk = None
                    for kch in range(FCH):
                        dep = (m_lrp, d_ala) if kch == 0 else ()
                        mmk = T.te(lambda e, k=kch, n=TTi: e.matmul(
                            ps[6][0:64, 0:n], sap(w_ala, 0, 128, k * 64, [[1, 64]]),
                            fmS(lrp, k, 0, n), start=(k == 0), stop=(k == FCH - 1)), deps=dep)
                    m_amid = T.se(lambda e, n=TTi: e.copy(mid_b[0:64, 0:n], ps[6][0:64, 0:n]),
                                  deps=(mmk, m_d1))
                    ev_a = None
                    evl = []
                    for mch in range(FCH):
                        bank = ps[6 + mch % 2]
                        dep = (m_amid, d_alb) if mch == 0 else ()
                        if mch >= 2:
                            dep = dep + (evl[mch - 2],)
                        mmk = T.te(lambda e, m=mch, b=bank, n=TTi: e.matmul(
                            b[:, 0:n], sap(w_alb, 0, 64, m * 128, [[1, 128]]),
                            mid_b[0:64, 0:n], start=True, stop=True), deps=dep)
                        ev_a = T.se(lambda e, m=mch, b=bank, n=TTi: e.activation(
                            fmS(ah_f, m, 0, n), b[:, 0:n], AF.Sigmoid,
                            bias=sap(vec["a_lora_bias"], 0, 128, m, [[1, 1]])), deps=(mmk,))
                        evl.append(ev_a)
                    m_ah = ev_a

                    m_kms = None
                    for ch in range(FCH):
                        dep = (m_ah, m_omka) if ch == 0 else ()
                        m_kms = T.ve(lambda e, ch=ch, n=TTi: e.tensor_scalar(
                            fmS(scr_b, ch, 0, n, width=NW1), fmS(ah_f, ch, 0, n),
                            sap(vec["k_a"], 0, 128, ch, [[1, 1]]),
                            sap(v_omka, 0, 128, ch, [[1, 1]]), ALU.mult, ALU.add), deps=dep)


                if KSTAGE >= 15:
                    m_lrp = lerp("x_k", extra=(m_kms,))

                    def evac_k(mch, bank, dep0):
                        dep = dep0 + ((m_kms, vdeps["k_k"], m_d3) if mch == 0 else ())
                        T.ve(lambda e, m=mch, b=bank, n=TTi: e.tensor_scalar_mul(
                            fmS(kk_f, m, 0, n), b[:, 0:n],
                            sap(vec["k_k"], 0, 128, m, [[1, 1]])), deps=dep)
                        return T.ve(lambda e, m=mch, b=bank, n=TTi: e.tensor_mul(
                            fmS(g_f, m, 0, n), b[:, 0:n],
                            fmS(scr_b, m, 0, n, width=NW1)))
                    m_kev = proj("W_k", m_lrp, evac_k, extra=(m_ah,))
                    m_k = m_kev[-1]

                    m_kb2 = T.ve(lambda e, n=TTi: e.tensor_mul(
                        fm3(g_f, 0, n), fm3(g_f, 0, n), fm3(sp_f, 0, n)),
                        deps=(m_k, m_d3))
                    m_bkF = T.se(lambda e, n=TTi: e.copy(
                        fm3(bk_b, 0, n, base=FCH * TT), fm3(g_f, 0, n)), deps=(m_kb2,))
                    m_kh = T.ve(lambda e, nj=ncht: e.tensor_mul(
                        sap(kh_b, 0, 128, 0, [[TT, FCH], [L, nj], [1, L]]),
                        sap(g_f, 0, 128, 0, [[TT, FCH], [L, nj], [1, L]]),
                        sap(d1_f, 0, 128, L - 1, [[TT, FCH], [L, nj], [0, L]])),
                        deps=(m_kb2,))


                if KSTAGE >= 16:
                    m_ksq = T.se(lambda e, n=TTi: e.activation(fm3(scr_b, 0, n, width=NW1),
                                                               fm3(kk_f, 0, n), AF.Square),
                                 deps=(m_k, m_kh))
                    m_nrm = None
                    for ch in range(FCH):
                        mm = T.te(lambda e, ch=ch, n=TTi: e.matmul(
                            ps[6][0:2, 0:n],
                            sap(k_onespat, 0, 128, 0, [[1, 2]]),
                            fmS(scr_b, ch, 0, n, width=NW1),
                            start=True, stop=True),
                            deps=(m_ksq,) if ch == 0 else ((m_nrm,) if m_nrm else ()))
                        m_nrm = T.se(lambda e, ch=ch, n=TTi: e.activation(
                            sap(nrm_b, 0, 2, ch * TT, [[1, n]]),
                            ps[6][0:2, 0:n], AF.Sqrt), deps=(mm,))
                    T.ve(lambda e, n=TTi: e.tensor_scalar_max(
                        sap(nrm_b, 0, 2, 0, [[TT, FCH], [1, n]]),
                        sap(nrm_b, 0, 2, 0, [[TT, FCH], [1, n]]), 1e-12), deps=(m_nrm,))
                    T.ve(lambda e, n=TTi: e.reciprocal(
                        sap(nrm_b, 0, 2, 0, [[TT, FCH], [1, n]]),
                        sap(nrm_b, 0, 2, 0, [[TT, FCH], [1, n]])))
                    m_ninv = T.ve(lambda e, n=TTi: e.tensor_scalar_mul(
                        sap(nrm_b, 0, 2, 0, [[TT, FCH], [1, n]]),
                        sap(nrm_b, 0, 2, 0, [[TT, FCH], [1, n]]), -1.0))
                    mkk = None
                    for ch in range(FCH):
                        dep = (m_ninv,) if ch == 0 else (mkk,)
                        mmb = T.te(lambda e, ch=ch, n=TTi: e.matmul(
                            ps[7][:, 0:n], k_sel2[:],
                            sap(nrm_b, 0, 2, ch * TT, [[1, n]]),
                            start=True, stop=True), deps=dep)
                        mkk = T.ve(lambda e, ch=ch, n=TTi: e.tensor_mul(
                            fmS(kk_f, ch, 0, n), fmS(kk_f, ch, 0, n), ps[7][:, 0:n]),
                            deps=(mmb,))
                    m_kkneg = mkk

                    FT = FCH * TT
                    m_ab1 = T.ve(lambda e, nj=ncht: e.tensor_mul(
                        sap(ar_b, 0, 128, 1, [[TT, FCH], [L, nj], [1, L - 1]]),
                        sap(kk_f, 0, 128, 1, [[TT, FCH], [L, nj], [1, L - 1]]),
                        sap(d1_f, 0, 128, 0, [[TT, FCH], [L, nj], [1, L - 1]])),
                        deps=(m_kkneg,))
                    m_ab2 = T.ve(lambda e, nj=ncht: e.tensor_copy(
                        sap(ar_b, 0, 128, 0, [[TT, FCH], [L, nj], [1, 1]]),
                        sap(kk_f, 0, 128, 0, [[TT, FCH], [L, nj], [1, 1]])))
                    m_bb1 = T.ve(lambda e, n=TTi: e.tensor_mul(
                        fm3(g_f, 0, n), fm3(kk_f, 0, n), fm3(ah_f, 0, n)),
                        deps=(m_kkneg, m_bkF))
                    m_bb2 = T.ve(lambda e, n=TTi: e.scalar_tensor_tensor(
                        fm3(g_f, 0, n), fm3(g_f, 0, n), -1.0,
                        fm3(sp_f, 0, n), ALU.mult, ALU.mult))
                    m_bk1 = T.se(lambda e, n=TTi: e.copy(
                        fm3(bk_b, 0, n), fm3(g_f, 0, n)), deps=(m_bb2,))
                    m_bh = T.ve(lambda e, nj=ncht: e.tensor_mul(
                        sap(bh_b, 0, 128, 0, [[TT, FCH], [L, nj], [1, L]]),
                        sap(g_f, 0, 128, 0, [[TT, FCH], [L, nj], [1, L]]),
                        sap(d1_f, 0, 128, L - 1, [[TT, FCH], [L, nj], [0, L]])),
                        deps=(m_bb2,))


                if KSTAGE >= 17:
                    m_lrp = lerp("x_r", extra=(m_bh,))

                    r_se = []

                    def evac_r(mch, bank, dep0):
                        dep = dep0 + ((m_bh, m_bk1, vdeps["r_k"]) if mch == 0 else ())
                        T.ve(lambda e, m=mch, b=bank, n=TTi: e.tensor_mul(
                            sap(ar_b, 0, 128, FT + m * TT, [[1, n]]), b[:, 0:n],
                            fmS(d1_f, m, 0, n)), deps=dep)
                        mvst = T.ve(lambda e, m=mch, b=bank, n=TTi: e.scalar_tensor_tensor(
                            fmS(g_f, m, 0, n), b[:, 0:n],
                            sap(vec["r_k"], 0, 128, m, [[1, 1]]),
                            sap(bk_b, 0, 128, FT + m * TT, [[1, n]]),
                            ALU.mult, ALU.mult))
                        r_se.append(T.se(lambda e, m=mch, b=bank, n=TTi: e.copy(
                            fmS(r_b, m, 0, n), b[:, 0:n]), deps=(mvst,)))
                        return mvst
                    m_rev = proj("W_r", m_lrp, evac_r, extra=(m_kkneg, m_nrm))
                    m_r = m_rev[-1]

                    if KSTAGE >= 18:
                        mmk = None
                        for kch in range(FCH):
                            dep = (m_r, r_se[-1], d_gla) if kch == 0 else ()
                            mmk = T.te(lambda e, k=kch, n=TTi: e.matmul(
                                ps[4][:, 0:n], sap(w_gla, 0, 128, k * 128, [[1, 128]]),
                                fmS(r_b, k, 0, n), start=(k == 0), stop=(k == FCH - 1)),
                                deps=dep)
                        m_gmid = T.se(lambda e, n=TTi: e.activation(mid_b[:, 0:n], ps[4][:, 0:n],
                                                                    AF.Sigmoid), deps=(mmk,))
                        ev_g = None
                        evl = []
                        for mch in range(FCH):
                            bank = ps[4 + mch % 2]
                            dep = (m_gmid, d_glb) if mch == 0 else ()
                            if mch >= 2:
                                dep = dep + (evl[mch - 2],)
                            mmk = T.te(lambda e, m=mch, b=bank, n=TTi: e.matmul(
                                b[:, 0:n], sap(w_glb, 0, 128, m * 128, [[1, 128]]),
                                mid_b[:, 0:n], start=True, stop=True), deps=dep)
                            ev_g = T.se(lambda e, m=mch, b=bank, n=TTi: e.activation(
                                fmS(gate_b, m, 0, n), b[:, 0:n], AF.Identity,
                                bias=sap(vec["g_lora_bias"], 0, 128, m, [[1, 1]])), deps=(mmk,))
                            evl.append(ev_g)
                        m_gate = ev_g


                if KSTAGE >= 18 and KSEL & 2:
                    m_b1 = T.ve(lambda e, n=TTi: e.tensor_mul(
                        fm3(g_f, 0, n), fm3(g_f, 0, n), fm3(d1_f, 0, n)),
                        deps=(m_r,))
                    m_bcast = T.se(lambda e, n=TTi: e.activation(
                        fm3(scr_b, 0, n, width=NW1), fm3(g_f, 0, n), AF.Identity),
                        deps=(m_b1, m_gate))
                    m_bs = None
                    if KSEL & 4:
                        for ch in range(FCH):
                            mm = T.te(lambda e, ch=ch, n=TTi: e.matmul(
                                ps[6][0:2, 0:n], sap(k_onespat, 0, 128, 0, [[1, 2]]),
                                fmS(scr_b, ch, 0, n, width=NW1), start=True, stop=True),
                                deps=(m_bcast,) if ch == 0 else ((m_bs,) if m_bs else ()))
                            m_bs = T.se(lambda e, ch=ch, n=TTi: e.copy(
                                sap(nrm_b, 0, 2, ch * TT, [[1, n]]), ps[6][0:2, 0:n]),
                                deps=(mm,))
                    m_bse = None
                    if KSEL & 8:
                        for ch in range(FCH):
                            dep = (m_bs,) if ch == 0 else ()
                            mmb = T.te(lambda e, ch=ch, n=TTi: e.matmul(
                                ps[7][:, 0:n], k_sel2[:],
                                sap(nrm_b, 0, 2, ch * TT, [[1, n]]),
                                start=True, stop=True),
                                deps=dep + ((m_bse,) if m_bse else ()))
                            m_bse = T.se(lambda e, ch=ch, n=TTi: e.copy(
                                fmS(scr_b, ch, 0, n, width=NW1), ps[7][:, 0:n]),
                                deps=(mmb,))
                    if m_bse is None:
                        m_bse = m_bcast

                    if KSTAGE >= 19:
                        m_lrp = lerp("x_v", extra=(m_b1,))

                        v_se = []

                        def evac_v(mch, bank, dep0):
                            dep = dep0 + ((m_bse, m_gate) if mch == 0 else ())
                            mvb = T.ve(lambda e, m=mch, b=bank, n=TTi: e.tensor_mul(
                                fmS(bonus_b, m, 0, n), b[:, 0:n],
                                fmS(scr_b, m, 0, n, width=NW1)), deps=dep)
                            v_se.append(T.se(lambda e, m=mch, b=bank, n=TTi: e.copy(
                                fmS(v_b, m, 0, n), b[:, 0:n]), deps=(mvb,)))
                            return mvb
                        m_vev = proj("W_v", m_lrp, evac_v, extra=(m_bse,))
                        m_v = m_vev[-1]
                        m_bonus = m_v
                        m_vars = (m_ab1, m_ab2, m_bh, m_kh, m_bkF, m_bk1,
                                  m_clc, m_gc, m_v, m_r, m_gate,
                                  r_se[-1], v_se[-1])

                if not (KSEL & 1):
                    m_gate = m_ln
                if not (KSEL & 2):
                    m_bse = m_b1 = m_ln
                prev_ev = (m_bonus,) if KSTAGE >= 19 else (m_ln,)
                prev_xb = prev_ev[0]
                prev_s3 = prev_ev[0]
                prev_oe = prev_ev[0]
                if KSTAGE < 19:
                    m_vars = (m_ln,)
                for j in range(ncht if KSTAGE >= 20 else 0):
                    cglob = scan_t0 // L + j
                    band = 32 * (cglob % 4)
                    colc = j * L

                    def hsl(buf, hh, col, n, base=0):
                        return sap(buf, (hh % 2) * 64, 64,
                                   base + (hh // 2) * TT + col, [[1, n]])

                    ev = None
                    for t6, src in enumerate((v_b, bh_b, kh_b)):
                        m_s0 = None
                        for p in range(FCH):
                            bnk = ps[6 + (0 if p < 4 else 1)]
                            dep = ()
                            if m_s0 is None:
                                dep = (ev,) if ev else (m_vars + (prev_xb,))
                            m_s0 = T.te(lambda e, p=p, b=bnk, s=src, bd=band,
                                        cc=colc: e.matmul(
                                b[bd:bd + 32, (p % 4) * 128:(p % 4) * 128 + 128],
                                sap(s, 0, 128, p * TT + cc, [[1, L]]), k_id[:],
                                start=True, stop=True, tile_position=(0, bd)),
                                deps=dep)
                        T.se(lambda e, t6=t6, bd=band: e.copy(
                            sap(stm, bd, 32, t6 * 1024, [[1, 512]]),
                            ps[6][bd:bd + 32, :]), deps=(m_s0,))
                        ev = T.se(lambda e, t6=t6, bd=band: e.copy(
                            sap(stm, bd, 32, t6 * 1024 + 512, [[1, 512]]),
                            ps[7][bd:bd + 32, :]))
                    m_s1 = ev

                    if KCH >= 2:
                        m_s2 = None
                        for hh in range(16):
                            bnk = ps[0] if hh < 8 else ps[1]
                            dep = (m_s1, prev_s3) if hh == 0 else ()
                            m_s2 = T.te(lambda e, hh=hh, b=bnk, bd=band, cc=colc: e.matmul(
                                b[bd:bd + 32, (hh % 8) * 64:(hh % 8) * 64 + 64],
                                hsl(bk_b, hh, cc, L),
                                sap(ar_b, (hh % 2) * 64, 64, (hh // 2) * TT + cc,
                                    [[FCH * TT, 2], [1, L]]),
                                start=True, stop=True,
                                tile_position=((hh % 2) * 64, bd)), deps=dep)
                        for hh in range(16):
                            bnk = ps[2] if hh < 8 else ps[3]
                            m_s2 = T.te(lambda e, hh=hh, b=bnk, bd=band, cc=colc: e.matmul(
                                b[bd:bd + 32, (hh % 8) * 64:(hh % 8) * 64 + 64],
                                hsl(bk_b, hh, cc, L, base=FCH * TT),
                                sap(ar_b, (hh % 2) * 64, 64, (hh // 2) * TT + cc,
                                    [[FCH * TT, 2], [1, L]]),
                                start=True, stop=True,
                                tile_position=((hh % 2) * 64, bd)))
                        for hh in range(16):
                            m_s2 = T.te(lambda e, hh=hh, bd=band, cc=colc: e.matmul(
                                ps[4][bd:bd + 32, hh * 32:hh * 32 + 32],
                                hsl(ar_b, hh, cc, L), hsl(bk_b, hh, cc, L),
                                start=True, stop=True,
                                tile_position=((hh % 2) * 64, bd)))
                    else:
                        m_s2 = m_s1

                    if KCH >= 3:
                        T.ve(lambda e, bd=band: e.tensor_mul(
                            sap(strb, bd, 32, 0, [[1, 512]]), ps[0][bd:bd + 32, :],
                            k_mtrb[bd:bd + 32, :]), deps=(m_s2,))
                        T.ve(lambda e, bd=band: e.tensor_mul(
                            sap(strb, bd, 32, 512, [[1, 512]]), ps[1][bd:bd + 32, :],
                            k_mtrb[bd:bd + 32, :]))
                        T.ve(lambda e, bd=band: e.tensor_mul(
                            sap(smrk, bd, 32, 0, [[1, 512]]), ps[2][bd:bd + 32, :],
                            k_mtrb[bd:bd + 32, :]))
                        T.ve(lambda e, bd=band: e.tensor_mul(
                            sap(smrk, bd, 32, 512, [[1, 512]]), ps[3][bd:bd + 32, :],
                            k_mtrb[bd:bd + 32, :]))
                        m_s3 = T.ve(lambda e, bd=band: e.tensor_mul(
                            sap(sw0, bd, 32, 0, [[1, 512]]), ps[4][bd:bd + 32, :],
                            k_mw0[bd:bd + 32, :]))

                        def ztile(k, hh, n=32):
                            if k == 0:
                                return sap(strb, band, 32, hh * 64, [[1, n]])
                            return sap(ssq, band, 32, (k - 1) * 512 + hh * 32, [[1, n]])

                        def wtile(k, hh, n=32):
                            if k == 0:
                                return sap(sw0, band, 32, hh * 32, [[1, n]])
                            return sap(ssq, band, 32, (3 + k) * 512 + hh * 32, [[1, n]])

                    else:
                        m_s3 = m_s1

                    if KCH >= 4:
                        m_prev = m_s3
                        for lv in range(4):
                            m_mm = None
                            for hh in range(16):
                                dep = (m_prev,) if hh == 0 else ()
                                m_mm = T.te(lambda e, hh=hh, bd=band, wt=wtile(lv, hh),
                                            zt=ztile(lv, hh): e.matmul(
                                    ps[5][bd:bd + 32, hh * 32:hh * 32 + 32],
                                    wt, zt,
                                    start=True, stop=True, tile_position=(bd, bd)),
                                    deps=dep)
                            if lv < 3:
                                for hh in range(16):
                                    m_mm = T.te(lambda e, hh=hh, bd=band, wt=wtile(lv, hh),
                                                zt=ztile(lv, hh): e.matmul(
                                        ps[6][bd:bd + 32, hh * 32:hh * 32 + 32],
                                        zt, wt,
                                        start=True, stop=True, tile_position=(bd, bd)))
                            e1 = T.se(lambda e, lv=lv, bd=band: e.copy(
                                sap(ssq, bd, 32, lv * 512, [[1, 512]]),
                                ps[5][bd:bd + 32, :]), deps=(m_mm,))
                            if lv < 3:
                                m_prev = T.se(lambda e, lv=lv, bd=band: e.copy(
                                    sap(ssq, bd, 32, (4 + lv) * 512, [[1, 512]]),
                                    ps[6][bd:bd + 32, :]))
                            else:
                                m_prev = e1
                        m_s4 = m_prev

                    else:
                        m_s4 = m_s3

                    if KCH >= 5:
                        m_g1 = None
                        for hh in range(16):
                            bnk = ps[6] if hh < 8 else ps[7]
                            dep = (m_s4, prev_zb) if hh == 0 else ()
                            m_g1 = T.te(lambda e, hh=hh, b=bnk, bd=band, cc=colc: e.matmul(
                                b[bd:bd + 32, (hh % 8) * 64:(hh % 8) * 64 + 64],
                                hsl(ar_b, hh, cc, L),
                                sap(z_bf, (hh % 2) * 64, 64, (hh // 2) * 64, [[1, 64]]),
                                start=True, stop=True,
                                tile_position=((hh % 2) * 64, bd)), deps=dep)
                        T.se(lambda e, bd=band: e.copy(
                            sap(sxp_f[0], bd, 32, 0, [[1, 512]]),
                            ps[6][bd:bd + 32, :]), deps=(m_g1,))
                        m_g1e = T.se(lambda e, bd=band: e.copy(
                            sap(sxp_f[0], bd, 32, 512, [[1, 512]]), ps[7][bd:bd + 32, :]))
                    else:
                        m_g1e = m_s4

                    if KCH >= 6:
                        m_mv = None
                        for hh in range(16):
                            bnk = ps[6] if hh < 8 else ps[7]
                            dep = (m_g1e, m_s3) if hh == 0 else ()
                            m_mv = T.te(lambda e, hh=hh, b=bnk, bd=band: e.matmul(
                                b[bd:bd + 32, (hh % 8) * 64:(hh % 8) * 64 + 64],
                                sap(smrk, bd, 32, hh * 64, [[1, 32]]),
                                sap(stm, bd, 32, (hh // 2) * 128 + (hh % 2) * 64, [[1, 64]]),
                                start=True, stop=True, tile_position=(bd, bd)), deps=dep)
                    else:
                        m_mv = m_g1e

                    if KCH >= 7:
                        T.ve(lambda e, bd=band: e.tensor_add(
                            sap(sxp_f[0], bd, 32, 0, [[1, 512]]),
                            sap(sxp_f[0], bd, 32, 0, [[1, 512]]),
                            ps[6][bd:bd + 32, :]), deps=(m_mv, m_g1e))
                        m_x = T.ve(lambda e, bd=band: e.tensor_add(
                            sap(sxp_f[0], bd, 32, 512, [[1, 512]]),
                            sap(sxp_f[0], bd, 32, 512, [[1, 512]]),
                            ps[7][bd:bd + 32, :]))
                        m_xb = T.se(lambda e, bd=band: e.copy(
                            sap(sxp_b, bd, 32, 0, [[1, 1024]]),
                            sap(sxp_f[0], bd, 32, 0, [[1, 1024]])), deps=(m_x,))
                    else:
                        m_xb = m_x = m_mv

                    if KCH >= 8:
                        for lv in range(5):
                            srcf = sxp_f[lv % 2]
                            dstf = sxp_f[(lv + 1) % 2]
                            m_mm = None
                            for hh in range(16):
                                bnk = ps[6] if hh < 8 else ps[7]
                                dep = ((m_xb, m_s4) if lv == 0 else (m_xb,)) if hh == 0 else ()
                                m_mm = T.te(lambda e, hh=hh, b=bnk, bd=band,
                                            zt=ztile(lv, hh): e.matmul(
                                    b[bd:bd + 32, (hh % 8) * 64:(hh % 8) * 64 + 64],
                                    zt, sap(sxp_b, bd, 32, hh * 64, [[1, 64]]),
                                    start=True, stop=True, tile_position=(bd, bd)),
                                    deps=dep)
                            T.ve(lambda e, d=dstf, s=srcf, bd=band: e.tensor_add(
                                sap(d, bd, 32, 0, [[1, 512]]), ps[6][bd:bd + 32, :],
                                sap(s, bd, 32, 0, [[1, 512]])), deps=(m_mm,))
                            m_x = T.ve(lambda e, d=dstf, s=srcf, bd=band: e.tensor_add(
                                sap(d, bd, 32, 512, [[1, 512]]), ps[7][bd:bd + 32, :],
                                sap(s, bd, 32, 512, [[1, 512]])))
                            m_xb = T.se(lambda e, d=dstf, bd=band: e.copy(
                                sap(sxp_b, bd, 32, 0, [[1, 1024]]),
                                sap(d, bd, 32, 0, [[1, 1024]])), deps=(m_x,))
                        sP = sxp_b
                        m_p = m_xb
                        prev_xb = m_xb

                    else:
                        sP = sxp_b
                        m_p = m_xb
                        prev_xb = m_xb

                    if KCH >= 9:
                        m_f = None
                        for hh in range(16):
                            dep = (m_p,) if hh == 0 else ()
                            T.te(lambda e, hh=hh, s=sP, bd=band: e.matmul(
                                ps[5][(hh % 2) * 64:(hh % 2) * 64 + 64,
                                      (hh // 2) * 64:(hh // 2) * 64 + 64],
                                sap(stm, bd, 32, 1024 + (hh // 2) * 128 + (hh % 2) * 64,
                                    [[1, 64]]),
                                sap(s, bd, 32, hh * 64, [[1, 64]]),
                                start=True, stop=False,
                                tile_position=(bd, (hh % 2) * 64)), deps=dep)
                            m_f = T.te(lambda e, hh=hh, bd=band: e.matmul(
                                ps[5][(hh % 2) * 64:(hh % 2) * 64 + 64,
                                      (hh // 2) * 64:(hh // 2) * 64 + 64],
                                sap(stm, bd, 32, 2048 + (hh // 2) * 128 + (hh % 2) * 64,
                                    [[1, 64]]),
                                sap(stm, bd, 32, (hh // 2) * 128 + (hh % 2) * 64, [[1, 64]]),
                                start=False, stop=True,
                                tile_position=(bd, (hh % 2) * 64)))

                    else:
                        m_f = m_p

                    if KCH >= 10:
                        if not is_warm:
                            m_oa = None
                            for hh in range(16):
                                bnk = ps[6] if hh < 8 else ps[7]
                                m_oa = T.te(lambda e, hh=hh, b=bnk, bd=band, cc=colc:
                                            e.matmul(
                                    b[bd:bd + 32, (hh % 8) * 64:(hh % 8) * 64 + 64],
                                    hsl(ar_b, hh, cc, L, base=FCH * TT),
                                    sap(z_bf, (hh % 2) * 64, 64, (hh // 2) * 64, [[1, 64]]),
                                    start=True, stop=True,
                                    tile_position=((hh % 2) * 64, bd)),
                                    deps=(m_p, prev_oe) if hh == 0 else ())
                            tg = (j * L) // 128
                            T.se(lambda e, tg=tg, bd=band: e.copy(
                                sap(o_tm, bd, 32, tg * C, [[1, 512]]),
                                ps[6][bd:bd + 32, :]), deps=(m_oa,))
                            m_oae = T.se(lambda e, tg=tg, bd=band: e.copy(
                                sap(o_tm, bd, 32, tg * C + 512, [[1, 512]]),
                                ps[7][bd:bd + 32, :]))
                            m_ob = None
                            for hh in range(16):
                                bnk = ps[6] if hh < 8 else ps[7]
                                dep = (m_oae,) if hh == 0 else ()
                                T.te(lambda e, hh=hh, b=bnk, s=sP, bd=band: e.matmul(
                                    b[bd:bd + 32, (hh % 8) * 64:(hh % 8) * 64 + 64],
                                    sap(strb, bd, 32, hh * 64 + 32, [[1, 32]]),
                                    sap(s, bd, 32, hh * 64, [[1, 64]]),
                                    start=True, stop=False, tile_position=(bd, bd)),
                                    deps=dep)
                                m_ob = T.te(lambda e, hh=hh, b=bnk, bd=band: e.matmul(
                                    b[bd:bd + 32, (hh % 8) * 64:(hh % 8) * 64 + 64],
                                    sap(smrk, bd, 32, hh * 64 + 32, [[1, 32]]),
                                    sap(stm, bd, 32, (hh // 2) * 128 + (hh % 2) * 64,
                                        [[1, 64]]),
                                    start=False, stop=True, tile_position=(bd, bd)))
                            T.ve(lambda e, tg=tg, bd=band: e.tensor_add(
                                sap(o_tm, bd, 32, tg * C, [[1, 512]]),
                                sap(o_tm, bd, 32, tg * C, [[1, 512]]),
                                ps[6][bd:bd + 32, :]),
                                deps=(m_oae, m_ob))
                            m_oe = T.ve(lambda e, tg=tg, bd=band: e.tensor_add(
                                sap(o_tm, bd, 32, tg * C + 512, [[1, 512]]),
                                sap(o_tm, bd, 32, tg * C + 512, [[1, 512]]),
                                ps[7][bd:bd + 32, :]))
                        else:
                            m_oe = m_s3

                    else:
                        m_oe = m_s3

                    if KCH >= 11:
                        mz = None
                        for p in range(FCH):
                            dep = (m_f, m_clc) if p == 0 else ()
                            mz = T.ve(lambda e, p=p, j=j: e.scalar_tensor_tensor(
                                sap(z_f32, 0, 128, p * 64, [[1, 64]]),
                                sap(z_f32, 0, 128, p * 64, [[1, 64]]),
                                sap(clc_f, 0, 128, p * 8 + j, [[1, 1]]),
                                ps[5][:, p * 64:p * 64 + 64], ALU.mult, ALU.add), deps=dep)
                        prev_zb = T.ve(lambda e: e.tensor_copy(z_bf[:], z_f32[:]), deps=(mz,))
                        prev_ev = (prev_zb, m_oe)
                        prev_s3 = m_s3
                        prev_oe = m_oe
                    else:
                        mz = m_f
                        prev_zb = prev_zb

                    if (not is_warm) and (j % 4 == 3):
                        tg = (j * L) // 128
                        m1 = T.ve(lambda e, tg=tg: e.reduce_sum(
                            sap(gnst, 0, 128, 0, [[1, 16]]),
                            sap(o_tm, 0, 128, tg * C, [[64, 16], [1, 64]]),
                            axis=AX.X), deps=prev_ev)
                        T.ve(lambda e: e.tensor_scalar_mul(
                            sap(gnst, 0, 128, 0, [[1, 16]]),
                            sap(gnst, 0, 128, 0, [[1, 16]]), 1.0 / 64))
                        m_sqg = T.se(lambda e, tg=tg: e.activation(
                            sap(scr_b, 0, 128, 0, [[1, C]]),
                            sap(o_tm, 0, 128, tg * C, [[1, C]]), AF.Square),
                            deps=(m_oe,))
                        T.ve(lambda e: e.reduce_sum(
                            sap(gnst, 0, 128, 16, [[1, 16]]),
                            sap(scr_b, 0, 128, 0, [[64, 16], [1, 64]]), axis=AX.X),
                            deps=(m_sqg,))
                        T.ve(lambda e: e.tensor_mul(sap(gnst, 0, 128, 32, [[1, 16]]),
                                                    sap(gnst, 0, 128, 0, [[1, 16]]),
                                                    sap(gnst, 0, 128, 0, [[1, 16]])))
                        m_vr = T.ve(lambda e: e.scalar_tensor_tensor(
                            sap(gnst, 0, 128, 48, [[1, 16]]),
                            sap(gnst, 0, 128, 16, [[1, 16]]), 1.0 / 64,
                            sap(gnst, 0, 128, 32, [[1, 16]]), ALU.mult, ALU.subtract))
                        m_sd2 = T.se(lambda e: e.activation(
                            sap(gnst, 0, 128, 64, [[1, 16]]),
                            sap(gnst, 0, 128, 48, [[1, 16]]), AF.Sqrt, bias=EPS),
                            deps=(m_vr,))
                        m_rs2 = T.ve(lambda e: e.reciprocal(
                            sap(gnst, 0, 128, 80, [[1, 16]]),
                            sap(gnst, 0, 128, 64, [[1, 16]])), deps=(m_sd2,))
                        T.ve(lambda e, tg=tg: e.tensor_sub(
                            sap(o_tm, 0, 128, tg * C, [[64, 16], [1, 64]]),
                            sap(o_tm, 0, 128, tg * C, [[64, 16], [1, 64]]),
                            sap(gnst, 0, 128, 0, [[1, 16], [0, 64]])), deps=(m_rs2,))
                        T.ve(lambda e, tg=tg: e.tensor_mul(
                            sap(o_tm, 0, 128, tg * C, [[64, 16], [1, 64]]),
                            sap(o_tm, 0, 128, tg * C, [[64, 16], [1, 64]]),
                            sap(gnst, 0, 128, 80, [[1, 16], [0, 64]])))
                        T.ve(lambda e, tg=tg: e.tensor_mul(
                            sap(o_tm, 0, 128, tg * C, [[1, C]]),
                            sap(o_tm, 0, 128, tg * C, [[1, C]]), gw_tm[:]),
                            deps=(m_gnbc,))
                        m_gn = T.ve(lambda e, tg=tg: e.scalar_tensor_tensor(
                            sap(scr_b, 0, 128, 0, [[1, C]]),
                            sap(o_tm, 0, 128, tg * C, [[1, C]]), 1.0, gb_tm[:],
                            ALU.mult, ALU.add))
                        m_tp = None
                        for ch in range(FCH):
                            dept = (m_gn,)
                            if ch == 0 and prev_gn_add is not None:
                                dept = dept + (prev_gn_add,)
                            m_tp = T.dma_sc(lambda e, ch=ch, tg=tg: e.dma_start(
                                sap(scr_b, 0, 128, 1024 + ch * 128, [[1, 128]]),
                                sap(scr_b, 0, 128, ch * 128, [[1, 128]]),
                                transpose=True), deps=dept)
                        real0 = scan_t0 - WARM + tg * 128
                        T.ve(lambda e, tg=tg, r0=real0: e.tensor_add(
                            sap(wo_in, 0, 128, r0, [[SEG, FCH], [1, 128]]),
                            sap(scr_b, 0, 128, 1024, [[128, FCH], [1, 128]]),
                            sap(bonus_b, 0, 128, tg * 128, [[TT, FCH], [1, 128]])),
                            deps=(m_tp, m_bonus))
                        prev_gn_add = T.ve(lambda e, tg=tg, r0=real0: e.tensor_mul(
                            sap(wo_in, 0, 128, r0, [[SEG, FCH], [1, 128]]),
                            sap(wo_in, 0, 128, r0, [[SEG, FCH], [1, 128]]),
                            sap(gate_b, 0, 128, tg * 128, [[TT, FCH], [1, 128]])),
                            deps=(m_gate,))

                if DBG and ti == 1:
                    T.barrier()
                    for nm_, buf_ in (("ln", ln_f), ("r_b", r_b),
                                      ("v_b", v_b), ("ah_f", ah_f), ("sp_f", sp_f),
                                      ("kk_f", kk_f),
                                      ("gate_b", gate_b), ("bonus_b", bonus_b),
                                      ("d1_f", d1_f),
                                      ("bk_b", bk_b), ("ar_b", ar_b), ("bh_b", bh_b),
                                      ("kh_b", kh_b),
                                      ("o_tm", o_tm), ("strb", strb),
                                      ("stm", stm), ("sw0", sw0)):
                        t_ = dbg_t("t1_" + nm_, tuple(buf_.shape),
                                   F32 if buf_.dtype == F32 else F32)
                        if buf_.dtype == F32:
                            T.dma_sy(lambda e, t_=t_, b_=buf_: e.dma_start(t_.ap(), b_[:]),
                                     deps=(prev_zb, prev_ev[0]))
                        else:
                            T.dma_ge(lambda e, t_=t_, b_=buf_: e.dma_start(t_.ap(), b_[:]),
                                     deps=(prev_zb, prev_ev[0]))
                T.barrier()
                scan_t0 += TTi
            if DBG:
                t = dbg_t("wo_in", (128, FCH * SEG))
                T.dma_ge(lambda e: e.dma_start(t.ap(), wo_in[:]), deps=(prev_zb,))
                t2 = dbg_t("z", (128, FCH * 64))
                T.dma_sy(lambda e: e.dma_start(t2.ap(), z_f32[:]), deps=(prev_zb,))
            T.barrier()

        if KSKIP2:
            dmy = []
            for m in range(FCH):
                dmy.append(T.dma_out(lambda e, m=m: e.dma_start(
                    AP(out_fm, m * 128 * SEG, [[SEG, 128], [1, SEG]]),
                    sap(xf, 0, 128, 0, [[1, SEG]]))))
            T._rec("ge", lambda e: e.nop(), None, 0,
                   (("outQ", T.semn["outQ"]),))
            with nc.allow_low_precision(reason="debug"):
                with nc.Block() as block:
                    T.finalize(block)
            return nc
        p2 = contextlib.ExitStack()
        with p2:
            def sb2(name, shape, dt):
                return p2.enter_context(nc.sbuf_tensor(name, list(shape), dt))

            w_big = sb2("w_big", (128, FCH * C), BF16)
            xf_t = sb2("xf_t", (128, FCH * 512), F32)
            w_2 = sb2("w_2", (128, 4 * SLOTS), BF16)
            w_gs = sb2("w_gs", (128, FCH * 512), BF16)
            s_slots = sb2("s_slots", (SLOTS, C), F32)
            s_slotsb = sb2("s_slotsb", (SLOTS, C), BF16)
            s_alpha = sb2("s_alpha", (128, 16 * SLOTS), BF16)
            s_alpsm = sb2("s_alpsm", (SLOTS, SEG), BF16)
            s_si = sb2("s_si", (SLOTS, C + 16), F32)
            s_lgt = sb2("s_lgt", (128, SLOTS), F32)
            s_m8 = sb2("s_m8", (128, 12), F32)
            s_rep = sb2("s_rep", (128, SLOTS), F32)
            s_msk = sb2("s_msk", (128, SLOTS), F32)
            s_el = sb2("s_el", (128, SLOTS), F32)
            s_meanb = sb2("s_meanb", (1, C), BF16)
            s_fm64 = sb2("s_fm64", (128, FCH * 64), BF16)
            s_fm64b = sb2("s_fm64b", (128, FCH * 64), BF16)
            s_tau = sb2("s_tau", (1, 4), F32)
            s_taub = sb2("s_taub", (128, 1), F32)
            s_b2bc = sb2("s_b2bc", (128, SLOTS), F32)
            s_ofin = sb2("s_ofin", (128, 2 * 512), F32)
            biash = sb2("biash", (128, 4), F32)
            y_tm = wo_in

            p2t = contextlib.ExitStack()
            with p2t:
                def sb2t(name, shape, dt):
                    return p2t.enter_context(nc.sbuf_tensor(name, list(shape), dt))

                y_fm = sb2t("y_fm", (128, FCH * SEG), BF16)
                x2t_f = sb2t("x2t_f", (128, FCH * 512), F32)
                w_1a = sb2t("w_1a", (128, FCH * HFF), BF16)
                w_1b = sb2t("w_1b", (128, FCH * HFF), BF16)
                s_hfm = sb2t("s_hfm", (128, 4 * SEG), BF16)
                row2_f = sb2t("row2_f", (1, 4 * 512), F32)
                row2_c0 = sb2t("row2_c0", (1, 512), BF16)
                row2_c1 = sb2t("row2_c1", (1, 512), BF16)
                cb2 = sb2t("cb2", (128, 2 * 512), BF16)

                wo_ld = [None, None]
                wo_cs = [None, None]
                for kch in range(FCH):
                    h = kch % 2
                    wo_ld[h] = T.dma_sy(lambda e, k=kch, h=h: e.dma_start(
                        sap(xf_t, 0, 128, h * 1024, [[1, C]]),
                        dap("W_o", k * 128 * C, [[C, 128], [1, C]])),
                        deps=(wo_cs[h],) if wo_cs[h] else (), sem=("wA", "wB")[h])
                    wo_cs[h] = T.se(lambda e, k=kch, h=h: e.copy(
                        sap(w_big, 0, 128, k * C, [[1, C]]),
                        sap(xf_t, 0, 128, h * 1024, [[1, C]])),
                        deps=(wo_ld[h],))
                d_WO = wo_cs[1]
                d_WO2 = wo_cs[0]

                m_yg = []
                prev_stat2 = None
                prev_x2h = None
                for grp in range(4):
                    m_xf = T.dma_sy(lambda e, g=grp: e.dma_start(
                        sap(xf_t, 0, 128, 0, [[512, FCH], [1, 512]]),
                        dap("xT", WARM + 1 + g * 512,
                            [[TS + 1, 128], [128 * (TS + 1), FCH], [1, 512]])),
                        deps=tuple(m_yg[-1:]) + ((d_WO, d_WO2) if grp == 0 else ()) +
                        ((prev_stat2,) if prev_stat2 else ()))
                    mmk = None
                    for kch in range(FCH):
                        for mch in range(FCH):
                            bank = ps[mch]
                            dep = (d_WO, d_WO2) if (kch == 0 and mch == 0) else ()
                            if kch == 0 and mch == 0 and grp > 0:
                                dep = dep + tuple(m_yg[-1:])
                            mmk = T.te(lambda e, m=mch, k=kch, b=bank, g=grp: e.matmul(
                                b[:, :], sap(w_big, 0, 128, k * C + m * 128, [[1, 128]]),
                                sap(wo_in, 0, 128, k * SEG + g * 512, [[1, 512]]),
                                start=(k == 0), stop=(k == FCH - 1)), deps=dep)
                    m_x2 = None
                    for mch in range(FCH):
                        m_x2 = T.ve(lambda e, m=mch, g=grp: e.tensor_add(
                            sap(x2t_f, 0, 128, m * 512, [[1, 512]]),
                            ps[m][:, :],
                            sap(xf_t, 0, 128, m * 512, [[1, 512]])),
                            deps=(mmk, m_xf) + ((prev_x2h,) if prev_x2h else ()))
                    m_x2h = T.dma_sy(lambda e, g=grp: e.dma_start(
                        AP(x2_hbm, g * 512, [[SEG, 128], [128 * SEG, FCH], [1, 512]]),
                        sap(x2t_f, 0, 128, 0, [[512, FCH], [1, 512]])), deps=(m_x2,),
                        sem="x2s")
                    prev_x2h = m_x2h
                    m_yb = T.ve(lambda e, g=grp: e.tensor_copy(
                        sap(y_fm, 0, 128, g * 512, [[SEG, FCH], [1, 512]]),
                        sap(x2t_f, 0, 128, 0, [[512, FCH], [1, 512]])), deps=(m_x2,))
                    for ch in range(FCH):
                        mm = T.te(lambda e, ch=ch, g=grp: e.matmul(
                            ps[4][0:1, :], k_onescol[:],
                            sap(y_fm, 0, 128, ch * SEG + g * 512, [[1, 512]]),
                            start=(ch == 0), stop=(ch == FCH - 1)),
                            deps=(m_yb,) if ch == 0 else ())
                    m_sq = T.se(lambda e, g=grp: e.activation(
                        sap(xf_t, 0, 128, 0, [[512, FCH], [1, 512]]),
                        sap(y_fm, 0, 128, g * 512, [[SEG, FCH], [1, 512]]), AF.Square),
                        deps=(m_yb, m_x2h))
                    for ch in range(FCH):
                        mm = T.te(lambda e, ch=ch: e.matmul(
                            ps[5][0:1, :], k_onescolf[:],
                            sap(xf_t, 0, 128, ch * 512, [[1, 512]]),
                            start=(ch == 0), stop=(ch == FCH - 1)),
                            deps=(m_sq,) if ch == 0 else ())
                    m_stat = mm
                    prev_stat2 = m_stat
                    T.ve(lambda e: e.tensor_scalar_mul(AP(row2_f, 0 * 512, [[4 * 512, 1], [1, 512]]), ps[4][0:1, :],
                                                       1.0 / C), deps=(m_stat,))
                    T.ve(lambda e: e.tensor_mul(AP(row2_f, 1 * 512, [[4 * 512, 1], [1, 512]]), AP(row2_f, 0 * 512, [[4 * 512, 1], [1, 512]]),
                                                AP(row2_f, 0 * 512, [[4 * 512, 1], [1, 512]])))
                    m_vr = T.ve(lambda e: e.scalar_tensor_tensor(
                        AP(row2_f, 2 * 512, [[4 * 512, 1], [1, 512]]), ps[5][0:1, :], 1.0 / C, AP(row2_f, 1 * 512, [[4 * 512, 1], [1, 512]]),
                        ALU.mult, ALU.subtract))
                    m_sd = T.se(lambda e: e.activation(AP(row2_f, 3 * 512, [[4 * 512, 1], [1, 512]]), AP(row2_f, 2 * 512, [[4 * 512, 1], [1, 512]]),
                                                       AF.Sqrt, bias=EPS), deps=(m_vr,))
                    T.ve(lambda e: e.reciprocal(AP(row2_f, 2 * 512, [[4 * 512, 1], [1, 512]]), AP(row2_f, 3 * 512, [[4 * 512, 1], [1, 512]])),
                         deps=(m_sd,))
                    T.ve(lambda e: e.tensor_copy(row2_c1[0:1, :], AP(row2_f, 2 * 512, [[4 * 512, 1], [1, 512]])))
                    m_rw = T.ve(lambda e: e.scalar_tensor_tensor(
                        row2_c0[0:1, :], AP(row2_f, 0 * 512, [[4 * 512, 1], [1, 512]]), -1.0, AP(row2_f, 2 * 512, [[4 * 512, 1], [1, 512]]),
                        ALU.mult, ALU.mult))
                    T.te(lambda e: e.matmul(ps[6][:, :], k_onesrow[0:1, :],
                                            row2_c0[0:1, :], start=True, stop=True),
                         deps=(m_rw,))
                    m_bc = T.te(lambda e: e.matmul(ps[7][:, :], k_onesrow[0:1, :],
                                                   row2_c1[0:1, :], start=True, stop=True))
                    T.se(lambda e: e.copy(sap(cb2, 0, 128, 0, [[1, 512]]), ps[6][:, :]),
                         deps=(m_bc,))
                    m_cb = T.se(lambda e: e.copy(sap(cb2, 0, 128, 512, [[1, 512]]),
                                                 ps[7][:, :]))
                    T.ve(lambda e, g=grp: e.tensor_mul(
                        sap(y_fm, 0, 128, g * 512, [[SEG, FCH], [1, 512]]),
                        sap(y_fm, 0, 128, g * 512, [[SEG, FCH], [1, 512]]),
                        sap(cb2, 0, 128, 512, [[0, FCH], [1, 512]])), deps=(m_cb,))
                    m_y = T.ve(lambda e, g=grp: e.tensor_add(
                        sap(y_fm, 0, 128, g * 512, [[SEG, FCH], [1, 512]]),
                        sap(y_fm, 0, 128, g * 512, [[SEG, FCH], [1, 512]]),
                        sap(cb2, 0, 128, 0, [[0, FCH], [1, 512]])))
                    m_yg.append(m_y)
                m_yall = m_yg[-1]
                T.barrier()

                m_yt = None
                for tg in range(16):
                    for ch in range(FCH):
                        m_yt = T.dma_sc(lambda e, tg=tg, ch=ch: e.dma_start(
                            sap(y_tm, 0, 128, tg * C + ch * 128, [[1, 128]]),
                            sap(y_fm, 0, 128, ch * SEG + tg * 128, [[1, 128]]),
                            transpose=True), deps=(m_yall,))

                m_w1x = None
                for wnm, wdst in (("w1aT", w_1a), ("w1bT", w_1b)):
                    dep = (prev_x2h, prev_stat2, m_yall) if m_w1x is None \
                        else (m_w1x,)
                    d_l = T.dma_sy(lambda e, wnm=wnm: e.dma_start(
                        sap(xf_t, 0, 128, 0, [[HFF, FCH], [1, HFF]]),
                        dap(wnm, 0, [[HFF, 128], [128 * HFF, FCH], [1, HFF]])),
                        deps=dep, sem="wA")
                    m_w1x = T.se(lambda e, wdst=wdst: e.copy(
                        sap(wdst, 0, 128, 0, [[1, FCH * HFF]]),
                        sap(xf_t, 0, 128, 0, [[1, FCH * HFF]])),
                        deps=(d_l,))
                d_w1a = d_w1b = m_w1x
                d_w2 = T.dma_ge(lambda e: e.dma_start(
                    sap(w_2, 0, 128, 0, [[SLOTS, 4], [1, SLOTS]]),
                    dap("w2T", 0, [[SLOTS, 128], [128 * SLOTS, 4], [1, SLOTS]])))
                T.dma_sy(lambda e: e.dma_start(s_slots[:], din["slot_init"].ap()))
                T.dma_sy(lambda e: e.dma_start(
                    sap(s_si, 0, SLOTS, 0, [[1, 1]]), din["v_slot_scale"].ap()))
                T.dma_sy(lambda e: e.dma_start(
                    AP(s_tau, 0, [[4, 1], [1, 1]]), din["tau"].ap()))
                T.dma_sy(lambda e: e.dma_start(biash[:, 0:4], din["v_b1"].ap()))
                T.dma_sy(lambda e: e.dma_start(s_b2bc[:], din["b2_rep"].ap()))
                d_slot = d_scale = d_tau = d_b1 = d_b2r = ("syQ", T.semn["syQ"])

                m_S = T.ve(lambda e: e.tensor_scalar_mul(
                    s_slots[:], s_slots[:], sap(s_si, 0, SLOTS, 0, [[1, 1]])),
                    deps=(d_slot, d_scale))
                m_Sb = T.ve(lambda e: e.tensor_copy(s_slotsb[:], s_slots[:]))
                T.te(lambda e: e.matmul(ps[4][0:1, :], k_onescol[0:64, :],
                                        s_slotsb[:, 0:512], start=True, stop=True),
                     deps=(m_Sb,))
                m_sm = T.te(lambda e: e.matmul(ps[5][0:1, :], k_onescol[0:64, :],
                                               s_slotsb[:, 512:1024],
                                               start=True, stop=True))
                T.ve(lambda e: e.tensor_scalar_mul(
                    AP(s_meanb, 0, [[C, 1], [1, 512]]), ps[4][0:1, :], 1.0 / 64),
                    deps=(m_sm,))
                m_smb = T.ve(lambda e: e.tensor_scalar_mul(
                    AP(s_meanb, 512, [[C, 1], [1, 512]]), ps[5][0:1, :], 1.0 / 64))
                for ch in range(FCH):
                    mm = T.te(lambda e, ch=ch: e.matmul(
                        ps[6][:, ch:ch + 1],
                        AP(s_meanb, ch * 128, [[C, 1], [1, 128]]),
                        k_onesrow[0:1, 0:1], start=True, stop=True),
                        deps=(m_smb,) if ch == 0 else ())
                m_mfm = T.ve(lambda e: e.tensor_copy(sap(s_fm64, 0, 128, 0, [[1, FCH]]),
                                                     ps[6][:, 0:FCH]), deps=(mm,))
                mmk = None
                for mh in range(4):
                    for k in range(FCH):
                        dep = (m_mfm, d_w1b) if (k == 0 and mh == 0) else ()
                        mmk = T.te(lambda e, mh=mh, k=k: e.matmul(
                            ps[7][:, mh:mh + 1],
                            sap(w_1b, 0, 128, k * HFF + mh * 128, [[1, 128]]),
                            sap(s_fm64, 0, 128, k, [[1, 1]]),
                            start=(k == 0), stop=(k == FCH - 1)), deps=dep)
                m_bh = T.ve(lambda e: e.tensor_add(biash[:, 0:4], biash[:, 0:4],
                                                   ps[7][:, 0:4]), deps=(mmk, d_b1))

                m_h = None
                for grp in range(4):
                    for mh in range(4):
                        bank = ps[mh % 2]
                        mmk = None
                        for k in range(FCH):
                            dep = ()
                            if k == 0:
                                dep = (m_yall, d_w1a, m_bh)
                                if m_h is not None:
                                    dep = dep + (m_h,)
                            mmk = T.te(lambda e, g=grp, mh=mh, k=k, b=bank: e.matmul(
                                b[:, :],
                                sap(w_1a, 0, 128, k * HFF + mh * 128, [[1, 128]]),
                                sap(y_fm, 0, 128, k * SEG + g * 512, [[1, 512]]),
                                start=(k == 0), stop=(k == FCH - 1)), deps=dep)
                        m_h = T.se(lambda e, g=grp, mh=mh, b=bank: e.activation(
                            sap(s_hfm, 0, 128, mh * SEG + g * 512, [[1, 512]]),
                            b[:, :], AF.Gelu, bias=sap(biash, 0, 128, mh, [[1, 1]])),
                            deps=(mmk,))

                m_t1 = T.se(lambda e: e.activation(AP(s_tau, 1, [[4, 1], [1, 1]]),
                                                   AP(s_tau, 0, [[4, 1], [1, 1]]),
                                                   AF.Abs), deps=(d_tau,))
                T.ve(lambda e: e.tensor_scalar_add(AP(s_tau, 1, [[4, 1], [1, 1]]),
                                                   AP(s_tau, 1, [[4, 1], [1, 1]]),
                                                   0.1), deps=(m_t1,))
                m_t2 = T.ve(lambda e: e.reciprocal(AP(s_tau, 2, [[4, 1], [1, 1]]),
                                                   AP(s_tau, 1, [[4, 1], [1, 1]])))
                m_tmm = T.te(lambda e: e.matmul(ps[6][:, 0:1], k_onesrowf[0:1, :],
                                                AP(s_tau, 2, [[4, 1], [1, 1]]),
                                                start=True, stop=True), deps=(m_t2,))
                m_tb = T.ve(lambda e: e.tensor_copy(s_taub[:], ps[6][:, 0:1]),
                            deps=(m_tmm,))
                m_b2bc = T.ve(lambda e: e.tensor_scalar_mul(
                    s_b2bc[:], s_b2bc[:], sap(s_taub, 0, 128, 0, [[1, 1]])),
                    deps=(m_tb, d_b2r))

                m_sia = None
                prev_lg = None
                prev_ae = None
                m_aes = None
                for tg in range(16):
                    mmk = None
                    for k in range(4):
                        dep = (m_h, d_w2) if (k == 0 and tg == 0) else ()
                        if k == 0 and prev_lg is not None:
                            dep = dep + (prev_lg,)
                        mmk = T.te(lambda e, tg=tg, k=k: e.matmul(
                            ps[2][:, 0:SLOTS],
                            sap(s_hfm, 0, 128, k * SEG + tg * 128, [[1, 128]]),
                            sap(w_2, 0, 128, k * SLOTS, [[1, SLOTS]]),
                            start=(k == 0), stop=(k == 3)), deps=dep)
                    m_lg = T.ve(lambda e: e.scalar_tensor_tensor(
                        s_lgt[:], ps[2][:, 0:SLOTS], sap(s_taub, 0, 128, 0, [[1, 1]]),
                        s_b2bc[:], ALU.mult, ALU.add), deps=(mmk, m_tb, m_b2bc))
                    prev_lg = m_lg
                    m_m8 = T.ve(lambda e: e.max(out=s_m8[:, 0:8], in_=s_lgt[:]),
                                deps=(m_lg,))
                    T.ve(lambda e: e.tensor_scalar_mul(s_m8[:, 8:9], s_m8[:, 0:1],
                                                       -1.0))
                    m_rep = T.ve(lambda e: e.match_replace(out=s_rep[:],
                                                           in_to_replace=s_m8[:, 0:8],
                                                           in_values=s_lgt[:],
                                                           imm_value=-1e30))
                    m_e8 = T.se(lambda e: e.activation(s_m8[:, 0:8], s_m8[:, 0:8],
                                                       AF.Exp, bias=s_m8[:, 8:9]),
                                deps=(m_rep,))
                    m_den = T.ve(lambda e: e.reduce_sum(s_m8[:, 9:10], s_m8[:, 0:8],
                                                        axis=AX.X), deps=(m_e8,))
                    m_rd = T.ve(lambda e: e.reciprocal(s_m8[:, 10:11], s_m8[:, 9:10]))
                    m_el = T.se(lambda e: e.activation(s_el[:], s_lgt[:], AF.Exp,
                                                       bias=s_m8[:, 8:9]), deps=(m_rd,))
                    m_msk = T.ve(lambda e: e.tensor_tensor(s_msk[:], s_lgt[:], s_rep[:],
                                                           ALU.not_equal), deps=(m_el,))
                    T.ve(lambda e: e.tensor_mul(s_el[:], s_el[:], s_msk[:]))
                    m_al = T.ve(lambda e, tg=tg: e.tensor_scalar_mul(
                        sap(s_alpha, 0, 128, tg * SLOTS, [[1, SLOTS]]), s_el[:],
                        sap(s_m8, 0, 128, 10, [[1, 1]])))
                    m_at = T.te(lambda e, tg=tg: e.matmul(
                        ps[3][0:64, 0:128],
                        sap(s_alpha, 0, 128, tg * SLOTS, [[1, SLOTS]]), k_id[:],
                        start=True, stop=True),
                        deps=(m_al,) + ((prev_ae,) if prev_ae else ()))
                    m_ae = T.se(lambda e, tg=tg: e.copy(
                        sap(s_alpsm, 0, SLOTS, tg * 128, [[1, 128]]),
                        ps[3][0:64, 0:128]), deps=(m_at,))
                    prev_ae = m_ae
                    m_aes = m_ae
                    dep0 = (m_al, m_yt) if tg == 0 else (m_al,)
                    T.te(lambda e, tg=tg: e.matmul(
                        ps[0][0:64, :],
                        sap(s_alpha, 0, 128, tg * SLOTS, [[1, SLOTS]]),
                        sap(y_tm, 0, 128, tg * C, [[1, 512]]),
                        start=(tg == 0), stop=(tg == 15)), deps=dep0)
                    T.te(lambda e, tg=tg: e.matmul(
                        ps[1][0:64, :],
                        sap(s_alpha, 0, 128, tg * SLOTS, [[1, SLOTS]]),
                        sap(y_tm, 0, 128, tg * C + 512, [[1, 512]]),
                        start=(tg == 0), stop=(tg == 15)))
                    m_sia = T.te(lambda e, tg=tg: e.matmul(
                        ps[4][0:64, 0:1],
                        sap(s_alpha, 0, 128, tg * SLOTS, [[1, SLOTS]]),
                        k_onescol[:], start=(tg == 0), stop=(tg == 15)))
                T.se(lambda e: e.copy(sap(s_si, 0, SLOTS, 0, [[1, 512]]),
                                      ps[0][0:64, :]), deps=(m_sia,))
                T.se(lambda e: e.copy(sap(s_si, 0, SLOTS, 512, [[1, 512]]),
                                      ps[1][0:64, :]))
                m_sie = T.se(lambda e: e.copy(sap(s_si, 0, SLOTS, C, [[1, 1]]),
                                              ps[4][0:64, 0:1]))
                T.barrier()
                m_x2all = ("x2s", T.semn["x2s"])

            m_z16 = T.ve(lambda e: e.memset(s_si[:, C + 1:C + 16], 0.0))
            m_ccw = T.dma_ge(lambda e: e.dma_start(cc_in, s_si[:, 0:C + 16]),
                             deps=(m_z16,))
            m_cc = T._rec("ge", lambda e: e.collective_compute(
                "AllReduce", ALU.add, ins=[cc_in], outs=[cc_out],
                replica_groups=[list(range(NCORES))]), "cc", 1, (m_ccw,))
            m_ccr = T.dma_ge(lambda e: e.dma_start(s_si[:, 0:C + 16], cc_out),
                             deps=(m_cc,))

            s_gx = sb2("s_gx", (SLOTS, GDIM), F32)
            s_gh = sb2("s_gh", (SLOTS, GDIM), F32)
            s_gate2 = sb2("s_gate2", (SLOTS, 2 * C), F32)
            s_gtmp = sb2("s_gtmp", (SLOTS, C), F32)
            s_snew = sb2("s_snew", (SLOTS, C), F32)
            s_snb = sb2("s_snb", (SLOTS, C), BF16)
            s_gib = sb2("s_gib", (SLOTS, C), BF16)
            s_sv = sb2("s_sv", (SLOTS, C), F32)
            s_svb = sb2("s_svb", (SLOTS, C), BF16)
            s_svw = sb2("s_svw", (SLOTS, C), BF16)
            s_bc = sb2("s_bc", (SLOTS, GDIM), F32)

            T.ve(lambda e: e.tensor_scalar_add(sap(s_si, 0, SLOTS, C + 1, [[1, 1]]),
                                               sap(s_si, 0, SLOTS, C, [[1, 1]]),
                                               1e-8), deps=(m_ccr,))
            m_rw2 = T.ve(lambda e: e.reciprocal(sap(s_si, 0, SLOTS, C + 2, [[1, 1]]),
                                                sap(s_si, 0, SLOTS, C + 1, [[1, 1]])))
            m_gib = T.ve(lambda e: e.tensor_scalar_mul(
                s_gib[:], sap(s_si, 0, SLOTS, 0, [[1, C]]),
                sap(s_si, 0, SLOTS, C + 2, [[1, 1]])), deps=(m_rw2,))

            prev_gwc = None
            m_gfm = None
            for ch in range(FCH):
                mm = T.te(lambda e, ch=ch: e.matmul(
                    ps[4][:, 0:64], s_gib[:, ch * 128:(ch + 1) * 128],
                    k_id[0:64, 0:64], start=True, stop=True),
                    deps=(m_gib,) if ch == 0 else (m_gfm,))
                m_gfm = T.se(lambda e, ch=ch: e.copy(
                    sap(s_fm64, 0, 128, ch * 64, [[1, 64]]), ps[4][:, 0:64]),
                    deps=(mm,))
            m_hfm = None
            for ch in range(FCH):
                mm = T.te(lambda e, ch=ch: e.matmul(
                    ps[5][:, 0:64], s_slotsb[:, ch * 128:(ch + 1) * 128],
                    k_id[0:64, 0:64], start=True, stop=True),
                    deps=(m_gib,) if ch == 0 else (m_hfm,))
                m_hfm = T.se(lambda e, ch=ch: e.copy(
                    sap(s_fm64b, 0, 128, ch * 64, [[1, 64]]), ps[5][:, 0:64]),
                    deps=(mm,))

            d_bih = T.dma_sy(lambda e: e.dma_start(s_bc[:], din["bih_rep"].ap()))
            m_gx = None
            for n6 in range(6):
                d_wl = T.dma_sy(lambda e, n6=n6: e.dma_start(
                    sap(xf_t, 0, 128, 0, [[512, FCH], [1, 512]]),
                    dap("gru_wihT", n6 * 512,
                        [[GDIM, 128], [128 * GDIM, FCH], [1, 512]])),
                    deps=((m_gx,) if m_gx else ()) + ((prev_gwc,) if prev_gwc else ()),
                    sem="wA")
                d_ws = prev_gwc = T.se(lambda e: e.copy(
                    sap(w_gs, 0, 128, 0, [[1, FCH * 512]]),
                    sap(xf_t, 0, 128, 0, [[1, FCH * 512]])), deps=(d_wl,))
                mmk = None
                for k in range(FCH):
                    dep = (d_ws, m_gfm) if k == 0 else ()
                    mmk = T.te(lambda e, k=k: e.matmul(
                        ps[0][0:64, :], sap(s_fm64, 0, 128, k * 64, [[1, 64]]),
                        sap(w_gs, 0, 128, k * 512, [[1, 512]]),
                        start=(k == 0), stop=(k == FCH - 1)), deps=dep)
                m_gx = T.ve(lambda e, n6=n6: e.tensor_add(
                    sap(s_gx, 0, SLOTS, n6 * 512, [[1, 512]]), ps[0][0:64, :],
                    sap(s_bc, 0, SLOTS, n6 * 512, [[1, 512]])), deps=(mmk, d_bih))
            d_bhh = T.dma_sy(lambda e: e.dma_start(s_bc[:], din["bhh_rep"].ap()),
                             deps=(m_gx,))
            m_gh = None
            for n6 in range(6):
                d_wl = T.dma_sy(lambda e, n6=n6: e.dma_start(
                    sap(xf_t, 0, 128, 0, [[512, FCH], [1, 512]]),
                    dap("gru_whhT", n6 * 512,
                        [[GDIM, 128], [128 * GDIM, FCH], [1, 512]])),
                    deps=((m_gh,) if m_gh else (m_gx,)) + ((prev_gwc,) if prev_gwc else ()),
                    sem="wA")
                d_ws = prev_gwc = T.se(lambda e: e.copy(
                    sap(w_gs, 0, 128, 0, [[1, FCH * 512]]),
                    sap(xf_t, 0, 128, 0, [[1, FCH * 512]])), deps=(d_wl,))
                mmk = None
                for k in range(FCH):
                    dep = (d_ws, m_hfm) if k == 0 else ()
                    mmk = T.te(lambda e, k=k: e.matmul(
                        ps[0][0:64, :], sap(s_fm64b, 0, 128, k * 64, [[1, 64]]),
                        sap(w_gs, 0, 128, k * 512, [[1, 512]]),
                        start=(k == 0), stop=(k == FCH - 1)), deps=dep)
                m_gh = T.ve(lambda e, n6=n6: e.tensor_add(
                    sap(s_gh, 0, SLOTS, n6 * 512, [[1, 512]]), ps[0][0:64, :],
                    sap(s_bc, 0, SLOTS, n6 * 512, [[1, 512]])), deps=(mmk, d_bhh))

            m_rz0 = T.ve(lambda e: e.tensor_add(s_gate2[:, 0:2048], s_gx[:, 0:2048],
                                                s_gh[:, 0:2048]), deps=(m_gh, m_gx))
            m_rz = T.se(lambda e: e.activation(s_gate2[:, 0:2048],
                                               s_gate2[:, 0:2048], AF.Sigmoid),
                        deps=(m_rz0,))
            T.ve(lambda e: e.tensor_mul(s_gtmp[:], s_gate2[:, 0:1024],
                                        s_gh[:, 2048:3072]), deps=(m_rz,))
            m_n0 = T.ve(lambda e: e.tensor_add(s_gtmp[:], s_gtmp[:],
                                               s_gx[:, 2048:3072]))
            m_n = T.se(lambda e: e.activation(s_gtmp[:], s_gtmp[:], AF.Tanh),
                       deps=(m_n0,))
            T.ve(lambda e: e.tensor_sub(s_snew[:], s_slots[:], s_gtmp[:]),
                 deps=(m_n, m_S))
            T.ve(lambda e: e.tensor_mul(s_snew[:], s_snew[:], s_gate2[:, 1024:2048]))
            m_sn = T.ve(lambda e: e.tensor_add(s_snew[:], s_snew[:], s_gtmp[:]))
            m_snb = T.ve(lambda e: e.tensor_copy(s_snb[:], s_snew[:]))
            m_sfm = None
            for ch in range(FCH):
                mm = T.te(lambda e, ch=ch: e.matmul(
                    ps[4][:, 0:64], s_snb[:, ch * 128:(ch + 1) * 128],
                    k_id[0:64, 0:64], start=True, stop=True),
                    deps=(m_snb,) if ch == 0 else (m_sfm,))
                m_sfm = T.se(lambda e, ch=ch: e.copy(
                    sap(s_fm64, 0, 128, ch * 64, [[1, 64]]), ps[4][:, 0:64]),
                    deps=(mm,))

            wv_cs = None
            for kch in range(FCH):
                d_l = T.dma_sy(lambda e, k=kch: e.dma_start(
                    sap(xf_t, 0, 128, (k % 2) * 1024, [[1, C]]),
                    dap("wvT", k * 128 * C, [[C, 128], [1, C]])),
                    deps=((prev_gwc,) if kch == 0 else ()) +
                    ((wv_cs,) if kch >= 2 else ()), sem=("wA", "wB")[kch % 2])
                wv_cs = T.se(lambda e, k=kch: e.copy(
                    sap(w_big, 0, 128, k * C, [[1, C]]),
                    sap(xf_t, 0, 128, (k % 2) * 1024, [[1, C]])), deps=(d_l,))
            d_wv = wv_cs
            d_bv = T.dma_sy(lambda e: e.dma_start(
                sap(s_bc, 0, SLOTS, 0, [[1, C]]), din["bv_rep"].ap()),
                deps=(m_gh,))
            m_sv = None
            for n2 in range(2):
                mmk = None
                for k in range(FCH):
                    dep = (d_wv, m_sfm) if (k == 0 and n2 == 0) else ()
                    if k == 0 and m_sv is not None:
                        dep = dep + (m_sv,)
                    mmk = T.te(lambda e, k=k, n2=n2: e.matmul(
                        ps[0][0:64, :], sap(s_fm64, 0, 128, k * 64, [[1, 64]]),
                        sap(w_big, 0, 128, k * C + n2 * 512, [[1, 512]]),
                        start=(k == 0), stop=(k == FCH - 1)), deps=dep)
                m_sv = T.ve(lambda e, n2=n2: e.tensor_add(
                    sap(s_sv, 0, SLOTS, n2 * 512, [[1, 512]]), ps[0][0:64, :],
                    sap(s_bc, 0, SLOTS, n2 * 512, [[1, 512]])), deps=(mmk, d_bv))
            m_svb = T.ve(lambda e: e.tensor_copy(s_svb[:], s_sv[:]), deps=(m_sv,))
            m_svfm = None
            for ch in range(FCH):
                mm = T.te(lambda e, ch=ch: e.matmul(
                    ps[4][:, 0:64], s_svb[:, ch * 128:(ch + 1) * 128],
                    k_id[0:64, 0:64], start=True, stop=True),
                    deps=(m_svb,) if ch == 0 else (m_svfm,))
                m_svfm = T.se(lambda e, ch=ch: e.copy(
                    sap(s_fm64b, 0, 128, ch * 64, [[1, 64]]), ps[4][:, 0:64]),
                    deps=(mm,))
            wo2_cs = None
            for kch in range(FCH):
                d_l = T.dma_sy(lambda e, k=kch: e.dma_start(
                    sap(xf_t, 0, 128, (k % 2) * 1024, [[1, C]]),
                    dap("woT", k * 128 * C, [[C, 128], [1, C]])),
                    deps=((m_sv,) if kch == 0 else ()) +
                    ((wo2_cs,) if kch >= 2 else ()), sem=("wA", "wB")[kch % 2])
                wo2_cs = T.se(lambda e, k=kch: e.copy(
                    sap(w_big, 0, 128, k * C, [[1, C]]),
                    sap(xf_t, 0, 128, (k % 2) * 1024, [[1, C]])), deps=(d_l,))
            d_wo2 = wo2_cs
            m_svw = None
            for n2 in range(2):
                mmk = None
                for k in range(FCH):
                    dep = (d_wo2, m_svfm) if (k == 0 and n2 == 0) else ()
                    if k == 0 and m_svw is not None:
                        dep = dep + (m_svw,)
                    mmk = T.te(lambda e, k=k, n2=n2: e.matmul(
                        ps[0][0:64, :], sap(s_fm64b, 0, 128, k * 64, [[1, 64]]),
                        sap(w_big, 0, 128, k * C + n2 * 512, [[1, 512]]),
                        start=(k == 0), stop=(k == FCH - 1)), deps=dep)
                m_svw = T.se(lambda e, n2=n2: e.copy(
                    sap(s_svw, 0, SLOTS, n2 * 512, [[1, 512]]), ps[0][0:64, :]),
                    deps=(mmk,))

            m_fin = None
            out_dmas = []
            finl = []
            for grp in range(4):
                m_xf2 = T.dma_sy(lambda e, g=grp: e.dma_start(
                    sap(xf_t, 0, 128, 0, [[512, FCH], [1, 512]]),
                    AP(x2_hbm, g * 512, [[SEG, 128], [128 * SEG, FCH], [1, 512]])),
                    deps=((m_fin,) if m_fin else (m_x2all, wo2_cs)))
                for mch in range(FCH):
                    bank = ps[1 + mch % 4]
                    slot = (grp * FCH + mch) % 2
                    mdep = (m_svw, m_aes) if (mch == 0 and grp == 0) else ()
                    if len(finl) >= 4:
                        mdep = mdep + (finl[-4],)
                    mmk = T.te(lambda e, m=mch, g=grp, b=bank: e.matmul(
                        b[:, :], s_svw[:, m * 128:(m + 1) * 128],
                        sap(s_alpsm, 0, SLOTS, g * 512, [[1, 512]]),
                        start=True, stop=True),
                        deps=mdep)
                    findep = [mmk, m_xf2, vdeps["bo"]]
                    if len(out_dmas) >= 2:
                        findep.append(out_dmas[-2])
                    m_fin = T.ve(lambda e, m=mch, b=bank, sl=slot:
                                 e.scalar_tensor_tensor(
                        sap(s_ofin, 0, 128, sl * 512, [[1, 512]]),
                        b[:, :], sap(vec["bo"], 0, 128, m, [[1, 1]]),
                        sap(xf_t, 0, 128, m * 512, [[1, 512]]), ALU.add, ALU.add),
                        deps=tuple(findep))
                    finl.append(m_fin)
                    out_dmas.append(T.dma_out(lambda e, m=mch, g=grp, sl=slot:
                                              e.dma_start(
                        AP(out_fm, m * 128 * SEG + g * 512, [[SEG, 128], [1, 512]]),
                        sap(s_ofin, 0, 128, sl * 512, [[1, 512]])), deps=(m_fin,),
                        sem=("oA", "oB")[slot]))
            T._rec("ge", lambda e: e.nop(), None, 0,
                   (("oA", T.semn["oA"]), ("oB", T.semn["oB"])))

        with nc.allow_low_precision(reason="bf16 elementwise validated against problem tolerance"):
            with nc.Block() as block:
                T.finalize(block)

    return nc


def kernel(**inputs):
    per_core = _host_prepare(inputs)
    nc = build_program()
    res = run_bass_kernel_spmd(nc, per_core, core_ids=list(range(NCORES)))
    outs = [r["out_fm"] for r in res.results]
    full = np.concatenate([o.T for o in outs], axis=0)
    return np.ascontiguousarray(full[None], dtype=np.float32)


# version 12
# speedup vs baseline: 1.0096x; 1.0096x over previous
import os
import numpy as np
import ml_dtypes
import concourse.bass as bass
import concourse.mybir as mybir
from concourse.bass_utils import run_bass_kernel_spmd

F32 = mybir.dt.float32
BF16 = mybir.dt.bfloat16
AF = mybir.ActivationFunctionType
ALU = mybir.AluOpType
AX = mybir.AxisListType
AP = bass.AP

C = 1024
NCORES = 8
SEG = 2048
WARM = 128
TS = SEG + WARM
L = 32
FCH = 8
TT = 256
TILES = [WARM] + [TT] * (SEG // TT)
EPS = 1e-5
SLOTS = 64
HFF = 512
GDIM = 3 * C

DBG = os.environ.get("KDBG", "") == "1"
KMAXT = int(os.environ.get("KMAXT", "99"))
KSKIP2 = os.environ.get("KSKIP2", "") == "1"
KSTAGE = int(os.environ.get("KSTAGE", "99"))
KREV = int(os.environ.get("KREV", "7"))
KSEL = int(os.environ.get("KSEL", "15"))
KCH = int(os.environ.get("KCH", "99"))
KBRK = int(os.environ.get("KBRK", "1"))
KLAG = int(os.environ.get("KLAG", "0"))


def _host_prepare(inputs):
    g = lambda n: np.ascontiguousarray(np.asarray(inputs["L0_attn_" + n], np.float32))
    f = lambda n: np.ascontiguousarray(np.asarray(inputs["L0_ffn_" + n], np.float32))
    x = np.ascontiguousarray(np.asarray(inputs["L0_attn_hidden_states"], np.float32)[0])

    shared = {}
    for n in ("W_r", "W_k", "W_v", "W_o"):
        shared[n] = g(n)
    for n in ("w_lora_a", "w_lora_b", "a_lora_a", "a_lora_b", "g_lora_a", "g_lora_b"):
        shared[n] = g(n)

    def fmcol(v):
        return np.ascontiguousarray(v.reshape(FCH, 128).T)

    for n in ("x_r", "x_w", "x_k", "x_v", "x_a", "k_k", "k_a",
              "w_lora_bias", "a_lora_bias", "g_lora_bias"):
        shared["v_" + n] = fmcol(g(n))
    shared["v_r_k"] = fmcol(g("r_k").reshape(-1))
    shared["v_bo"] = fmcol(f("bo"))
    shared["gnw_rep"] = np.ascontiguousarray(np.broadcast_to(g("gn_weight").reshape(1, C), (128, C)))
    shared["gnb_rep"] = np.ascontiguousarray(np.broadcast_to(g("gn_bias").reshape(1, C), (128, C)))

    w1 = f("w1")
    shared["w1aT"] = np.ascontiguousarray(w1[:, :C].T)
    shared["w1bT"] = np.ascontiguousarray(w1[:, C:].T)
    shared["w2T"] = np.ascontiguousarray(f("w2").T)
    shared["gru_wihT"] = np.ascontiguousarray(f("gru_wih").T)
    shared["gru_whhT"] = np.ascontiguousarray(f("gru_whh").T)
    shared["wvT"] = np.ascontiguousarray(f("wv").T)
    shared["woT"] = np.ascontiguousarray(f("wo").T)
    shared["slot_init"] = f("slot_init")
    shared["v_slot_scale"] = f("slot_scale").reshape(SLOTS, 1)
    shared["v_b1"] = np.ascontiguousarray(f("b1").reshape(4, 128).T)
    shared["b2_rep"] = np.ascontiguousarray(np.broadcast_to(f("b2").reshape(1, SLOTS), (128, SLOTS)))
    shared["bv_rep"] = np.ascontiguousarray(np.broadcast_to(f("bv").reshape(1, C), (SLOTS, C)))
    shared["bih_rep"] = np.ascontiguousarray(np.broadcast_to(f("gru_bih").reshape(1, GDIM), (SLOTS, GDIM)))
    shared["bhh_rep"] = np.ascontiguousarray(np.broadcast_to(f("gru_bhh").reshape(1, GDIM), (SLOTS, GDIM)))
    shared["tau"] = f("tau").reshape(1, 1)

    shared["identity128"] = np.eye(128, dtype=np.float32)
    shared["ones_col"] = np.ones((128, 1), np.float32)
    shared["ones_row"] = np.ones((1, 128), np.float32)
    shared["ones_rowf"] = np.ones((1, 128), np.float32)
    shared["ones_f32"] = np.ones((128, TT), np.float32)
    onespat = np.zeros((128, FCH, 16), np.float32)
    for ch in range(FCH):
        onespat[0:64, ch, 2 * ch] = 1.0
        onespat[64:128, ch, 2 * ch + 1] = 1.0
    shared["onespat"] = np.ascontiguousarray(onespat.reshape(128, FCH * 16))
    sel2 = np.zeros((2, 128), np.float32)
    sel2[0, 0:64] = 1.0
    sel2[1, 64:128] = 1.0
    shared["sel2"] = sel2
    mtrb = np.zeros((32, 64), np.float32)
    for s in range(32):
        for t in range(32):
            mtrb[s, t] = 1.0 if s < t else 0.0
            mtrb[s, 32 + t] = 1.0 if s <= t else 0.0
    shared["mask_trb"] = np.ascontiguousarray(np.tile(np.tile(mtrb, (4, 1)), (1, 8)))
    mw0 = np.tril(np.ones((32, 32), np.float32), -1)
    shared["mask_w0"] = np.ascontiguousarray(np.tile(np.tile(mw0, (4, 1)), (1, 16)))

    per_core = []
    for cid in range(NCORES):
        lo = cid * SEG
        ext = WARM + 1
        if lo - ext >= 0:
            xs = x[lo - ext:lo + SEG]
        else:
            xs = np.concatenate([np.zeros((ext - lo, C), np.float32), x[:lo + SEG]], 0)
        m = dict(shared)
        m["xT"] = np.ascontiguousarray(xs.T)
        per_core.append(m)
    return per_core


class KB:
    ENGS = ("te", "ve", "se", "ge", "sy")
    SEM_OF = {"te": "pe", "ve": "ve", "se": "se", "ge": "geC", "sy": "syC"}

    class Mark:
        __slots__ = ("sem", "val")

        def __init__(self, sem, val):
            self.sem = sem
            self.val = val

    def __init__(self, nc, es):
        self.nc = nc
        self.recs = {k: [] for k in self.ENGS}
        self.semh = {}
        self.semn = {}
        for name in ("pe", "ve", "se", "geC", "syC", "syQ", "geQ", "scQ",
                     "cc", "outQ", "wA", "wB", "oA", "oB", "x2s"):
            self.semh[name] = es.enter_context(nc.semaphore("s_" + name))
            self.semn[name] = 0

    DMA_SEMS = ("syQ", "geQ", "scQ", "outQ", "wA", "wB", "oA", "oB", "x2s")

    def _rec(self, eng, fn, inc, k, deps):
        own = self.SEM_OF.get(eng)
        if eng != "te" and own is not None and self.semn[own] > 0:
            deps = tuple(deps) + (self.Mark(own, self.semn[own]),)
        if inc in self.DMA_SEMS:
            deps = tuple(deps) + (self.Mark(inc, self.semn[inc]),)
        tgt = None
        if inc is not None:
            self.semn[inc] += k
            tgt = self.Mark(inc, self.semn[inc])
        self.recs[eng].append((fn, inc, k, tuple(d for d in deps
                                                 if d is not None)))
        return tgt

    def te(self, fn, deps=()):
        return self._rec("te", fn, "pe", 1, deps)

    def ve(self, fn, deps=()):
        return self._rec("ve", fn, "ve", 1, deps)

    def se(self, fn, deps=()):
        return self._rec("se", fn, "se", 1, deps)

    def ge(self, fn, deps=()):
        return self._rec("ge", fn, "geC", 1, deps)

    def dma_sy(self, fn, deps=(), sem="syQ"):
        return self._rec("sy", fn, sem, 16, deps)

    def dma_ge(self, fn, deps=(), sem="geQ"):
        return self._rec("ge", fn, sem, 16, deps)

    def dma_sc(self, fn, deps=(), sem="scQ"):
        return self._rec("se", fn, sem, 16, deps)

    def dma_out(self, fn, deps=(), sem=None):
        return self._rec("sy", fn, sem or "outQ", 16, deps)

    def barrier(self):
        marks = [self.te(lambda e: e.nop()),
                 self.ve(lambda e: e.nop()),
                 self.se(lambda e: e.nop()),
                 self.ge(lambda e: e.nop())]
        marks += [self.Mark(s, self.semn[s]) for s in
                  ("syQ", "geQ", "scQ", "wA", "wB", "oA", "oB", "x2s")]
        for eng in ("te", "ve", "se", "ge", "sy"):
            self._rec(eng, lambda e: None, None, 0, marks)
        return marks

    @staticmethod
    def _norm(d):
        if isinstance(d, KB.Mark):
            return (d.sem, d.val)
        return d

    def finalize(self, block):
        te_waits = []
        te_real = []
        waited = {}
        pe_seen = 0
        run_len = 0
        prev_bnd = 0
        for fn, inc, k, deps in self.recs["te"]:
            w = []
            for d in deps:
                sem, val = self._norm(d)
                if sem == "pe":
                    continue
                if waited.get(sem, -1) < val:
                    waited[sem] = val
                    w.append((sem, val))
            te_real.append(bool(w))
            if w:
                prev_bnd = pe_seen
                run_len = 0
            else:
                run_len += 1
                if KBRK and prev_bnd > 0 and run_len >= KBRK:
                    w.append(("pe", prev_bnd))
                    run_len = 0
            te_waits.append(w)
            pe_seen += k if inc else 0
        n_te = len(self.recs["te"])
        run_end = [0] * n_te
        cnt = 0
        ends = []
        starts = []
        pe_cnt = [0] * n_te
        c = 0
        for i in range(n_te):
            c += self.recs["te"][i][2] if self.recs["te"][i][1] else 0
            pe_cnt[i] = c
        i = 0
        while i < n_te:
            j = i + 1
            while j < n_te and not te_real[j]:
                j += 1
            for q in range(i, j):
                run_end[q] = pe_cnt[j - 1]
            i = j
        val2idx = {}
        for i in range(n_te):
            val2idx[pe_cnt[i]] = i

        def resolve(sem, val):
            if sem == "pe" and val in val2idx:
                return (sem, run_end[val2idx[val]])
            return (sem, val)

        decs = {"te": block.tensor, "ve": block.vector, "se": block.scalar,
                "ge": block.gpsimd, "sy": block.sync}
        semh = self.semh
        for eng in self.ENGS:
            if eng == "te":
                oplist = []
                for idx, (fn, inc, k, deps) in enumerate(self.recs["te"]):
                    oplist.append((fn, inc, k, tuple(te_waits[idx])))
            else:
                oplist = []
                waited = {}
                for fn, inc, k, deps in self.recs[eng]:
                    w = []
                    for d in deps:
                        sem, val = resolve(*self._norm(d))
                        if waited.get(sem, -1) < val:
                            waited[sem] = val
                            w.append((sem, val))
                    oplist.append((fn, inc, k, tuple(w)))

            def body(e, oplist=oplist):
                for fn, inc, k, waits in oplist:
                    for sname, val in waits:
                        e.wait_ge(semh[sname], val)
                    ins = fn(e)
                    if inc is not None and ins is not None:
                        ins.then_inc(semh[inc], k)
            decs[eng](body)


def sap(buf, p0, npart, col, free=()):
    pitch = int(np.prod(buf.shape[1:]))
    dims = [[pitch, npart]] + [list(d) for d in free]
    return AP(buf, p0 * pitch + col, dims)


def build_program():
    import contextlib
    nc = bass.Bass("TRN2", target_bir_lowering=False)
    es = contextlib.ExitStack()

    din = {}

    def dram_in(name, shape, dt=F32):
        t = nc.dram_tensor(name, list(shape), dt, kind="ExternalInput")
        din[name] = t

    dram_in("xT", (C, TS + 1))
    for n in ("W_r", "W_k", "W_v", "W_o", "wvT", "woT"):
        dram_in(n, (C, C))
    dram_in("w_lora_a", (C, 64)); dram_in("w_lora_b", (64, C))
    dram_in("a_lora_a", (C, 64)); dram_in("a_lora_b", (64, C))
    dram_in("g_lora_a", (C, 128)); dram_in("g_lora_b", (128, C))
    for n in ("x_r", "x_w", "x_k", "x_v", "x_a", "k_k", "k_a",
              "w_lora_bias", "a_lora_bias", "g_lora_bias", "r_k", "bo"):
        dram_in("v_" + n, (128, FCH))
    dram_in("gnw_rep", (128, C)); dram_in("gnb_rep", (128, C))
    dram_in("w1aT", (C, HFF)); dram_in("w1bT", (C, HFF)); dram_in("w2T", (HFF, SLOTS))
    dram_in("gru_wihT", (C, GDIM)); dram_in("gru_whhT", (C, GDIM))
    dram_in("slot_init", (SLOTS, C)); dram_in("v_slot_scale", (SLOTS, 1))
    dram_in("v_b1", (128, 4)); dram_in("b2_rep", (128, SLOTS))
    dram_in("bv_rep", (SLOTS, C))
    dram_in("bih_rep", (SLOTS, GDIM)); dram_in("bhh_rep", (SLOTS, GDIM))
    dram_in("tau", (1, 1))
    dram_in("identity128", (128, 128)); dram_in("ones_col", (128, 1))
    dram_in("ones_row", (1, 128)); dram_in("ones_rowf", (1, 128))
    dram_in("ones_f32", (128, TT))
    dram_in("onespat", (128, FCH * 16)); dram_in("sel2", (2, 128))
    dram_in("mask_trb", (128, 512)); dram_in("mask_w0", (128, 512))

    def dap(name, col, dims):
        return AP(din[name], col, dims)

    out_fm = nc.dram_tensor("out_fm", [C, SEG], F32, kind="ExternalOutput")
    dbg = {}

    def dbg_t(name, shape, dt=F32):
        if DBG and name not in dbg:
            dbg[name] = nc.dram_tensor("dbg_" + name, list(shape), dt,
                                       kind="ExternalOutput")
        return dbg.get(name)

    wsc = {n: nc.dram_tensor("wsc_" + n, [C, C], BF16, kind="Internal")
           for n in ("W_r", "W_k", "W_v")}
    cc_in = nc.dram_tensor("cc_in", [SLOTS, C + 16], F32, kind="Internal").ap().opt()
    cc_out = nc.dram_tensor("cc_out", [SLOTS, C + 16], F32, kind="Internal").ap().opt()
    x2_hbm = nc.dram_tensor("x2_hbm", [C, SEG], F32, kind="Internal")

    with es:
        T = KB(nc, es)

        def sbuf(name, shape, dt):
            return es.enter_context(nc.sbuf_tensor(name, list(shape), dt))

        k_id = sbuf("k_id", (128, 128), BF16)
        k_onescol = sbuf("k_onescol", (128, 1), BF16)
        k_onescolf = sbuf("k_onescolf", (128, 1), F32)
        k_onesrow = sbuf("k_onesrow", (1, 128), BF16)
        k_onesrowf = sbuf("k_onesrowf", (1, 128), F32)
        k_onesf32 = sbuf("k_onesf32", (128, TT), F32)
        k_onespat = sbuf("k_onespat", (128, FCH * 16), BF16)
        k_sel2 = sbuf("k_sel2", (2, 128), BF16)
        k_mtrb = sbuf("k_mtrb", (128, 512), BF16)
        k_mw0 = sbuf("k_mw0", (128, 512), BF16)
        vec = {}
        for n in ("x_r", "x_w", "x_k", "x_v", "x_a", "k_k", "k_a",
                  "w_lora_bias", "a_lora_bias", "g_lora_bias", "r_k", "bo"):
            vec[n] = sbuf("vec_" + n, (128, FCH), F32)
        v_omka = sbuf("v_omka", (128, FCH), F32)
        v_negwb = sbuf("v_negwb", (128, FCH), F32)
        gw_tm = sbuf("gw_tm", (128, C), BF16)
        gb_tm = sbuf("gb_tm", (128, C), BF16)
        k_consts = sbuf("k_consts", (128, 4), F32)
        wo_in = sbuf("wo_in", (128, FCH * SEG), BF16)

        ps = [es.enter_context(nc.psum_tensor("ps%d" % i, [128, 512], F32))
              for i in range(8)]

        dep_c = []
        dep_c.append(T.dma_sy(lambda e: e.dma_start(k_onesf32[:], din["ones_f32"].ap())))
        dep_c.append(T.dma_sy(lambda e: e.dma_start(k_onesrowf[:], din["ones_rowf"].ap())))
        dep_c.append(T.dma_sy(lambda e: e.dma_start(k_onescolf[:],
                                                    AP(din["ones_col"], 0, [[1, 128], [1, 1]]))))
        vdeps = {}
        for n in vec:
            vdeps[n] = T.dma_sy(lambda e, n=n: e.dma_start(vec[n][:], din["v_" + n].ap()))
        for name, buf in (("identity128", k_id), ("ones_col", k_onescol),
                          ("ones_row", k_onesrow), ("onespat", k_onespat),
                          ("sel2", k_sel2), ("mask_trb", k_mtrb), ("mask_w0", k_mw0)):
            dep_c.append(T.dma_ge(lambda e, b=buf, nm=name: e.dma_start(b[:], din[nm].ap())))
        T.ve(lambda e: e.memset(k_consts[:, 0:1], EPS))
        T.ve(lambda e: e.memset(k_consts[:, 1:2], -1.0))
        T.ve(lambda e: e.memset(k_consts[:, 2:3], -0.5))
        nc.const_aps.register(F32, EPS, k_consts[:, 0:1])
        nc.const_aps.register(F32, -1.0, k_consts[:, 1:2])
        nc.const_aps.register(F32, -0.5, k_consts[:, 2:3])
        m_omka = T.ve(lambda e: e.tensor_scalar(v_omka[:], vec["k_a"][:], -1.0, 1.0,
                                                ALU.mult, ALU.add), deps=(vdeps["k_a"],))
        m_negwb = T.ve(lambda e: e.tensor_scalar_mul(v_negwb[:], vec["w_lora_bias"][:],
                                                     -1.0), deps=(vdeps["w_lora_bias"],))
        T.barrier()

        p1 = contextlib.ExitStack()
        with p1:
            def sb1(name, shape, dt):
                return p1.enter_context(nc.sbuf_tensor(name, list(shape), dt))

            w_wla = sb1("w_wla", (128, FCH * 64), BF16)
            w_wlb = sb1("w_wlb", (64, C), BF16)
            w_ala = sb1("w_ala", (128, FCH * 64), BF16)
            w_alb = sb1("w_alb", (64, C), BF16)
            w_gla = sb1("w_gla", (128, FCH * 128), BF16)
            w_glb = sb1("w_glb", (128, C), BF16)
            w_st = sb1("w_st", (128, 2 * C), BF16)

            z_f32 = sb1("z_f32", (128, FCH * 64), F32)
            z_bf = sb1("z_bf", (128, FCH * 64), BF16)
            g_carry = sb1("g_carry", (128, FCH), F32)

            NW1 = TT + 1
            xf = sb1("xf", (128, FCH * NW1), F32)
            ln_f = sb1("ln_f", (128, FCH * NW1), F32)
            cb_f = sb1("cb_f", (128, 2 * NW1), F32)
            dd_b = sb1("dd_b", (128, FCH * TT), BF16)
            lrp = sb1("lrp", (128, FCH * TT), BF16)
            r_b = sb1("r_b", (128, FCH * TT), BF16)
            v_b = sb1("v_b", (128, FCH * TT), BF16)
            ah_f = sb1("ah_f", (128, FCH * TT), F32)
            kk_f = sb1("kk_f", (128, FCH * TT), F32)
            sp_f = sb1("sp_f", (128, FCH * TT), F32)
            g_f = sb1("g_f", (128, FCH * TT), F32)
            gbase = sb1("gbase", (128, FCH * 8), F32)
            clc_f = sb1("clc_f", (128, FCH * 8), F32)
            d1_f = sb1("d1_f", (128, FCH * TT), F32)
            bk_b = sb1("bk_b", (128, 2 * FCH * TT), BF16)
            ar_b = sb1("ar_b", (128, 2 * FCH * TT), BF16)
            bh_b = sb1("bh_b", (128, FCH * TT), BF16)
            kh_b = sb1("kh_b", (128, FCH * TT), BF16)
            gate_b = sb1("gate_b", (128, FCH * TT), BF16)
            bonus_b = sb1("bonus_b", (128, FCH * TT), BF16)
            scr_b = sb1("scr_b", (128, FCH * NW1), BF16)
            mid_b = sb1("mid_b", (128, TT), BF16)
            row_f = sb1("row_f", (1, 4 * NW1), F32)
            row_c0 = sb1("row_c0", (1, NW1), F32)
            row_c1 = sb1("row_c1", (1, NW1), F32)
            nrm_b = sb1("nrm_b", (2, FCH * TT), BF16)
            stm = sb1("stm", (128, 3 * 1024), BF16)
            strb = sb1("strb", (128, 1024), BF16)
            smrk = sb1("smrk", (128, 1024), BF16)
            sw0 = sb1("sw0", (128, 512), BF16)
            ssq = sb1("ssq", (128, 7 * 512), BF16)
            sxp_f = [sb1("sxpf0", (128, 1024), F32), sb1("sxpf1", (128, 1024), F32)]
            sxp_b = sb1("sxp_b", (128, 1024), BF16)
            o_tm = xf
            gnst = sb1("gnst", (128, 6 * 16), F32)

            def fmS(buf, ch, col, n, width=TT):
                return sap(buf, 0, 128, ch * width + col, [[1, n]])

            def fm3(buf, col, n, width=TT, nch=FCH, base=0):
                return sap(buf, 0, 128, base + col, [[width, nch], [1, n]])

            def ldw_small(dst, nm, mcols):
                return T.dma_ge(lambda e, d=dst, nm=nm, m=mcols: e.dma_start(
                    sap(d, 0, 128, 0, [[m, FCH], [1, m]]),
                    dap(nm, 0, [[m, 128], [128 * m, FCH], [1, m]])))

            ldw_small(w_wla, "w_lora_a", 64)
            ldw_small(w_ala, "a_lora_a", 64)
            ldw_small(w_gla, "g_lora_a", 128)
            T.dma_ge(lambda e: e.dma_start(w_wlb[:], din["w_lora_b"].ap()))
            T.dma_ge(lambda e: e.dma_start(w_alb[:], din["a_lora_b"].ap()))
            T.dma_ge(lambda e: e.dma_start(w_glb[:], din["g_lora_b"].ap()))
            d_lora = ("geQ", T.semn["geQ"])
            d_wla = d_ala = d_gla = d_wlb = d_alb = d_glb = d_lora

            T.ve(lambda e: e.memset(z_f32[:], 0.0))
            m_zb0 = T.ve(lambda e: e.memset(z_bf[:], 0.0))
            m_zi = T.ve(lambda e: e.memset(g_carry[:], 0.0))

            wl_marks = [None, None]
            
            wc_marks = [None, None]
            ws_marks = [None, None]
            blocks = [(nm, k) for nm in ("W_k", "W_r", "W_v") for k in range(FCH)]
            for i, (nm, k) in enumerate(blocks):
                h = i % 2
                ldep = (wc_marks[h],) if wc_marks[h] else ()
                wl_marks[h] = T.dma_sy(
                    lambda e, nm=nm, k=k, h=h: e.dma_start(
                        sap(xf, 0, 128, h * 1024, [[1, C]]),
                        dap(nm, k * 128 * C, [[C, 128], [1, C]])),
                    deps=ldep, sem=("wA", "wB")[h])
                cdep = (wl_marks[h],) + ((ws_marks[h],) if ws_marks[h] else ())
                wc_marks[h] = T.se(lambda e, h=h: e.copy(
                    sap(w_st, 0, 128, h * 1024, [[1, C]]),
                    sap(xf, 0, 128, h * 1024, [[1, C]])), deps=cdep)
                ws_marks[h] = T.dma_sc(
                    lambda e, nm=nm, k=k, h=h: e.dma_start(
                        AP(wsc[nm], k * 128 * C, [[C, 128], [1, C]]),
                        sap(w_st, 0, 128, h * 1024, [[1, C]])),
                    deps=(wc_marks[h],), sem=("oA", "oB")[h])
            m_wsc = (("oA", T.semn["oA"]), ("oB", T.semn["oB"]),
                     ("wA", T.semn["wA"]), ("wB", T.semn["wB"]))
            d_gnw = T.dma_sy(lambda e: e.dma_start(
                sap(xf, 0, 128, 0, [[1, C]]), din["gnw_rep"].ap()),
                deps=(wc_marks[0],), sem="wA")
            d_gnb = T.dma_sc(lambda e: e.dma_start(
                sap(xf, 0, 128, 1024, [[1, C]]), din["gnb_rep"].ap()),
                deps=(wc_marks[1],), sem="wB")
            T.se(lambda e: e.copy(gw_tm[:], sap(xf, 0, 128, 0, [[1, C]])),
                 deps=(d_gnw, ws_marks[0]))
            m_gnbc = T.se(lambda e: e.copy(gb_tm[:], sap(xf, 0, 128, 1024, [[1, C]])),
                          deps=(d_gnb, ws_marks[1]))

            scan_t0 = 0
            prev_zb = m_zb0
            prev_gn_add = None
            for ti, TTi in enumerate(TILES[:KMAXT]):
                is_warm = (ti == 0)
                ncht = TTi // L
                NW = TTi + 1

                m_xf = T.dma_sc(lambda e, t0=scan_t0, n=NW: e.dma_start(
                    fm3(xf, 0, n, width=NW1),
                    dap("xT", t0, [[TS + 1, 128], [128 * (TS + 1), FCH], [1, n]])),
                    deps=(m_gnbc,) if ti == 0 else ())

                for ch in range(FCH):
                    mm = T.te(lambda e, ch=ch, n=NW: e.matmul(
                        ps[4][0:1, 0:n], k_onescolf[:], fmS(xf, ch, 0, n, width=NW1),
                        start=(ch == 0), stop=(ch == FCH - 1)),
                        deps=(m_xf,) if ch == 0 else ())
                m_sq = T.se(lambda e, n=NW: e.activation(
                    sap(scr_b, 0, 128, 0, [[1, FCH * n]]) if n == NW1 else
                    fm3(scr_b, 0, n, width=NW1),
                    sap(xf, 0, 128, 0, [[1, FCH * n]]) if n == NW1 else
                    fm3(xf, 0, n, width=NW1),
                    AF.Square), deps=(m_xf,))
                for ch in range(FCH):
                    mm = T.te(lambda e, ch=ch, n=NW: e.matmul(
                        ps[5][0:1, 0:n], k_onescol[:], fmS(scr_b, ch, 0, n, width=NW1),
                        start=(ch == 0), stop=(ch == FCH - 1)),
                        deps=(m_sq,) if ch == 0 else ())
                m_stat = mm
                T.ve(lambda e, n=NW: e.tensor_scalar_mul(AP(row_f, 0 * NW1, [[4 * NW1, 1], [1, n]]),
                                                         ps[4][0:1, 0:n], 1.0 / C),
                     deps=(m_stat,))
                T.ve(lambda e, n=NW: e.tensor_mul(AP(row_f, 1 * NW1, [[4 * NW1, 1], [1, n]]), AP(row_f, 0 * NW1, [[4 * NW1, 1], [1, n]]),
                                                  AP(row_f, 0 * NW1, [[4 * NW1, 1], [1, n]])))
                m_var = T.ve(lambda e, n=NW: e.scalar_tensor_tensor(
                    AP(row_f, 2 * NW1, [[4 * NW1, 1], [1, n]]), ps[5][0:1, 0:n], 1.0 / C, AP(row_f, 1 * NW1, [[4 * NW1, 1], [1, n]]),
                    ALU.mult, ALU.subtract))
                m_sd = T.se(lambda e, n=NW: e.activation(AP(row_f, 3 * NW1, [[4 * NW1, 1], [1, n]]),
                                                         AP(row_f, 2 * NW1, [[4 * NW1, 1], [1, n]]), AF.Sqrt,
                                                         bias=EPS), deps=(m_var,))
                T.ve(lambda e, n=NW: e.reciprocal(AP(row_f, 2 * NW1, [[4 * NW1, 1], [1, n]]), AP(row_f, 3 * NW1, [[4 * NW1, 1], [1, n]])),
                     deps=(m_sd,))
                T.ve(lambda e, n=NW: e.tensor_copy(row_c1[0:1, 0:n], AP(row_f, 2 * NW1, [[4 * NW1, 1], [1, n]])))
                m_rows = T.ve(lambda e, n=NW: e.scalar_tensor_tensor(
                    row_c0[0:1, 0:n], AP(row_f, 0 * NW1, [[4 * NW1, 1], [1, n]]), -1.0, AP(row_f, 2 * NW1, [[4 * NW1, 1], [1, n]]),
                    ALU.mult, ALU.mult))
                T.te(lambda e, n=NW: e.matmul(ps[6][:, 0:n], k_onesrowf[0:1, :],
                                              row_c0[0:1, 0:n], start=True, stop=True),
                     deps=(m_rows,))
                m_bc = T.te(lambda e, n=NW: e.matmul(ps[7][:, 0:n], k_onesrowf[0:1, :],
                                                     row_c1[0:1, 0:n],
                                                     start=True, stop=True))
                T.se(lambda e, n=NW: e.copy(sap(cb_f, 0, 128, 0, [[1, n]]),
                                            ps[6][:, 0:n]), deps=(m_bc,))
                m_cb = T.se(lambda e, n=NW: e.copy(sap(cb_f, 0, 128, NW1, [[1, n]]),
                                                   ps[7][:, 0:n]))
                T.ve(lambda e, n=NW: e.tensor_mul(
                    fm3(ln_f, 0, n, width=NW1), fm3(xf, 0, n, width=NW1),
                    sap(cb_f, 0, 128, NW1, [[0, FCH], [1, n]])), deps=(m_cb,))
                m_ln = T.ve(lambda e, n=NW: e.tensor_add(
                    fm3(ln_f, 0, n, width=NW1), fm3(ln_f, 0, n, width=NW1),
                    sap(cb_f, 0, 128, 0, [[0, FCH], [1, n]])))
                m_dd = T.ve(lambda e, n=TTi: e.tensor_sub(
                    fm3(dd_b, 0, n), fm3(ln_f, 0, n, width=NW1),
                    fm3(ln_f, 1, n, width=NW1)))

                def lerp(nm, extra=()):
                    mk = None
                    for ch in range(FCH):
                        mk = T.ve(lambda e, ch=ch, n=TTi, nm=nm: e.scalar_tensor_tensor(
                            fmS(lrp, ch, 0, n), fmS(dd_b, ch, 0, n),
                            sap(vec[nm], 0, 128, ch, [[1, 1]]),
                            sap(ln_f, 0, 128, ch * NW1 + 1, [[1, n]]),
                            ALU.mult, ALU.add),
                            deps=((m_dd, vdeps[nm]) + tuple(extra)) if ch == 0 else ())
                    return mk

                def proj(wname, m_lrp, evac_fn, extra=()):
                    d_w = [None, None]
                    d_w[0] = T.dma_sy(lambda e, nm=wname: e.dma_start(
                        sap(w_st, 0, 128, 0, [[1, C]]),
                        AP(wsc[nm], 0, [[C, 128], [1, C]])),
                        deps=(m_lrp,) + m_wsc, sem="wA")
                    mmk = None
                    for kch in range(FCH):
                        if kch + 1 < FCH:
                            dmaf = T.dma_sy if (kch + 1) % 2 == 0 else T.dma_sc
                            d_w[(kch + 1) % 2] = dmaf(
                                lambda e, nm=wname, k=kch + 1: e.dma_start(
                                    sap(w_st, 0, 128, (k % 2) * C, [[1, C]]),
                                    AP(wsc[nm], k * 128 * C, [[C, 128], [1, C]])),
                                deps=((mmk,) if kch >= 1 else m_wsc),
                                sem=("wA", "wB")[(kch + 1) % 2])
                        for mch in range(FCH):
                            bank = ps[mch]
                            dep = ()
                            if mch == 0:
                                dep = (d_w[kch % 2], m_lrp)
                                if kch == 0:
                                    dep = dep + tuple(extra)
                            mmk = T.te(lambda e, m=mch, k=kch, b=bank, n=TTi:
                                       e.matmul(
                                b[:, 0:n],
                                sap(w_st, 0, 128, (k % 2) * C + m * 128, [[1, 128]]),
                                fmS(lrp, k, 0, n),
                                start=(k == 0), stop=(k == FCH - 1)), deps=dep)
                    marks = []
                    for mch in range(FCH):
                        marks.append(evac_fn(mch, ps[mch],
                                             (mmk,) if mch == 0 else ()))
                    return marks

                if KSTAGE >= 12:
                    m_lrp = lerp("x_w")
                    mmk = None
                    for kch in range(FCH):
                        dep = (m_lrp, d_wla) if kch == 0 else ()
                        mmk = T.te(lambda e, k=kch, n=TTi: e.matmul(
                            ps[6][0:64, 0:n], sap(w_wla, 0, 128, k * 64, [[1, 64]]),
                            fmS(lrp, k, 0, n), start=(k == 0), stop=(k == FCH - 1)), deps=dep)
                    m_wmid = T.se(lambda e, n=TTi: e.activation(mid_b[0:64, 0:n],
                                                                ps[6][0:64, 0:n], AF.Tanh),
                                  deps=(mmk,))
                    ev_w = None
                    evl = []
                    for mch in range(FCH):
                        bank = ps[6 + mch % 2]
                        dep = (m_wmid, d_wlb, m_negwb) if mch == 0 else ()
                        if mch >= 2:
                            dep = dep + (evl[mch - 2],)
                        mmk = T.te(lambda e, m=mch, b=bank, n=TTi: e.matmul(
                            b[:, 0:n], sap(w_wlb, 0, 64, m * 128, [[1, 128]]),
                            mid_b[0:64, 0:n], start=True, stop=True), deps=dep)
                        ev_x = T.se(lambda e, m=mch, b=bank, n=TTi: e.activation(
                            fmS(g_f, m, 0, n), b[:, 0:n], AF.Exp,
                            bias=sap(v_negwb, 0, 128, m, [[1, 1]]), scale=-1.0),
                            deps=(mmk,))
                        evl.append(ev_x)
                        ev_w = T.se(lambda e, m=mch, n=TTi: e.activation(
                            fmS(sp_f, m, 0, n), fmS(g_f, m, 0, n),
                            AF.Ln, bias=1.0))
                    m_w = T.ve(lambda e, n=TTi: e.tensor_scalar(
                        fm3(sp_f, 0, n), fm3(sp_f, 0, n), -1.0, -0.5, ALU.mult, ALU.add),
                        deps=(ev_w,))


                if KSTAGE >= 13:
                    mG = None
                    for ch in range(FCH):
                        dep = (m_w, m_zi) if ch == 0 else ()
                        mG = T.ve(lambda e, ch=ch, n=TTi: e.tensor_tensor_scan(
                            fmS(g_f, ch, 0, n), k_onesf32[:, 0:n], fmS(sp_f, ch, 0, n),
                            sap(g_carry, 0, 128, ch, [[1, 1]]), ALU.mult, ALU.add), deps=dep)
                    T.ve(lambda e: e.tensor_sub(
                        sap(gbase, 0, 128, 0, [[8, FCH], [1, 1]]),
                        sap(g_f, 0, 128, 0, [[TT, FCH], [1, 1]]),
                        sap(sp_f, 0, 128, 0, [[TT, FCH], [1, 1]])))
                    if ncht > 1:
                        m_gb = T.ve(lambda e, nj=ncht: e.tensor_copy(
                            sap(gbase, 0, 128, 1, [[8, FCH], [1, nj - 1]]),
                            sap(g_f, 0, 128, L - 1, [[TT, FCH], [L, nj - 1]])))
                    else:
                        m_gb = mG
                    m_gc = T.ve(lambda e, n=TTi: e.tensor_copy(
                        g_carry[:, 0:FCH],
                        sap(g_f, 0, 128, n - 1, [[TT, FCH], [1, 1]])))
                    m_a1 = T.ve(lambda e, nj=ncht: e.tensor_sub(
                        sap(sp_f, 0, 128, 0, [[TT, FCH], [L, nj], [1, L]]),
                        sap(g_f, 0, 128, 0, [[TT, FCH], [L, nj], [1, L]]),
                        sap(gbase, 0, 128, 0, [[8, FCH], [1, nj], [0, L]])))
                    m_d1 = T.se(lambda e, n=TTi: e.activation(fm3(d1_f, 0, n),
                                                              fm3(sp_f, 0, n), AF.Exp),
                                deps=(m_a1,))
                    m_clc = T.se(lambda e, nj=ncht: e.activation(
                        sap(clc_f, 0, 128, 0, [[8, FCH], [1, nj]]),
                        sap(sp_f, 0, 128, L - 1, [[TT, FCH], [L, nj]]), AF.Exp))
                    m_d3 = T.ve(lambda e, n=TTi: e.reciprocal(fm3(sp_f, 0, n),
                                                              fm3(d1_f, 0, n)),
                                deps=(m_d1, m_clc))


                if KSTAGE >= 14:
                    m_lrp = lerp("x_a", extra=(m_w,))
                    mmk = None
                    for kch in range(FCH):
                        dep = (m_lrp, d_ala) if kch == 0 else ()
                        mmk = T.te(lambda e, k=kch, n=TTi: e.matmul(
                            ps[6][0:64, 0:n], sap(w_ala, 0, 128, k * 64, [[1, 64]]),
                            fmS(lrp, k, 0, n), start=(k == 0), stop=(k == FCH - 1)), deps=dep)
                    m_amid = T.se(lambda e, n=TTi: e.copy(mid_b[0:64, 0:n], ps[6][0:64, 0:n]),
                                  deps=(mmk, m_d1))
                    ev_a = None
                    evl = []
                    for mch in range(FCH):
                        bank = ps[6 + mch % 2]
                        dep = (m_amid, d_alb) if mch == 0 else ()
                        if mch >= 2:
                            dep = dep + (evl[mch - 2],)
                        mmk = T.te(lambda e, m=mch, b=bank, n=TTi: e.matmul(
                            b[:, 0:n], sap(w_alb, 0, 64, m * 128, [[1, 128]]),
                            mid_b[0:64, 0:n], start=True, stop=True), deps=dep)
                        ev_a = T.se(lambda e, m=mch, b=bank, n=TTi: e.activation(
                            fmS(ah_f, m, 0, n), b[:, 0:n], AF.Sigmoid,
                            bias=sap(vec["a_lora_bias"], 0, 128, m, [[1, 1]])), deps=(mmk,))
                        evl.append(ev_a)
                    m_ah = ev_a

                    m_kms = None
                    for ch in range(FCH):
                        dep = (m_ah, m_omka) if ch == 0 else ()
                        m_kms = T.ve(lambda e, ch=ch, n=TTi: e.tensor_scalar(
                            fmS(scr_b, ch, 0, n, width=NW1), fmS(ah_f, ch, 0, n),
                            sap(vec["k_a"], 0, 128, ch, [[1, 1]]),
                            sap(v_omka, 0, 128, ch, [[1, 1]]), ALU.mult, ALU.add), deps=dep)


                if KSTAGE >= 15:
                    m_lrp = lerp("x_k", extra=(m_kms,))

                    def evac_k(mch, bank, dep0):
                        dep = dep0 + ((m_kms, vdeps["k_k"], m_d3) if mch == 0 else ())
                        T.ve(lambda e, m=mch, b=bank, n=TTi: e.tensor_scalar_mul(
                            fmS(kk_f, m, 0, n), b[:, 0:n],
                            sap(vec["k_k"], 0, 128, m, [[1, 1]])), deps=dep)
                        return T.ve(lambda e, m=mch, b=bank, n=TTi: e.tensor_mul(
                            fmS(g_f, m, 0, n), b[:, 0:n],
                            fmS(scr_b, m, 0, n, width=NW1)))
                    m_kev = proj("W_k", m_lrp, evac_k, extra=(m_ah,))
                    m_k = m_kev[-1]

                    m_kb2 = T.ve(lambda e, n=TTi: e.tensor_mul(
                        fm3(g_f, 0, n), fm3(g_f, 0, n), fm3(sp_f, 0, n)),
                        deps=(m_k, m_d3))
                    m_bkF = T.se(lambda e, n=TTi: e.copy(
                        fm3(bk_b, 0, n, base=FCH * TT), fm3(g_f, 0, n)), deps=(m_kb2,))
                    m_kh = T.ve(lambda e, nj=ncht: e.tensor_mul(
                        sap(kh_b, 0, 128, 0, [[TT, FCH], [L, nj], [1, L]]),
                        sap(g_f, 0, 128, 0, [[TT, FCH], [L, nj], [1, L]]),
                        sap(d1_f, 0, 128, L - 1, [[TT, FCH], [L, nj], [0, L]])),
                        deps=(m_kb2,))


                if KSTAGE >= 16:
                    m_ksq = T.se(lambda e, n=TTi: e.activation(fm3(scr_b, 0, n, width=NW1),
                                                               fm3(kk_f, 0, n), AF.Square),
                                 deps=(m_k, m_kh))
                    m_nrm = None
                    for ch in range(FCH):
                        mm = T.te(lambda e, ch=ch, n=TTi: e.matmul(
                            ps[6][0:2, 0:n],
                            sap(k_onespat, 0, 128, 0, [[1, 2]]),
                            fmS(scr_b, ch, 0, n, width=NW1),
                            start=True, stop=True),
                            deps=(m_ksq,) if ch == 0 else ((m_nrm,) if m_nrm else ()))
                        m_nrm = T.se(lambda e, ch=ch, n=TTi: e.activation(
                            sap(nrm_b, 0, 2, ch * TT, [[1, n]]),
                            ps[6][0:2, 0:n], AF.Sqrt), deps=(mm,))
                    T.ve(lambda e, n=TTi: e.tensor_scalar_max(
                        sap(nrm_b, 0, 2, 0, [[TT, FCH], [1, n]]),
                        sap(nrm_b, 0, 2, 0, [[TT, FCH], [1, n]]), 1e-12), deps=(m_nrm,))
                    T.ve(lambda e, n=TTi: e.reciprocal(
                        sap(nrm_b, 0, 2, 0, [[TT, FCH], [1, n]]),
                        sap(nrm_b, 0, 2, 0, [[TT, FCH], [1, n]])))
                    m_ninv = T.ve(lambda e, n=TTi: e.tensor_scalar_mul(
                        sap(nrm_b, 0, 2, 0, [[TT, FCH], [1, n]]),
                        sap(nrm_b, 0, 2, 0, [[TT, FCH], [1, n]]), -1.0))
                    mkk = None
                    for ch in range(FCH):
                        dep = (m_ninv,) if ch == 0 else (mkk,)
                        mmb = T.te(lambda e, ch=ch, n=TTi: e.matmul(
                            ps[7][:, 0:n], k_sel2[:],
                            sap(nrm_b, 0, 2, ch * TT, [[1, n]]),
                            start=True, stop=True), deps=dep)
                        mkk = T.ve(lambda e, ch=ch, n=TTi: e.tensor_mul(
                            fmS(kk_f, ch, 0, n), fmS(kk_f, ch, 0, n), ps[7][:, 0:n]),
                            deps=(mmb,))
                    m_kkneg = mkk

                    FT = FCH * TT
                    m_ab1 = T.ve(lambda e, nj=ncht: e.tensor_mul(
                        sap(ar_b, 0, 128, 1, [[TT, FCH], [L, nj], [1, L - 1]]),
                        sap(kk_f, 0, 128, 1, [[TT, FCH], [L, nj], [1, L - 1]]),
                        sap(d1_f, 0, 128, 0, [[TT, FCH], [L, nj], [1, L - 1]])),
                        deps=(m_kkneg,))
                    m_ab2 = T.ve(lambda e, nj=ncht: e.tensor_copy(
                        sap(ar_b, 0, 128, 0, [[TT, FCH], [L, nj], [1, 1]]),
                        sap(kk_f, 0, 128, 0, [[TT, FCH], [L, nj], [1, 1]])))
                    m_bb1 = T.ve(lambda e, n=TTi: e.tensor_mul(
                        fm3(g_f, 0, n), fm3(kk_f, 0, n), fm3(ah_f, 0, n)),
                        deps=(m_kkneg, m_bkF))
                    m_bb2 = T.ve(lambda e, n=TTi: e.scalar_tensor_tensor(
                        fm3(g_f, 0, n), fm3(g_f, 0, n), -1.0,
                        fm3(sp_f, 0, n), ALU.mult, ALU.mult))
                    m_bk1 = T.se(lambda e, n=TTi: e.copy(
                        fm3(bk_b, 0, n), fm3(g_f, 0, n)), deps=(m_bb2,))
                    m_bh = T.ve(lambda e, nj=ncht: e.tensor_mul(
                        sap(bh_b, 0, 128, 0, [[TT, FCH], [L, nj], [1, L]]),
                        sap(g_f, 0, 128, 0, [[TT, FCH], [L, nj], [1, L]]),
                        sap(d1_f, 0, 128, L - 1, [[TT, FCH], [L, nj], [0, L]])),
                        deps=(m_bb2,))


                if KSTAGE >= 17:
                    m_lrp = lerp("x_r", extra=(m_bh,))

                    r_se = []

                    def evac_r(mch, bank, dep0):
                        dep = dep0 + ((m_bh, m_bk1, vdeps["r_k"]) if mch == 0 else ())
                        T.ve(lambda e, m=mch, b=bank, n=TTi: e.tensor_mul(
                            sap(ar_b, 0, 128, FT + m * TT, [[1, n]]), b[:, 0:n],
                            fmS(d1_f, m, 0, n)), deps=dep)
                        mvst = T.ve(lambda e, m=mch, b=bank, n=TTi: e.scalar_tensor_tensor(
                            fmS(g_f, m, 0, n), b[:, 0:n],
                            sap(vec["r_k"], 0, 128, m, [[1, 1]]),
                            sap(bk_b, 0, 128, FT + m * TT, [[1, n]]),
                            ALU.mult, ALU.mult))
                        r_se.append(T.se(lambda e, m=mch, b=bank, n=TTi: e.copy(
                            fmS(r_b, m, 0, n), b[:, 0:n]), deps=(mvst,)))
                        return mvst
                    m_rev = proj("W_r", m_lrp, evac_r, extra=(m_kkneg, m_nrm))
                    m_r = m_rev[-1]

                    if KSTAGE >= 18:
                        mmk = None
                        for kch in range(FCH):
                            dep = (m_r, r_se[-1], d_gla) if kch == 0 else ()
                            mmk = T.te(lambda e, k=kch, n=TTi: e.matmul(
                                ps[4][:, 0:n], sap(w_gla, 0, 128, k * 128, [[1, 128]]),
                                fmS(r_b, k, 0, n), start=(k == 0), stop=(k == FCH - 1)),
                                deps=dep)
                        m_gmid = T.se(lambda e, n=TTi: e.activation(mid_b[:, 0:n], ps[4][:, 0:n],
                                                                    AF.Sigmoid), deps=(mmk,))
                        ev_g = None
                        evl = []
                        for mch in range(FCH):
                            bank = ps[4 + mch % 2]
                            dep = (m_gmid, d_glb) if mch == 0 else ()
                            if mch >= 2:
                                dep = dep + (evl[mch - 2],)
                            mmk = T.te(lambda e, m=mch, b=bank, n=TTi: e.matmul(
                                b[:, 0:n], sap(w_glb, 0, 128, m * 128, [[1, 128]]),
                                mid_b[:, 0:n], start=True, stop=True), deps=dep)
                            ev_g = T.se(lambda e, m=mch, b=bank, n=TTi: e.activation(
                                fmS(gate_b, m, 0, n), b[:, 0:n], AF.Identity,
                                bias=sap(vec["g_lora_bias"], 0, 128, m, [[1, 1]])), deps=(mmk,))
                            evl.append(ev_g)
                        m_gate = ev_g


                if KSTAGE >= 18 and KSEL & 2:
                    m_b1 = T.ve(lambda e, n=TTi: e.tensor_mul(
                        fm3(g_f, 0, n), fm3(g_f, 0, n), fm3(d1_f, 0, n)),
                        deps=(m_r,))
                    m_bcast = T.se(lambda e, n=TTi: e.activation(
                        fm3(scr_b, 0, n, width=NW1), fm3(g_f, 0, n), AF.Identity),
                        deps=(m_b1, m_gate))
                    m_bs = None
                    if KSEL & 4:
                        for ch in range(FCH):
                            mm = T.te(lambda e, ch=ch, n=TTi: e.matmul(
                                ps[6][0:2, 0:n], sap(k_onespat, 0, 128, 0, [[1, 2]]),
                                fmS(scr_b, ch, 0, n, width=NW1), start=True, stop=True),
                                deps=(m_bcast,) if ch == 0 else ((m_bs,) if m_bs else ()))
                            m_bs = T.se(lambda e, ch=ch, n=TTi: e.copy(
                                sap(nrm_b, 0, 2, ch * TT, [[1, n]]), ps[6][0:2, 0:n]),
                                deps=(mm,))
                    m_bse = None
                    if KSEL & 8:
                        for ch in range(FCH):
                            dep = (m_bs,) if ch == 0 else ()
                            mmb = T.te(lambda e, ch=ch, n=TTi: e.matmul(
                                ps[7][:, 0:n], k_sel2[:],
                                sap(nrm_b, 0, 2, ch * TT, [[1, n]]),
                                start=True, stop=True),
                                deps=dep + ((m_bse,) if m_bse else ()))
                            m_bse = T.se(lambda e, ch=ch, n=TTi: e.copy(
                                fmS(scr_b, ch, 0, n, width=NW1), ps[7][:, 0:n]),
                                deps=(mmb,))
                    if m_bse is None:
                        m_bse = m_bcast

                    if KSTAGE >= 19:
                        m_lrp = lerp("x_v", extra=(m_b1,))

                        v_se = []

                        def evac_v(mch, bank, dep0):
                            dep = dep0 + ((m_bse, m_gate) if mch == 0 else ())
                            mvb = T.ve(lambda e, m=mch, b=bank, n=TTi: e.tensor_mul(
                                fmS(bonus_b, m, 0, n), b[:, 0:n],
                                fmS(scr_b, m, 0, n, width=NW1)), deps=dep)
                            v_se.append(T.se(lambda e, m=mch, b=bank, n=TTi: e.copy(
                                fmS(v_b, m, 0, n), b[:, 0:n]), deps=(mvb,)))
                            return mvb
                        m_vev = proj("W_v", m_lrp, evac_v, extra=(m_bse,))
                        m_v = m_vev[-1]
                        m_bonus = m_v
                        m_vars = (m_ab1, m_ab2, m_bh, m_kh, m_bkF, m_bk1,
                                  m_clc, m_gc, m_v, m_r, m_gate,
                                  r_se[-1], v_se[-1])

                if not (KSEL & 1):
                    m_gate = m_ln
                if not (KSEL & 2):
                    m_bse = m_b1 = m_ln
                prev_ev = (m_bonus,) if KSTAGE >= 19 else (m_ln,)
                prev_xb = prev_ev[0]
                if KSTAGE < 19:
                    m_vars = (m_ln,)
                for j in range(ncht if KSTAGE >= 20 else 0):
                    cglob = scan_t0 // L + j
                    band = 32 * (cglob % 4)
                    colc = j * L

                    def hsl(buf, hh, col, n, base=0):
                        return sap(buf, (hh % 2) * 64, 64,
                                   base + (hh // 2) * TT + col, [[1, n]])

                    ev = None
                    for t6, src in enumerate((v_b, bh_b, kh_b)):
                        m_s0 = None
                        for p in range(FCH):
                            bnk = ps[6 + (0 if p < 4 else 1)]
                            dep = ()
                            if m_s0 is None:
                                dep = (ev,) if ev else (m_vars + (prev_xb,))
                            m_s0 = T.te(lambda e, p=p, b=bnk, s=src, bd=band,
                                        cc=colc: e.matmul(
                                b[bd:bd + 32, (p % 4) * 128:(p % 4) * 128 + 128],
                                sap(s, 0, 128, p * TT + cc, [[1, L]]), k_id[:],
                                start=True, stop=True, tile_position=(0, bd)),
                                deps=dep)
                        T.se(lambda e, t6=t6, bd=band: e.copy(
                            sap(stm, bd, 32, t6 * 1024, [[1, 512]]),
                            ps[6][bd:bd + 32, :]), deps=(m_s0,))
                        ev = T.se(lambda e, t6=t6, bd=band: e.copy(
                            sap(stm, bd, 32, t6 * 1024 + 512, [[1, 512]]),
                            ps[7][bd:bd + 32, :]))
                    m_s1 = ev

                    if KCH >= 2:
                        m_s2 = None
                        for hh in range(16):
                            bnk = ps[0] if hh < 8 else ps[1]
                            dep = ((m_s1,) + prev_ev) if hh == 0 else ()
                            m_s2 = T.te(lambda e, hh=hh, b=bnk, bd=band, cc=colc: e.matmul(
                                b[bd:bd + 32, (hh % 8) * 64:(hh % 8) * 64 + 64],
                                hsl(bk_b, hh, cc, L),
                                sap(ar_b, (hh % 2) * 64, 64, (hh // 2) * TT + cc,
                                    [[FCH * TT, 2], [1, L]]),
                                start=True, stop=True,
                                tile_position=((hh % 2) * 64, bd)), deps=dep)
                        for hh in range(16):
                            bnk = ps[2] if hh < 8 else ps[3]
                            m_s2 = T.te(lambda e, hh=hh, b=bnk, bd=band, cc=colc: e.matmul(
                                b[bd:bd + 32, (hh % 8) * 64:(hh % 8) * 64 + 64],
                                hsl(bk_b, hh, cc, L, base=FCH * TT),
                                sap(ar_b, (hh % 2) * 64, 64, (hh // 2) * TT + cc,
                                    [[FCH * TT, 2], [1, L]]),
                                start=True, stop=True,
                                tile_position=((hh % 2) * 64, bd)))
                        for hh in range(16):
                            m_s2 = T.te(lambda e, hh=hh, bd=band, cc=colc: e.matmul(
                                ps[4][bd:bd + 32, hh * 32:hh * 32 + 32],
                                hsl(ar_b, hh, cc, L), hsl(bk_b, hh, cc, L),
                                start=True, stop=True,
                                tile_position=((hh % 2) * 64, bd)))
                    else:
                        m_s2 = m_s1

                    if KCH >= 3:
                        T.ve(lambda e, bd=band: e.tensor_mul(
                            sap(strb, bd, 32, 0, [[1, 512]]), ps[0][bd:bd + 32, :],
                            k_mtrb[bd:bd + 32, :]), deps=(m_s2,))
                        T.ve(lambda e, bd=band: e.tensor_mul(
                            sap(strb, bd, 32, 512, [[1, 512]]), ps[1][bd:bd + 32, :],
                            k_mtrb[bd:bd + 32, :]))
                        T.ve(lambda e, bd=band: e.tensor_mul(
                            sap(smrk, bd, 32, 0, [[1, 512]]), ps[2][bd:bd + 32, :],
                            k_mtrb[bd:bd + 32, :]))
                        T.ve(lambda e, bd=band: e.tensor_mul(
                            sap(smrk, bd, 32, 512, [[1, 512]]), ps[3][bd:bd + 32, :],
                            k_mtrb[bd:bd + 32, :]))
                        m_s3 = T.ve(lambda e, bd=band: e.tensor_mul(
                            sap(sw0, bd, 32, 0, [[1, 512]]), ps[4][bd:bd + 32, :],
                            k_mw0[bd:bd + 32, :]))

                        def ztile(k, hh, n=32):
                            if k == 0:
                                return sap(strb, band, 32, hh * 64, [[1, n]])
                            return sap(ssq, band, 32, (k - 1) * 512 + hh * 32, [[1, n]])

                        def wtile(k, hh, n=32):
                            if k == 0:
                                return sap(sw0, band, 32, hh * 32, [[1, n]])
                            return sap(ssq, band, 32, (3 + k) * 512 + hh * 32, [[1, n]])

                    else:
                        m_s3 = m_s1

                    if KCH >= 4:
                        m_prev = m_s3
                        for lv in range(4):
                            m_mm = None
                            for hh in range(16):
                                dep = (m_prev,) if hh == 0 else ()
                                m_mm = T.te(lambda e, hh=hh, bd=band, wt=wtile(lv, hh),
                                            zt=ztile(lv, hh): e.matmul(
                                    ps[5][bd:bd + 32, hh * 32:hh * 32 + 32],
                                    wt, zt,
                                    start=True, stop=True, tile_position=(bd, bd)),
                                    deps=dep)
                            if lv < 3:
                                for hh in range(16):
                                    m_mm = T.te(lambda e, hh=hh, bd=band, wt=wtile(lv, hh),
                                                zt=ztile(lv, hh): e.matmul(
                                        ps[6][bd:bd + 32, hh * 32:hh * 32 + 32],
                                        zt, wt,
                                        start=True, stop=True, tile_position=(bd, bd)))
                            e1 = T.se(lambda e, lv=lv, bd=band: e.copy(
                                sap(ssq, bd, 32, lv * 512, [[1, 512]]),
                                ps[5][bd:bd + 32, :]), deps=(m_mm,))
                            if lv < 3:
                                m_prev = T.se(lambda e, lv=lv, bd=band: e.copy(
                                    sap(ssq, bd, 32, (4 + lv) * 512, [[1, 512]]),
                                    ps[6][bd:bd + 32, :]))
                            else:
                                m_prev = e1
                        m_s4 = m_prev

                    else:
                        m_s4 = m_s3

                    if (KCH >= 10) and not is_warm:
                        m_oa = None
                        for hh in range(16):
                            bnk = ps[2] if hh < 8 else ps[3]
                            m_oa = T.te(lambda e, hh=hh, b=bnk, bd=band, cc=colc:
                                        e.matmul(
                                b[bd:bd + 32, (hh % 8) * 64:(hh % 8) * 64 + 64],
                                hsl(ar_b, hh, cc, L, base=FCH * TT),
                                sap(z_bf, (hh % 2) * 64, 64, (hh // 2) * 64, [[1, 64]]),
                                start=True, stop=True,
                                tile_position=((hh % 2) * 64, bd)),
                                deps=(m_s3, prev_zb) + prev_ev if hh == 0 else ())
                        tg0 = (j * L) // 128
                        T.se(lambda e, tg=tg0, bd=band: e.copy(
                            sap(o_tm, bd, 32, tg * C, [[1, 512]]),
                            ps[2][bd:bd + 32, :]), deps=(m_oa,))
                        m_oae = T.se(lambda e, tg=tg0, bd=band: e.copy(
                            sap(o_tm, bd, 32, tg * C + 512, [[1, 512]]),
                            ps[3][bd:bd + 32, :]))

                    if KCH >= 5:
                        m_g1 = None
                        for hh in range(16):
                            bnk = ps[6] if hh < 8 else ps[7]
                            dep = (m_s4, prev_zb) if hh == 0 else ()
                            m_g1 = T.te(lambda e, hh=hh, b=bnk, bd=band, cc=colc: e.matmul(
                                b[bd:bd + 32, (hh % 8) * 64:(hh % 8) * 64 + 64],
                                hsl(ar_b, hh, cc, L),
                                sap(z_bf, (hh % 2) * 64, 64, (hh // 2) * 64, [[1, 64]]),
                                start=True, stop=True,
                                tile_position=((hh % 2) * 64, bd)), deps=dep)
                        T.se(lambda e, bd=band: e.copy(
                            sap(sxp_f[0], bd, 32, 0, [[1, 512]]),
                            ps[6][bd:bd + 32, :]), deps=(m_g1,))
                        m_g1e = T.se(lambda e, bd=band: e.copy(
                            sap(sxp_f[0], bd, 32, 512, [[1, 512]]), ps[7][bd:bd + 32, :]))
                    else:
                        m_g1e = m_s4

                    if KCH >= 6:
                        m_mv = None
                        for hh in range(16):
                            bnk = ps[6] if hh < 8 else ps[7]
                            dep = (m_g1e, m_s3) if hh == 0 else ()
                            m_mv = T.te(lambda e, hh=hh, b=bnk, bd=band: e.matmul(
                                b[bd:bd + 32, (hh % 8) * 64:(hh % 8) * 64 + 64],
                                sap(smrk, bd, 32, hh * 64, [[1, 32]]),
                                sap(stm, bd, 32, (hh // 2) * 128 + (hh % 2) * 64, [[1, 64]]),
                                start=True, stop=True, tile_position=(bd, bd)), deps=dep)
                    else:
                        m_mv = m_g1e

                    if KCH >= 7:
                        T.ve(lambda e, bd=band: e.tensor_add(
                            sap(sxp_f[0], bd, 32, 0, [[1, 512]]),
                            sap(sxp_f[0], bd, 32, 0, [[1, 512]]),
                            ps[6][bd:bd + 32, :]), deps=(m_mv, m_g1e))
                        m_x = T.ve(lambda e, bd=band: e.tensor_add(
                            sap(sxp_f[0], bd, 32, 512, [[1, 512]]),
                            sap(sxp_f[0], bd, 32, 512, [[1, 512]]),
                            ps[7][bd:bd + 32, :]))
                        m_xb = T.se(lambda e, bd=band: e.copy(
                            sap(sxp_b, bd, 32, 0, [[1, 1024]]),
                            sap(sxp_f[0], bd, 32, 0, [[1, 1024]])), deps=(m_x,))
                    else:
                        m_xb = m_x = m_mv

                    if KCH >= 8:
                        for lv in range(5):
                            srcf = sxp_f[lv % 2]
                            dstf = sxp_f[(lv + 1) % 2]
                            m_mm = None
                            for hh in range(16):
                                bnk = ps[6] if hh < 8 else ps[7]
                                dep = ((m_xb, m_s4) if lv == 0 else (m_xb,)) if hh == 0 else ()
                                m_mm = T.te(lambda e, hh=hh, b=bnk, bd=band,
                                            zt=ztile(lv, hh): e.matmul(
                                    b[bd:bd + 32, (hh % 8) * 64:(hh % 8) * 64 + 64],
                                    zt, sap(sxp_b, bd, 32, hh * 64, [[1, 64]]),
                                    start=True, stop=True, tile_position=(bd, bd)),
                                    deps=dep)
                            T.ve(lambda e, d=dstf, s=srcf, bd=band: e.tensor_add(
                                sap(d, bd, 32, 0, [[1, 512]]), ps[6][bd:bd + 32, :],
                                sap(s, bd, 32, 0, [[1, 512]])), deps=(m_mm,))
                            m_x = T.ve(lambda e, d=dstf, s=srcf, bd=band: e.tensor_add(
                                sap(d, bd, 32, 512, [[1, 512]]), ps[7][bd:bd + 32, :],
                                sap(s, bd, 32, 512, [[1, 512]])))
                            m_xb = T.se(lambda e, d=dstf, bd=band: e.copy(
                                sap(sxp_b, bd, 32, 0, [[1, 1024]]),
                                sap(d, bd, 32, 0, [[1, 1024]])), deps=(m_x,))
                        sP = sxp_b
                        m_p = m_xb
                        prev_xb = m_xb

                    else:
                        sP = sxp_b
                        m_p = m_xb
                        prev_xb = m_xb

                    if KCH >= 9:
                        m_f = None
                        for hh in range(16):
                            dep = (m_p,) if hh == 0 else ()
                            T.te(lambda e, hh=hh, s=sP, bd=band: e.matmul(
                                ps[0][(hh % 2) * 64:(hh % 2) * 64 + 64,
                                      (hh // 2) * 64:(hh // 2) * 64 + 64],
                                sap(stm, bd, 32, 1024 + (hh // 2) * 128 + (hh % 2) * 64,
                                    [[1, 64]]),
                                sap(s, bd, 32, hh * 64, [[1, 64]]),
                                start=True, stop=False,
                                tile_position=(bd, (hh % 2) * 64)), deps=dep)
                            m_f = T.te(lambda e, hh=hh, bd=band: e.matmul(
                                ps[0][(hh % 2) * 64:(hh % 2) * 64 + 64,
                                      (hh // 2) * 64:(hh // 2) * 64 + 64],
                                sap(stm, bd, 32, 2048 + (hh // 2) * 128 + (hh % 2) * 64,
                                    [[1, 64]]),
                                sap(stm, bd, 32, (hh // 2) * 128 + (hh % 2) * 64, [[1, 64]]),
                                start=False, stop=True,
                                tile_position=(bd, (hh % 2) * 64)))

                    else:
                        m_f = m_p

                    if KCH >= 10:
                        if not is_warm:
                            m_ob = None
                            for hh in range(16):
                                bnk = ps[4] if hh < 8 else ps[5]
                                dep = (m_p,) if hh == 0 else ()
                                T.te(lambda e, hh=hh, b=bnk, s=sP, bd=band: e.matmul(
                                    b[bd:bd + 32, (hh % 8) * 64:(hh % 8) * 64 + 64],
                                    sap(strb, bd, 32, hh * 64 + 32, [[1, 32]]),
                                    sap(s, bd, 32, hh * 64, [[1, 64]]),
                                    start=True, stop=False, tile_position=(bd, bd)),
                                    deps=dep)
                                m_ob = T.te(lambda e, hh=hh, b=bnk, bd=band: e.matmul(
                                    b[bd:bd + 32, (hh % 8) * 64:(hh % 8) * 64 + 64],
                                    sap(smrk, bd, 32, hh * 64 + 32, [[1, 32]]),
                                    sap(stm, bd, 32, (hh // 2) * 128 + (hh % 2) * 64,
                                        [[1, 64]]),
                                    start=False, stop=True, tile_position=(bd, bd)))
                            tg = (j * L) // 128
                            T.ve(lambda e, tg=tg, bd=band: e.tensor_add(
                                sap(o_tm, bd, 32, tg * C, [[1, 512]]),
                                sap(o_tm, bd, 32, tg * C, [[1, 512]]),
                                ps[4][bd:bd + 32, :]),
                                deps=(m_oae, m_ob))
                            m_oe = T.ve(lambda e, tg=tg, bd=band: e.tensor_add(
                                sap(o_tm, bd, 32, tg * C + 512, [[1, 512]]),
                                sap(o_tm, bd, 32, tg * C + 512, [[1, 512]]),
                                ps[5][bd:bd + 32, :]))
                        else:
                            m_oe = m_s3

                    else:
                        m_oe = m_s3

                    if KCH >= 11:
                        mz = None
                        for p in range(FCH):
                            dep = (m_f, m_clc) if p == 0 else ()
                            mz = T.ve(lambda e, p=p, j=j: e.scalar_tensor_tensor(
                                sap(z_f32, 0, 128, p * 64, [[1, 64]]),
                                sap(z_f32, 0, 128, p * 64, [[1, 64]]),
                                sap(clc_f, 0, 128, p * 8 + j, [[1, 1]]),
                                ps[0][:, p * 64:p * 64 + 64], ALU.mult, ALU.add), deps=dep)
                        prev_zb = T.ve(lambda e: e.tensor_copy(z_bf[:], z_f32[:]), deps=(mz,))
                        prev_ev = (prev_zb, m_oe)
                    else:
                        mz = m_f
                        prev_zb = prev_zb

                    if (not is_warm) and (j % 4 == 3):
                        tg = (j * L) // 128
                        m1 = T.ve(lambda e, tg=tg: e.reduce_sum(
                            sap(gnst, 0, 128, 0, [[1, 16]]),
                            sap(o_tm, 0, 128, tg * C, [[64, 16], [1, 64]]),
                            axis=AX.X), deps=prev_ev)
                        T.ve(lambda e: e.tensor_scalar_mul(
                            sap(gnst, 0, 128, 0, [[1, 16]]),
                            sap(gnst, 0, 128, 0, [[1, 16]]), 1.0 / 64))
                        m_sqg = T.se(lambda e, tg=tg: e.activation(
                            sap(scr_b, 0, 128, 0, [[1, C]]),
                            sap(o_tm, 0, 128, tg * C, [[1, C]]), AF.Square),
                            deps=(m_oe,))
                        T.ve(lambda e: e.reduce_sum(
                            sap(gnst, 0, 128, 16, [[1, 16]]),
                            sap(scr_b, 0, 128, 0, [[64, 16], [1, 64]]), axis=AX.X),
                            deps=(m_sqg,))
                        T.ve(lambda e: e.tensor_mul(sap(gnst, 0, 128, 32, [[1, 16]]),
                                                    sap(gnst, 0, 128, 0, [[1, 16]]),
                                                    sap(gnst, 0, 128, 0, [[1, 16]])))
                        m_vr = T.ve(lambda e: e.scalar_tensor_tensor(
                            sap(gnst, 0, 128, 48, [[1, 16]]),
                            sap(gnst, 0, 128, 16, [[1, 16]]), 1.0 / 64,
                            sap(gnst, 0, 128, 32, [[1, 16]]), ALU.mult, ALU.subtract))
                        m_sd2 = T.se(lambda e: e.activation(
                            sap(gnst, 0, 128, 64, [[1, 16]]),
                            sap(gnst, 0, 128, 48, [[1, 16]]), AF.Sqrt, bias=EPS),
                            deps=(m_vr,))
                        m_rs2 = T.ve(lambda e: e.reciprocal(
                            sap(gnst, 0, 128, 80, [[1, 16]]),
                            sap(gnst, 0, 128, 64, [[1, 16]])), deps=(m_sd2,))
                        T.ve(lambda e, tg=tg: e.tensor_sub(
                            sap(o_tm, 0, 128, tg * C, [[64, 16], [1, 64]]),
                            sap(o_tm, 0, 128, tg * C, [[64, 16], [1, 64]]),
                            sap(gnst, 0, 128, 0, [[1, 16], [0, 64]])), deps=(m_rs2,))
                        T.ve(lambda e, tg=tg: e.tensor_mul(
                            sap(o_tm, 0, 128, tg * C, [[64, 16], [1, 64]]),
                            sap(o_tm, 0, 128, tg * C, [[64, 16], [1, 64]]),
                            sap(gnst, 0, 128, 80, [[1, 16], [0, 64]])))
                        T.ve(lambda e, tg=tg: e.tensor_mul(
                            sap(o_tm, 0, 128, tg * C, [[1, C]]),
                            sap(o_tm, 0, 128, tg * C, [[1, C]]), gw_tm[:]),
                            deps=(m_gnbc,))
                        m_gn = T.ve(lambda e, tg=tg: e.scalar_tensor_tensor(
                            sap(scr_b, 0, 128, 0, [[1, C]]),
                            sap(o_tm, 0, 128, tg * C, [[1, C]]), 1.0, gb_tm[:],
                            ALU.mult, ALU.add))
                        m_tp = None
                        for ch in range(FCH):
                            dept = (m_gn,)
                            if ch == 0 and prev_gn_add is not None:
                                dept = dept + (prev_gn_add,)
                            m_tp = T.dma_sc(lambda e, ch=ch, tg=tg: e.dma_start(
                                sap(scr_b, 0, 128, 1024 + ch * 128, [[1, 128]]),
                                sap(scr_b, 0, 128, ch * 128, [[1, 128]]),
                                transpose=True), deps=dept)
                        real0 = scan_t0 - WARM + tg * 128
                        T.ve(lambda e, tg=tg, r0=real0: e.tensor_add(
                            sap(wo_in, 0, 128, r0, [[SEG, FCH], [1, 128]]),
                            sap(scr_b, 0, 128, 1024, [[128, FCH], [1, 128]]),
                            sap(bonus_b, 0, 128, tg * 128, [[TT, FCH], [1, 128]])),
                            deps=(m_tp, m_bonus))
                        prev_gn_add = T.ve(lambda e, tg=tg, r0=real0: e.tensor_mul(
                            sap(wo_in, 0, 128, r0, [[SEG, FCH], [1, 128]]),
                            sap(wo_in, 0, 128, r0, [[SEG, FCH], [1, 128]]),
                            sap(gate_b, 0, 128, tg * 128, [[TT, FCH], [1, 128]])),
                            deps=(m_gate,))

                if DBG and ti == 1:
                    T.barrier()
                    for nm_, buf_ in (("ln", ln_f), ("r_b", r_b),
                                      ("v_b", v_b), ("ah_f", ah_f), ("sp_f", sp_f),
                                      ("kk_f", kk_f),
                                      ("gate_b", gate_b), ("bonus_b", bonus_b),
                                      ("d1_f", d1_f),
                                      ("bk_b", bk_b), ("ar_b", ar_b), ("bh_b", bh_b),
                                      ("kh_b", kh_b),
                                      ("o_tm", o_tm), ("strb", strb),
                                      ("stm", stm), ("sw0", sw0)):
                        t_ = dbg_t("t1_" + nm_, tuple(buf_.shape),
                                   F32 if buf_.dtype == F32 else F32)
                        if buf_.dtype == F32:
                            T.dma_sy(lambda e, t_=t_, b_=buf_: e.dma_start(t_.ap(), b_[:]),
                                     deps=(prev_zb, prev_ev[0]))
                        else:
                            T.dma_ge(lambda e, t_=t_, b_=buf_: e.dma_start(t_.ap(), b_[:]),
                                     deps=(prev_zb, prev_ev[0]))
                T.barrier()
                scan_t0 += TTi
            if DBG:
                t = dbg_t("wo_in", (128, FCH * SEG))
                T.dma_ge(lambda e: e.dma_start(t.ap(), wo_in[:]), deps=(prev_zb,))
                t2 = dbg_t("z", (128, FCH * 64))
                T.dma_sy(lambda e: e.dma_start(t2.ap(), z_f32[:]), deps=(prev_zb,))
            T.barrier()

        if KSKIP2:
            dmy = []
            for m in range(FCH):
                dmy.append(T.dma_out(lambda e, m=m: e.dma_start(
                    AP(out_fm, m * 128 * SEG, [[SEG, 128], [1, SEG]]),
                    sap(xf, 0, 128, 0, [[1, SEG]]))))
            T._rec("ge", lambda e: e.nop(), None, 0,
                   (("outQ", T.semn["outQ"]),))
            with nc.allow_low_precision(reason="debug"):
                with nc.Block() as block:
                    T.finalize(block)
            return nc
        p2 = contextlib.ExitStack()
        with p2:
            def sb2(name, shape, dt):
                return p2.enter_context(nc.sbuf_tensor(name, list(shape), dt))

            w_big = sb2("w_big", (128, FCH * C), BF16)
            xf_t = sb2("xf_t", (128, FCH * 512), F32)
            w_2 = sb2("w_2", (128, 4 * SLOTS), BF16)
            w_gs = sb2("w_gs", (128, FCH * 512), BF16)
            s_slots = sb2("s_slots", (SLOTS, C), F32)
            s_slotsb = sb2("s_slotsb", (SLOTS, C), BF16)
            s_alpha = sb2("s_alpha", (128, 16 * SLOTS), BF16)
            s_alpsm = sb2("s_alpsm", (SLOTS, SEG), BF16)
            s_si = sb2("s_si", (SLOTS, C + 16), F32)
            s_lgt = sb2("s_lgt", (128, SLOTS), F32)
            s_m8 = sb2("s_m8", (128, 12), F32)
            s_rep = sb2("s_rep", (128, SLOTS), F32)
            s_msk = sb2("s_msk", (128, SLOTS), F32)
            s_el = sb2("s_el", (128, SLOTS), F32)
            s_meanb = sb2("s_meanb", (1, C), BF16)
            s_fm64 = sb2("s_fm64", (128, FCH * 64), BF16)
            s_fm64b = sb2("s_fm64b", (128, FCH * 64), BF16)
            s_tau = sb2("s_tau", (1, 4), F32)
            s_taub = sb2("s_taub", (128, 1), F32)
            s_b2bc = sb2("s_b2bc", (128, SLOTS), F32)
            s_ofin = sb2("s_ofin", (128, 2 * 512), F32)
            biash = sb2("biash", (128, 4), F32)
            y_tm = wo_in

            p2t = contextlib.ExitStack()
            with p2t:
                def sb2t(name, shape, dt):
                    return p2t.enter_context(nc.sbuf_tensor(name, list(shape), dt))

                y_fm = sb2t("y_fm", (128, FCH * SEG), BF16)
                x2t_f = sb2t("x2t_f", (128, FCH * 512), F32)
                w_1a = sb2t("w_1a", (128, FCH * HFF), BF16)
                w_1b = sb2t("w_1b", (128, FCH * HFF), BF16)
                s_hfm = sb2t("s_hfm", (128, 4 * SEG), BF16)
                row2_f = sb2t("row2_f", (1, 4 * 512), F32)
                row2_c0 = sb2t("row2_c0", (1, 512), BF16)
                row2_c1 = sb2t("row2_c1", (1, 512), BF16)
                cb2 = sb2t("cb2", (128, 2 * 512), BF16)

                wo_ld = [None, None]
                wo_cs = [None, None]
                for kch in range(FCH):
                    h = kch % 2
                    wo_ld[h] = T.dma_sy(lambda e, k=kch, h=h: e.dma_start(
                        sap(xf_t, 0, 128, h * 1024, [[1, C]]),
                        dap("W_o", k * 128 * C, [[C, 128], [1, C]])),
                        deps=(wo_cs[h],) if wo_cs[h] else (), sem=("wA", "wB")[h])
                    wo_cs[h] = T.se(lambda e, k=kch, h=h: e.copy(
                        sap(w_big, 0, 128, k * C, [[1, C]]),
                        sap(xf_t, 0, 128, h * 1024, [[1, C]])),
                        deps=(wo_ld[h],))
                d_WO = wo_cs[1]
                d_WO2 = wo_cs[0]

                m_yg = []
                prev_stat2 = None
                prev_x2h = None
                for grp in range(4):
                    m_xf = T.dma_sy(lambda e, g=grp: e.dma_start(
                        sap(xf_t, 0, 128, 0, [[512, FCH], [1, 512]]),
                        dap("xT", WARM + 1 + g * 512,
                            [[TS + 1, 128], [128 * (TS + 1), FCH], [1, 512]])),
                        deps=tuple(m_yg[-1:]) + ((d_WO, d_WO2) if grp == 0 else ()) +
                        ((prev_stat2,) if prev_stat2 else ()))
                    mmk = None
                    for kch in range(FCH):
                        for mch in range(FCH):
                            bank = ps[mch]
                            dep = (d_WO, d_WO2) if (kch == 0 and mch == 0) else ()
                            if kch == 0 and mch == 0 and grp > 0:
                                dep = dep + tuple(m_yg[-1:])
                            mmk = T.te(lambda e, m=mch, k=kch, b=bank, g=grp: e.matmul(
                                b[:, :], sap(w_big, 0, 128, k * C + m * 128, [[1, 128]]),
                                sap(wo_in, 0, 128, k * SEG + g * 512, [[1, 512]]),
                                start=(k == 0), stop=(k == FCH - 1)), deps=dep)
                    m_x2 = None
                    for mch in range(FCH):
                        m_x2 = T.ve(lambda e, m=mch, g=grp: e.tensor_add(
                            sap(x2t_f, 0, 128, m * 512, [[1, 512]]),
                            ps[m][:, :],
                            sap(xf_t, 0, 128, m * 512, [[1, 512]])),
                            deps=(mmk, m_xf) + ((prev_x2h,) if prev_x2h else ()))
                    m_x2h = T.dma_sy(lambda e, g=grp: e.dma_start(
                        AP(x2_hbm, g * 512, [[SEG, 128], [128 * SEG, FCH], [1, 512]]),
                        sap(x2t_f, 0, 128, 0, [[512, FCH], [1, 512]])), deps=(m_x2,),
                        sem="x2s")
                    prev_x2h = m_x2h
                    m_yb = T.ve(lambda e, g=grp: e.tensor_copy(
                        sap(y_fm, 0, 128, g * 512, [[SEG, FCH], [1, 512]]),
                        sap(x2t_f, 0, 128, 0, [[512, FCH], [1, 512]])), deps=(m_x2,))
                    for ch in range(FCH):
                        mm = T.te(lambda e, ch=ch, g=grp: e.matmul(
                            ps[4][0:1, :], k_onescol[:],
                            sap(y_fm, 0, 128, ch * SEG + g * 512, [[1, 512]]),
                            start=(ch == 0), stop=(ch == FCH - 1)),
                            deps=(m_yb,) if ch == 0 else ())
                    m_sq = T.se(lambda e, g=grp: e.activation(
                        sap(xf_t, 0, 128, 0, [[512, FCH], [1, 512]]),
                        sap(y_fm, 0, 128, g * 512, [[SEG, FCH], [1, 512]]), AF.Square),
                        deps=(m_yb, m_x2h))
                    for ch in range(FCH):
                        mm = T.te(lambda e, ch=ch: e.matmul(
                            ps[5][0:1, :], k_onescolf[:],
                            sap(xf_t, 0, 128, ch * 512, [[1, 512]]),
                            start=(ch == 0), stop=(ch == FCH - 1)),
                            deps=(m_sq,) if ch == 0 else ())
                    m_stat = mm
                    prev_stat2 = m_stat
                    T.ve(lambda e: e.tensor_scalar_mul(AP(row2_f, 0 * 512, [[4 * 512, 1], [1, 512]]), ps[4][0:1, :],
                                                       1.0 / C), deps=(m_stat,))
                    T.ve(lambda e: e.tensor_mul(AP(row2_f, 1 * 512, [[4 * 512, 1], [1, 512]]), AP(row2_f, 0 * 512, [[4 * 512, 1], [1, 512]]),
                                                AP(row2_f, 0 * 512, [[4 * 512, 1], [1, 512]])))
                    m_vr = T.ve(lambda e: e.scalar_tensor_tensor(
                        AP(row2_f, 2 * 512, [[4 * 512, 1], [1, 512]]), ps[5][0:1, :], 1.0 / C, AP(row2_f, 1 * 512, [[4 * 512, 1], [1, 512]]),
                        ALU.mult, ALU.subtract))
                    m_sd = T.se(lambda e: e.activation(AP(row2_f, 3 * 512, [[4 * 512, 1], [1, 512]]), AP(row2_f, 2 * 512, [[4 * 512, 1], [1, 512]]),
                                                       AF.Sqrt, bias=EPS), deps=(m_vr,))
                    T.ve(lambda e: e.reciprocal(AP(row2_f, 2 * 512, [[4 * 512, 1], [1, 512]]), AP(row2_f, 3 * 512, [[4 * 512, 1], [1, 512]])),
                         deps=(m_sd,))
                    T.ve(lambda e: e.tensor_copy(row2_c1[0:1, :], AP(row2_f, 2 * 512, [[4 * 512, 1], [1, 512]])))
                    m_rw = T.ve(lambda e: e.scalar_tensor_tensor(
                        row2_c0[0:1, :], AP(row2_f, 0 * 512, [[4 * 512, 1], [1, 512]]), -1.0, AP(row2_f, 2 * 512, [[4 * 512, 1], [1, 512]]),
                        ALU.mult, ALU.mult))
                    T.te(lambda e: e.matmul(ps[6][:, :], k_onesrow[0:1, :],
                                            row2_c0[0:1, :], start=True, stop=True),
                         deps=(m_rw,))
                    m_bc = T.te(lambda e: e.matmul(ps[7][:, :], k_onesrow[0:1, :],
                                                   row2_c1[0:1, :], start=True, stop=True))
                    T.se(lambda e: e.copy(sap(cb2, 0, 128, 0, [[1, 512]]), ps[6][:, :]),
                         deps=(m_bc,))
                    m_cb = T.se(lambda e: e.copy(sap(cb2, 0, 128, 512, [[1, 512]]),
                                                 ps[7][:, :]))
                    T.ve(lambda e, g=grp: e.tensor_mul(
                        sap(y_fm, 0, 128, g * 512, [[SEG, FCH], [1, 512]]),
                        sap(y_fm, 0, 128, g * 512, [[SEG, FCH], [1, 512]]),
                        sap(cb2, 0, 128, 512, [[0, FCH], [1, 512]])), deps=(m_cb,))
                    m_y = T.ve(lambda e, g=grp: e.tensor_add(
                        sap(y_fm, 0, 128, g * 512, [[SEG, FCH], [1, 512]]),
                        sap(y_fm, 0, 128, g * 512, [[SEG, FCH], [1, 512]]),
                        sap(cb2, 0, 128, 0, [[0, FCH], [1, 512]])))
                    m_yg.append(m_y)
                m_yall = m_yg[-1]
                T.barrier()

                m_yt = None
                for tg in range(16):
                    for ch in range(FCH):
                        m_yt = T.dma_sc(lambda e, tg=tg, ch=ch: e.dma_start(
                            sap(y_tm, 0, 128, tg * C + ch * 128, [[1, 128]]),
                            sap(y_fm, 0, 128, ch * SEG + tg * 128, [[1, 128]]),
                            transpose=True), deps=(m_yall,))

                m_w1x = None
                for wnm, wdst in (("w1aT", w_1a), ("w1bT", w_1b)):
                    dep = (prev_x2h, prev_stat2, m_yall) if m_w1x is None \
                        else (m_w1x,)
                    d_l = T.dma_sy(lambda e, wnm=wnm: e.dma_start(
                        sap(xf_t, 0, 128, 0, [[HFF, FCH], [1, HFF]]),
                        dap(wnm, 0, [[HFF, 128], [128 * HFF, FCH], [1, HFF]])),
                        deps=dep, sem="wA")
                    m_w1x = T.se(lambda e, wdst=wdst: e.copy(
                        sap(wdst, 0, 128, 0, [[1, FCH * HFF]]),
                        sap(xf_t, 0, 128, 0, [[1, FCH * HFF]])),
                        deps=(d_l,))
                d_w1a = d_w1b = m_w1x
                d_w2 = T.dma_ge(lambda e: e.dma_start(
                    sap(w_2, 0, 128, 0, [[SLOTS, 4], [1, SLOTS]]),
                    dap("w2T", 0, [[SLOTS, 128], [128 * SLOTS, 4], [1, SLOTS]])))
                T.dma_sy(lambda e: e.dma_start(s_slots[:], din["slot_init"].ap()))
                T.dma_sy(lambda e: e.dma_start(
                    sap(s_si, 0, SLOTS, 0, [[1, 1]]), din["v_slot_scale"].ap()))
                T.dma_sy(lambda e: e.dma_start(
                    AP(s_tau, 0, [[4, 1], [1, 1]]), din["tau"].ap()))
                T.dma_sy(lambda e: e.dma_start(biash[:, 0:4], din["v_b1"].ap()))
                T.dma_sy(lambda e: e.dma_start(s_b2bc[:], din["b2_rep"].ap()))
                d_slot = d_scale = d_tau = d_b1 = d_b2r = ("syQ", T.semn["syQ"])

                m_S = T.ve(lambda e: e.tensor_scalar_mul(
                    s_slots[:], s_slots[:], sap(s_si, 0, SLOTS, 0, [[1, 1]])),
                    deps=(d_slot, d_scale))
                m_Sb = T.ve(lambda e: e.tensor_copy(s_slotsb[:], s_slots[:]))
                T.te(lambda e: e.matmul(ps[4][0:1, :], k_onescol[0:64, :],
                                        s_slotsb[:, 0:512], start=True, stop=True),
                     deps=(m_Sb,))
                m_sm = T.te(lambda e: e.matmul(ps[5][0:1, :], k_onescol[0:64, :],
                                               s_slotsb[:, 512:1024],
                                               start=True, stop=True))
                T.ve(lambda e: e.tensor_scalar_mul(
                    AP(s_meanb, 0, [[C, 1], [1, 512]]), ps[4][0:1, :], 1.0 / 64),
                    deps=(m_sm,))
                m_smb = T.ve(lambda e: e.tensor_scalar_mul(
                    AP(s_meanb, 512, [[C, 1], [1, 512]]), ps[5][0:1, :], 1.0 / 64))
                for ch in range(FCH):
                    mm = T.te(lambda e, ch=ch: e.matmul(
                        ps[6][:, ch:ch + 1],
                        AP(s_meanb, ch * 128, [[C, 1], [1, 128]]),
                        k_onesrow[0:1, 0:1], start=True, stop=True),
                        deps=(m_smb,) if ch == 0 else ())
                m_mfm = T.ve(lambda e: e.tensor_copy(sap(s_fm64, 0, 128, 0, [[1, FCH]]),
                                                     ps[6][:, 0:FCH]), deps=(mm,))
                mmk = None
                for mh in range(4):
                    for k in range(FCH):
                        dep = (m_mfm, d_w1b) if (k == 0 and mh == 0) else ()
                        mmk = T.te(lambda e, mh=mh, k=k: e.matmul(
                            ps[7][:, mh:mh + 1],
                            sap(w_1b, 0, 128, k * HFF + mh * 128, [[1, 128]]),
                            sap(s_fm64, 0, 128, k, [[1, 1]]),
                            start=(k == 0), stop=(k == FCH - 1)), deps=dep)
                m_bh = T.ve(lambda e: e.tensor_add(biash[:, 0:4], biash[:, 0:4],
                                                   ps[7][:, 0:4]), deps=(mmk, d_b1))

                m_h = None
                for grp in range(4):
                    for mh in range(4):
                        bank = ps[mh % 2]
                        mmk = None
                        for k in range(FCH):
                            dep = ()
                            if k == 0:
                                dep = (m_yall, d_w1a, m_bh)
                                if m_h is not None:
                                    dep = dep + (m_h,)
                            mmk = T.te(lambda e, g=grp, mh=mh, k=k, b=bank: e.matmul(
                                b[:, :],
                                sap(w_1a, 0, 128, k * HFF + mh * 128, [[1, 128]]),
                                sap(y_fm, 0, 128, k * SEG + g * 512, [[1, 512]]),
                                start=(k == 0), stop=(k == FCH - 1)), deps=dep)
                        m_h = T.se(lambda e, g=grp, mh=mh, b=bank: e.activation(
                            sap(s_hfm, 0, 128, mh * SEG + g * 512, [[1, 512]]),
                            b[:, :], AF.Gelu, bias=sap(biash, 0, 128, mh, [[1, 1]])),
                            deps=(mmk,))

                m_t1 = T.se(lambda e: e.activation(AP(s_tau, 1, [[4, 1], [1, 1]]),
                                                   AP(s_tau, 0, [[4, 1], [1, 1]]),
                                                   AF.Abs), deps=(d_tau,))
                T.ve(lambda e: e.tensor_scalar_add(AP(s_tau, 1, [[4, 1], [1, 1]]),
                                                   AP(s_tau, 1, [[4, 1], [1, 1]]),
                                                   0.1), deps=(m_t1,))
                m_t2 = T.ve(lambda e: e.reciprocal(AP(s_tau, 2, [[4, 1], [1, 1]]),
                                                   AP(s_tau, 1, [[4, 1], [1, 1]])))
                m_tmm = T.te(lambda e: e.matmul(ps[6][:, 0:1], k_onesrowf[0:1, :],
                                                AP(s_tau, 2, [[4, 1], [1, 1]]),
                                                start=True, stop=True), deps=(m_t2,))
                m_tb = T.ve(lambda e: e.tensor_copy(s_taub[:], ps[6][:, 0:1]),
                            deps=(m_tmm,))
                m_b2bc = T.ve(lambda e: e.tensor_scalar_mul(
                    s_b2bc[:], s_b2bc[:], sap(s_taub, 0, 128, 0, [[1, 1]])),
                    deps=(m_tb, d_b2r))

                m_sia = None
                prev_lg = None
                prev_ae = None
                m_aes = None
                for tg in range(16):
                    mmk = None
                    for k in range(4):
                        dep = (m_h, d_w2) if (k == 0 and tg == 0) else ()
                        if k == 0 and prev_lg is not None:
                            dep = dep + (prev_lg,)
                        mmk = T.te(lambda e, tg=tg, k=k: e.matmul(
                            ps[2][:, 0:SLOTS],
                            sap(s_hfm, 0, 128, k * SEG + tg * 128, [[1, 128]]),
                            sap(w_2, 0, 128, k * SLOTS, [[1, SLOTS]]),
                            start=(k == 0), stop=(k == 3)), deps=dep)
                    m_lg = T.ve(lambda e: e.scalar_tensor_tensor(
                        s_lgt[:], ps[2][:, 0:SLOTS], sap(s_taub, 0, 128, 0, [[1, 1]]),
                        s_b2bc[:], ALU.mult, ALU.add), deps=(mmk, m_tb, m_b2bc))
                    prev_lg = m_lg
                    m_m8 = T.ve(lambda e: e.max(out=s_m8[:, 0:8], in_=s_lgt[:]),
                                deps=(m_lg,))
                    T.ve(lambda e: e.tensor_scalar_mul(s_m8[:, 8:9], s_m8[:, 0:1],
                                                       -1.0))
                    m_rep = T.ve(lambda e: e.match_replace(out=s_rep[:],
                                                           in_to_replace=s_m8[:, 0:8],
                                                           in_values=s_lgt[:],
                                                           imm_value=-1e30))
                    m_e8 = T.se(lambda e: e.activation(s_m8[:, 0:8], s_m8[:, 0:8],
                                                       AF.Exp, bias=s_m8[:, 8:9]),
                                deps=(m_rep,))
                    m_den = T.ve(lambda e: e.reduce_sum(s_m8[:, 9:10], s_m8[:, 0:8],
                                                        axis=AX.X), deps=(m_e8,))
                    m_rd = T.ve(lambda e: e.reciprocal(s_m8[:, 10:11], s_m8[:, 9:10]))
                    m_el = T.se(lambda e: e.activation(s_el[:], s_lgt[:], AF.Exp,
                                                       bias=s_m8[:, 8:9]), deps=(m_rd,))
                    m_msk = T.ve(lambda e: e.tensor_tensor(s_msk[:], s_lgt[:], s_rep[:],
                                                           ALU.not_equal), deps=(m_el,))
                    T.ve(lambda e: e.tensor_mul(s_el[:], s_el[:], s_msk[:]))
                    m_al = T.ve(lambda e, tg=tg: e.tensor_scalar_mul(
                        sap(s_alpha, 0, 128, tg * SLOTS, [[1, SLOTS]]), s_el[:],
                        sap(s_m8, 0, 128, 10, [[1, 1]])))
                    m_at = T.te(lambda e, tg=tg: e.matmul(
                        ps[3][0:64, 0:128],
                        sap(s_alpha, 0, 128, tg * SLOTS, [[1, SLOTS]]), k_id[:],
                        start=True, stop=True),
                        deps=(m_al,) + ((prev_ae,) if prev_ae else ()))
                    m_ae = T.se(lambda e, tg=tg: e.copy(
                        sap(s_alpsm, 0, SLOTS, tg * 128, [[1, 128]]),
                        ps[3][0:64, 0:128]), deps=(m_at,))
                    prev_ae = m_ae
                    m_aes = m_ae
                    dep0 = (m_al, m_yt) if tg == 0 else (m_al,)
                    T.te(lambda e, tg=tg: e.matmul(
                        ps[0][0:64, :],
                        sap(s_alpha, 0, 128, tg * SLOTS, [[1, SLOTS]]),
                        sap(y_tm, 0, 128, tg * C, [[1, 512]]),
                        start=(tg == 0), stop=(tg == 15)), deps=dep0)
                    T.te(lambda e, tg=tg: e.matmul(
                        ps[1][0:64, :],
                        sap(s_alpha, 0, 128, tg * SLOTS, [[1, SLOTS]]),
                        sap(y_tm, 0, 128, tg * C + 512, [[1, 512]]),
                        start=(tg == 0), stop=(tg == 15)))
                    m_sia = T.te(lambda e, tg=tg: e.matmul(
                        ps[4][0:64, 0:1],
                        sap(s_alpha, 0, 128, tg * SLOTS, [[1, SLOTS]]),
                        k_onescol[:], start=(tg == 0), stop=(tg == 15)))
                T.se(lambda e: e.copy(sap(s_si, 0, SLOTS, 0, [[1, 512]]),
                                      ps[0][0:64, :]), deps=(m_sia,))
                T.se(lambda e: e.copy(sap(s_si, 0, SLOTS, 512, [[1, 512]]),
                                      ps[1][0:64, :]))
                m_sie = T.se(lambda e: e.copy(sap(s_si, 0, SLOTS, C, [[1, 1]]),
                                              ps[4][0:64, 0:1]))
                T.barrier()
                m_x2all = ("x2s", T.semn["x2s"])

            m_z16 = T.ve(lambda e: e.memset(s_si[:, C + 1:C + 16], 0.0))
            m_ccw = T.dma_ge(lambda e: e.dma_start(cc_in, s_si[:, 0:C + 16]),
                             deps=(m_z16,))
            m_cc = T._rec("ge", lambda e: e.collective_compute(
                "AllReduce", ALU.add, ins=[cc_in], outs=[cc_out],
                replica_groups=[list(range(NCORES))]), "cc", 1, (m_ccw,))
            m_ccr = T.dma_ge(lambda e: e.dma_start(s_si[:, 0:C + 16], cc_out),
                             deps=(m_cc,))

            s_gx = sb2("s_gx", (SLOTS, GDIM), F32)
            s_gh = sb2("s_gh", (SLOTS, GDIM), F32)
            s_gate2 = sb2("s_gate2", (SLOTS, 2 * C), F32)
            s_gtmp = sb2("s_gtmp", (SLOTS, C), F32)
            s_snew = sb2("s_snew", (SLOTS, C), F32)
            s_snb = sb2("s_snb", (SLOTS, C), BF16)
            s_gib = sb2("s_gib", (SLOTS, C), BF16)
            s_sv = sb2("s_sv", (SLOTS, C), F32)
            s_svb = sb2("s_svb", (SLOTS, C), BF16)
            s_svw = sb2("s_svw", (SLOTS, C), BF16)
            s_bc = sb2("s_bc", (SLOTS, GDIM), F32)

            T.ve(lambda e: e.tensor_scalar_add(sap(s_si, 0, SLOTS, C + 1, [[1, 1]]),
                                               sap(s_si, 0, SLOTS, C, [[1, 1]]),
                                               1e-8), deps=(m_ccr,))
            m_rw2 = T.ve(lambda e: e.reciprocal(sap(s_si, 0, SLOTS, C + 2, [[1, 1]]),
                                                sap(s_si, 0, SLOTS, C + 1, [[1, 1]])))
            m_gib = T.ve(lambda e: e.tensor_scalar_mul(
                s_gib[:], sap(s_si, 0, SLOTS, 0, [[1, C]]),
                sap(s_si, 0, SLOTS, C + 2, [[1, 1]])), deps=(m_rw2,))

            prev_gwc = None
            m_gfm = None
            for ch in range(FCH):
                mm = T.te(lambda e, ch=ch: e.matmul(
                    ps[4][:, 0:64], s_gib[:, ch * 128:(ch + 1) * 128],
                    k_id[0:64, 0:64], start=True, stop=True),
                    deps=(m_gib,) if ch == 0 else (m_gfm,))
                m_gfm = T.se(lambda e, ch=ch: e.copy(
                    sap(s_fm64, 0, 128, ch * 64, [[1, 64]]), ps[4][:, 0:64]),
                    deps=(mm,))
            m_hfm = None
            for ch in range(FCH):
                mm = T.te(lambda e, ch=ch: e.matmul(
                    ps[5][:, 0:64], s_slotsb[:, ch * 128:(ch + 1) * 128],
                    k_id[0:64, 0:64], start=True, stop=True),
                    deps=(m_gib,) if ch == 0 else (m_hfm,))
                m_hfm = T.se(lambda e, ch=ch: e.copy(
                    sap(s_fm64b, 0, 128, ch * 64, [[1, 64]]), ps[5][:, 0:64]),
                    deps=(mm,))

            d_bih = T.dma_sy(lambda e: e.dma_start(s_bc[:], din["bih_rep"].ap()))
            m_gx = None
            for n6 in range(6):
                d_wl = T.dma_sy(lambda e, n6=n6: e.dma_start(
                    sap(xf_t, 0, 128, 0, [[512, FCH], [1, 512]]),
                    dap("gru_wihT", n6 * 512,
                        [[GDIM, 128], [128 * GDIM, FCH], [1, 512]])),
                    deps=((m_gx,) if m_gx else ()) + ((prev_gwc,) if prev_gwc else ()),
                    sem="wA")
                d_ws = prev_gwc = T.se(lambda e: e.copy(
                    sap(w_gs, 0, 128, 0, [[1, FCH * 512]]),
                    sap(xf_t, 0, 128, 0, [[1, FCH * 512]])), deps=(d_wl,))
                mmk = None
                for k in range(FCH):
                    dep = (d_ws, m_gfm) if k == 0 else ()
                    mmk = T.te(lambda e, k=k: e.matmul(
                        ps[0][0:64, :], sap(s_fm64, 0, 128, k * 64, [[1, 64]]),
                        sap(w_gs, 0, 128, k * 512, [[1, 512]]),
                        start=(k == 0), stop=(k == FCH - 1)), deps=dep)
                m_gx = T.ve(lambda e, n6=n6: e.tensor_add(
                    sap(s_gx, 0, SLOTS, n6 * 512, [[1, 512]]), ps[0][0:64, :],
                    sap(s_bc, 0, SLOTS, n6 * 512, [[1, 512]])), deps=(mmk, d_bih))
            d_bhh = T.dma_sy(lambda e: e.dma_start(s_bc[:], din["bhh_rep"].ap()),
                             deps=(m_gx,))
            m_gh = None
            for n6 in range(6):
                d_wl = T.dma_sy(lambda e, n6=n6: e.dma_start(
                    sap(xf_t, 0, 128, 0, [[512, FCH], [1, 512]]),
                    dap("gru_whhT", n6 * 512,
                        [[GDIM, 128], [128 * GDIM, FCH], [1, 512]])),
                    deps=((m_gh,) if m_gh else (m_gx,)) + ((prev_gwc,) if prev_gwc else ()),
                    sem="wA")
                d_ws = prev_gwc = T.se(lambda e: e.copy(
                    sap(w_gs, 0, 128, 0, [[1, FCH * 512]]),
                    sap(xf_t, 0, 128, 0, [[1, FCH * 512]])), deps=(d_wl,))
                mmk = None
                for k in range(FCH):
                    dep = (d_ws, m_hfm) if k == 0 else ()
                    mmk = T.te(lambda e, k=k: e.matmul(
                        ps[0][0:64, :], sap(s_fm64b, 0, 128, k * 64, [[1, 64]]),
                        sap(w_gs, 0, 128, k * 512, [[1, 512]]),
                        start=(k == 0), stop=(k == FCH - 1)), deps=dep)
                m_gh = T.ve(lambda e, n6=n6: e.tensor_add(
                    sap(s_gh, 0, SLOTS, n6 * 512, [[1, 512]]), ps[0][0:64, :],
                    sap(s_bc, 0, SLOTS, n6 * 512, [[1, 512]])), deps=(mmk, d_bhh))

            m_rz0 = T.ve(lambda e: e.tensor_add(s_gate2[:, 0:2048], s_gx[:, 0:2048],
                                                s_gh[:, 0:2048]), deps=(m_gh, m_gx))
            m_rz = T.se(lambda e: e.activation(s_gate2[:, 0:2048],
                                               s_gate2[:, 0:2048], AF.Sigmoid),
                        deps=(m_rz0,))
            T.ve(lambda e: e.tensor_mul(s_gtmp[:], s_gate2[:, 0:1024],
                                        s_gh[:, 2048:3072]), deps=(m_rz,))
            m_n0 = T.ve(lambda e: e.tensor_add(s_gtmp[:], s_gtmp[:],
                                               s_gx[:, 2048:3072]))
            m_n = T.se(lambda e: e.activation(s_gtmp[:], s_gtmp[:], AF.Tanh),
                       deps=(m_n0,))
            T.ve(lambda e: e.tensor_sub(s_snew[:], s_slots[:], s_gtmp[:]),
                 deps=(m_n, m_S))
            T.ve(lambda e: e.tensor_mul(s_snew[:], s_snew[:], s_gate2[:, 1024:2048]))
            m_sn = T.ve(lambda e: e.tensor_add(s_snew[:], s_snew[:], s_gtmp[:]))
            m_snb = T.ve(lambda e: e.tensor_copy(s_snb[:], s_snew[:]))
            m_sfm = None
            for ch in range(FCH):
                mm = T.te(lambda e, ch=ch: e.matmul(
                    ps[4][:, 0:64], s_snb[:, ch * 128:(ch + 1) * 128],
                    k_id[0:64, 0:64], start=True, stop=True),
                    deps=(m_snb,) if ch == 0 else (m_sfm,))
                m_sfm = T.se(lambda e, ch=ch: e.copy(
                    sap(s_fm64, 0, 128, ch * 64, [[1, 64]]), ps[4][:, 0:64]),
                    deps=(mm,))

            wv_cs = None
            for kch in range(FCH):
                d_l = T.dma_sy(lambda e, k=kch: e.dma_start(
                    sap(xf_t, 0, 128, (k % 2) * 1024, [[1, C]]),
                    dap("wvT", k * 128 * C, [[C, 128], [1, C]])),
                    deps=((prev_gwc,) if kch == 0 else ()) +
                    ((wv_cs,) if kch >= 2 else ()), sem=("wA", "wB")[kch % 2])
                wv_cs = T.se(lambda e, k=kch: e.copy(
                    sap(w_big, 0, 128, k * C, [[1, C]]),
                    sap(xf_t, 0, 128, (k % 2) * 1024, [[1, C]])), deps=(d_l,))
            d_wv = wv_cs
            d_bv = T.dma_sy(lambda e: e.dma_start(
                sap(s_bc, 0, SLOTS, 0, [[1, C]]), din["bv_rep"].ap()),
                deps=(m_gh,))
            m_sv = None
            for n2 in range(2):
                mmk = None
                for k in range(FCH):
                    dep = (d_wv, m_sfm) if (k == 0 and n2 == 0) else ()
                    if k == 0 and m_sv is not None:
                        dep = dep + (m_sv,)
                    mmk = T.te(lambda e, k=k, n2=n2: e.matmul(
                        ps[0][0:64, :], sap(s_fm64, 0, 128, k * 64, [[1, 64]]),
                        sap(w_big, 0, 128, k * C + n2 * 512, [[1, 512]]),
                        start=(k == 0), stop=(k == FCH - 1)), deps=dep)
                m_sv = T.ve(lambda e, n2=n2: e.tensor_add(
                    sap(s_sv, 0, SLOTS, n2 * 512, [[1, 512]]), ps[0][0:64, :],
                    sap(s_bc, 0, SLOTS, n2 * 512, [[1, 512]])), deps=(mmk, d_bv))
            m_svb = T.ve(lambda e: e.tensor_copy(s_svb[:], s_sv[:]), deps=(m_sv,))
            m_svfm = None
            for ch in range(FCH):
                mm = T.te(lambda e, ch=ch: e.matmul(
                    ps[4][:, 0:64], s_svb[:, ch * 128:(ch + 1) * 128],
                    k_id[0:64, 0:64], start=True, stop=True),
                    deps=(m_svb,) if ch == 0 else (m_svfm,))
                m_svfm = T.se(lambda e, ch=ch: e.copy(
                    sap(s_fm64b, 0, 128, ch * 64, [[1, 64]]), ps[4][:, 0:64]),
                    deps=(mm,))
            wo2_cs = None
            for kch in range(FCH):
                d_l = T.dma_sy(lambda e, k=kch: e.dma_start(
                    sap(xf_t, 0, 128, (k % 2) * 1024, [[1, C]]),
                    dap("woT", k * 128 * C, [[C, 128], [1, C]])),
                    deps=((m_sv,) if kch == 0 else ()) +
                    ((wo2_cs,) if kch >= 2 else ()), sem=("wA", "wB")[kch % 2])
                wo2_cs = T.se(lambda e, k=kch: e.copy(
                    sap(w_big, 0, 128, k * C, [[1, C]]),
                    sap(xf_t, 0, 128, (k % 2) * 1024, [[1, C]])), deps=(d_l,))
            d_wo2 = wo2_cs
            m_svw = None
            for n2 in range(2):
                mmk = None
                for k in range(FCH):
                    dep = (d_wo2, m_svfm) if (k == 0 and n2 == 0) else ()
                    if k == 0 and m_svw is not None:
                        dep = dep + (m_svw,)
                    mmk = T.te(lambda e, k=k, n2=n2: e.matmul(
                        ps[0][0:64, :], sap(s_fm64b, 0, 128, k * 64, [[1, 64]]),
                        sap(w_big, 0, 128, k * C + n2 * 512, [[1, 512]]),
                        start=(k == 0), stop=(k == FCH - 1)), deps=dep)
                m_svw = T.se(lambda e, n2=n2: e.copy(
                    sap(s_svw, 0, SLOTS, n2 * 512, [[1, 512]]), ps[0][0:64, :]),
                    deps=(mmk,))

            m_fin = None
            out_dmas = []
            finl = []
            for grp in range(4):
                m_xf2 = T.dma_sy(lambda e, g=grp: e.dma_start(
                    sap(xf_t, 0, 128, 0, [[512, FCH], [1, 512]]),
                    AP(x2_hbm, g * 512, [[SEG, 128], [128 * SEG, FCH], [1, 512]])),
                    deps=((m_fin,) if m_fin else (m_x2all, wo2_cs)))
                for mch in range(FCH):
                    bank = ps[1 + mch % 4]
                    slot = (grp * FCH + mch) % 2
                    mdep = (m_svw, m_aes) if (mch == 0 and grp == 0) else ()
                    if len(finl) >= 4:
                        mdep = mdep + (finl[-4],)
                    mmk = T.te(lambda e, m=mch, g=grp, b=bank: e.matmul(
                        b[:, :], s_svw[:, m * 128:(m + 1) * 128],
                        sap(s_alpsm, 0, SLOTS, g * 512, [[1, 512]]),
                        start=True, stop=True),
                        deps=mdep)
                    findep = [mmk, m_xf2, vdeps["bo"]]
                    if len(out_dmas) >= 2:
                        findep.append(out_dmas[-2])
                    m_fin = T.ve(lambda e, m=mch, b=bank, sl=slot:
                                 e.scalar_tensor_tensor(
                        sap(s_ofin, 0, 128, sl * 512, [[1, 512]]),
                        b[:, :], sap(vec["bo"], 0, 128, m, [[1, 1]]),
                        sap(xf_t, 0, 128, m * 512, [[1, 512]]), ALU.add, ALU.add),
                        deps=tuple(findep))
                    finl.append(m_fin)
                    out_dmas.append(T.dma_out(lambda e, m=mch, g=grp, sl=slot:
                                              e.dma_start(
                        AP(out_fm, m * 128 * SEG + g * 512, [[SEG, 128], [1, 512]]),
                        sap(s_ofin, 0, 128, sl * 512, [[1, 512]])), deps=(m_fin,),
                        sem=("oA", "oB")[slot]))
            T._rec("ge", lambda e: e.nop(), None, 0,
                   (("oA", T.semn["oA"]), ("oB", T.semn["oB"])))

        with nc.allow_low_precision(reason="bf16 elementwise validated against problem tolerance"):
            with nc.Block() as block:
                T.finalize(block)

    return nc


def kernel(**inputs):
    per_core = _host_prepare(inputs)
    nc = build_program()
    res = run_bass_kernel_spmd(nc, per_core, core_ids=list(range(NCORES)))
    outs = [r["out_fm"] for r in res.results]
    full = np.concatenate([o.T for o in outs], axis=0)
    return np.ascontiguousarray(full[None], dtype=np.float32)


# version 13
# speedup vs baseline: 1.0127x; 1.0030x over previous
import os
import numpy as np
import ml_dtypes
import concourse.bass as bass
import concourse.mybir as mybir
from concourse.bass_utils import run_bass_kernel_spmd

F32 = mybir.dt.float32
BF16 = mybir.dt.bfloat16
AF = mybir.ActivationFunctionType
ALU = mybir.AluOpType
AX = mybir.AxisListType
AP = bass.AP

C = 1024
NCORES = 8
SEG = 2048
WARM = 128
TS = SEG + WARM
L = 32
FCH = 8
TT = 256
TILES = [WARM] + [TT] * (SEG // TT)
EPS = 1e-5
SLOTS = 64
HFF = 512
GDIM = 3 * C

DBG = os.environ.get("KDBG", "") == "1"
KMAXT = int(os.environ.get("KMAXT", "99"))
KSKIP2 = os.environ.get("KSKIP2", "") == "1"
KSTAGE = int(os.environ.get("KSTAGE", "99"))
KREV = int(os.environ.get("KREV", "7"))
KSEL = int(os.environ.get("KSEL", "15"))
KCH = int(os.environ.get("KCH", "99"))
KBRK = int(os.environ.get("KBRK", "1"))
KLAG = int(os.environ.get("KLAG", "0"))


def _host_prepare(inputs):
    g = lambda n: np.ascontiguousarray(np.asarray(inputs["L0_attn_" + n], np.float32))
    f = lambda n: np.ascontiguousarray(np.asarray(inputs["L0_ffn_" + n], np.float32))
    x = np.ascontiguousarray(np.asarray(inputs["L0_attn_hidden_states"], np.float32)[0])

    shared = {}
    for n in ("W_r", "W_k", "W_v", "W_o"):
        shared[n] = g(n)
    for n in ("w_lora_a", "w_lora_b", "a_lora_a", "a_lora_b", "g_lora_a", "g_lora_b"):
        shared[n] = g(n)

    def fmcol(v):
        return np.ascontiguousarray(v.reshape(FCH, 128).T)

    for n in ("x_r", "x_w", "x_k", "x_v", "x_a", "k_k", "k_a",
              "w_lora_bias", "a_lora_bias", "g_lora_bias"):
        shared["v_" + n] = fmcol(g(n))
    shared["v_r_k"] = fmcol(g("r_k").reshape(-1))
    shared["v_bo"] = fmcol(f("bo"))
    shared["gnw_rep"] = np.ascontiguousarray(np.broadcast_to(g("gn_weight").reshape(1, C), (128, C)))
    shared["gnb_rep"] = np.ascontiguousarray(np.broadcast_to(g("gn_bias").reshape(1, C), (128, C)))

    w1 = f("w1")
    shared["w1aT"] = np.ascontiguousarray(w1[:, :C].T)
    shared["w1bT"] = np.ascontiguousarray(w1[:, C:].T)
    shared["w2T"] = np.ascontiguousarray(f("w2").T)
    shared["gru_wihT"] = np.ascontiguousarray(f("gru_wih").T)
    shared["gru_whhT"] = np.ascontiguousarray(f("gru_whh").T)
    shared["wvT"] = np.ascontiguousarray(f("wv").T)
    shared["woT"] = np.ascontiguousarray(f("wo").T)
    shared["slot_init"] = f("slot_init")
    shared["v_slot_scale"] = f("slot_scale").reshape(SLOTS, 1)
    shared["v_b1"] = np.ascontiguousarray(f("b1").reshape(4, 128).T)
    shared["b2_rep"] = np.ascontiguousarray(np.broadcast_to(f("b2").reshape(1, SLOTS), (128, SLOTS)))
    shared["bv_rep"] = np.ascontiguousarray(np.broadcast_to(f("bv").reshape(1, C), (SLOTS, C)))
    shared["bih_rep"] = np.ascontiguousarray(np.broadcast_to(f("gru_bih").reshape(1, GDIM), (SLOTS, GDIM)))
    shared["bhh_rep"] = np.ascontiguousarray(np.broadcast_to(f("gru_bhh").reshape(1, GDIM), (SLOTS, GDIM)))
    shared["tau"] = f("tau").reshape(1, 1)

    shared["identity128"] = np.eye(128, dtype=np.float32)
    shared["ones_col"] = np.ones((128, 1), np.float32)
    shared["ones_row"] = np.ones((1, 128), np.float32)
    shared["ones_rowf"] = np.ones((1, 128), np.float32)
    shared["ones_f32"] = np.ones((128, TT), np.float32)
    onespat = np.zeros((128, FCH, 16), np.float32)
    for ch in range(FCH):
        onespat[0:64, ch, 2 * ch] = 1.0
        onespat[64:128, ch, 2 * ch + 1] = 1.0
    shared["onespat"] = np.ascontiguousarray(onespat.reshape(128, FCH * 16))
    sel2 = np.zeros((2, 128), np.float32)
    sel2[0, 0:64] = 1.0
    sel2[1, 64:128] = 1.0
    shared["sel2"] = sel2
    mtrb = np.zeros((32, 64), np.float32)
    for s in range(32):
        for t in range(32):
            mtrb[s, t] = 1.0 if s < t else 0.0
            mtrb[s, 32 + t] = 1.0 if s <= t else 0.0
    shared["mask_trb"] = np.ascontiguousarray(np.tile(np.tile(mtrb, (4, 1)), (1, 8)))
    mw0 = np.tril(np.ones((32, 32), np.float32), -1)
    shared["mask_w0"] = np.ascontiguousarray(np.tile(np.tile(mw0, (4, 1)), (1, 16)))

    per_core = []
    for cid in range(NCORES):
        lo = cid * SEG
        ext = WARM + 1
        if lo - ext >= 0:
            xs = x[lo - ext:lo + SEG]
        else:
            xs = np.concatenate([np.zeros((ext - lo, C), np.float32), x[:lo + SEG]], 0)
        m = dict(shared)
        m["xT"] = np.ascontiguousarray(xs.T)
        per_core.append(m)
    return per_core


class KB:
    ENGS = ("te", "ve", "se", "ge", "sy")
    SEM_OF = {"te": "pe", "ve": "ve", "se": "se", "ge": "geC", "sy": "syC"}

    class Mark:
        __slots__ = ("sem", "val")

        def __init__(self, sem, val):
            self.sem = sem
            self.val = val

    def __init__(self, nc, es):
        self.nc = nc
        self.recs = {k: [] for k in self.ENGS}
        self.semh = {}
        self.semn = {}
        for name in ("pe", "ve", "se", "geC", "syC", "syQ", "geQ", "scQ",
                     "cc", "outQ", "wA", "wB", "oA", "oB", "x2s"):
            self.semh[name] = es.enter_context(nc.semaphore("s_" + name))
            self.semn[name] = 0

    DMA_SEMS = ("syQ", "geQ", "scQ", "outQ", "wA", "wB", "oA", "oB", "x2s")

    def _rec(self, eng, fn, inc, k, deps):
        own = self.SEM_OF.get(eng)
        if eng != "te" and own is not None and self.semn[own] > 0:
            deps = tuple(deps) + (self.Mark(own, self.semn[own]),)
        if inc in self.DMA_SEMS:
            deps = tuple(deps) + (self.Mark(inc, self.semn[inc]),)
        tgt = None
        if inc is not None:
            self.semn[inc] += k
            tgt = self.Mark(inc, self.semn[inc])
        self.recs[eng].append((fn, inc, k, tuple(d for d in deps
                                                 if d is not None)))
        return tgt

    def te(self, fn, deps=()):
        return self._rec("te", fn, "pe", 1, deps)

    def ve(self, fn, deps=()):
        return self._rec("ve", fn, "ve", 1, deps)

    def se(self, fn, deps=()):
        return self._rec("se", fn, "se", 1, deps)

    def ge(self, fn, deps=()):
        return self._rec("ge", fn, "geC", 1, deps)

    def dma_sy(self, fn, deps=(), sem="syQ"):
        return self._rec("sy", fn, sem, 16, deps)

    def dma_ge(self, fn, deps=(), sem="geQ"):
        return self._rec("ge", fn, sem, 16, deps)

    def dma_sc(self, fn, deps=(), sem="scQ"):
        return self._rec("se", fn, sem, 16, deps)

    def dma_out(self, fn, deps=(), sem=None):
        return self._rec("sy", fn, sem or "outQ", 16, deps)

    def barrier(self):
        marks = [self.te(lambda e: e.nop()),
                 self.ve(lambda e: e.nop()),
                 self.se(lambda e: e.nop()),
                 self.ge(lambda e: e.nop())]
        marks += [self.Mark(s, self.semn[s]) for s in
                  ("syQ", "geQ", "scQ", "wA", "wB", "oA", "oB", "x2s")]
        for eng in ("te", "ve", "se", "ge", "sy"):
            self._rec(eng, lambda e: None, None, 0, marks)
        return marks

    @staticmethod
    def _norm(d):
        if isinstance(d, KB.Mark):
            return (d.sem, d.val)
        return d

    def finalize(self, block):
        te_waits = []
        te_real = []
        waited = {}
        pe_seen = 0
        run_len = 0
        prev_bnd = 0
        for fn, inc, k, deps in self.recs["te"]:
            w = []
            for d in deps:
                sem, val = self._norm(d)
                if sem == "pe":
                    continue
                if waited.get(sem, -1) < val:
                    waited[sem] = val
                    w.append((sem, val))
            te_real.append(bool(w))
            if w:
                prev_bnd = pe_seen
                run_len = 0
            else:
                run_len += 1
                if KBRK and prev_bnd > 0 and run_len >= KBRK:
                    w.append(("pe", prev_bnd))
                    run_len = 0
            te_waits.append(w)
            pe_seen += k if inc else 0
        n_te = len(self.recs["te"])
        run_end = [0] * n_te
        cnt = 0
        ends = []
        starts = []
        pe_cnt = [0] * n_te
        c = 0
        for i in range(n_te):
            c += self.recs["te"][i][2] if self.recs["te"][i][1] else 0
            pe_cnt[i] = c
        i = 0
        while i < n_te:
            j = i + 1
            while j < n_te and not te_real[j]:
                j += 1
            for q in range(i, j):
                run_end[q] = pe_cnt[j - 1]
            i = j
        val2idx = {}
        for i in range(n_te):
            val2idx[pe_cnt[i]] = i

        def resolve(sem, val):
            if sem == "pe" and val in val2idx:
                return (sem, run_end[val2idx[val]])
            return (sem, val)

        decs = {"te": block.tensor, "ve": block.vector, "se": block.scalar,
                "ge": block.gpsimd, "sy": block.sync}
        semh = self.semh
        for eng in self.ENGS:
            if eng == "te":
                oplist = []
                for idx, (fn, inc, k, deps) in enumerate(self.recs["te"]):
                    oplist.append((fn, inc, k, tuple(te_waits[idx])))
            else:
                oplist = []
                waited = {}
                for fn, inc, k, deps in self.recs[eng]:
                    w = []
                    for d in deps:
                        sem, val = resolve(*self._norm(d))
                        if waited.get(sem, -1) < val:
                            waited[sem] = val
                            w.append((sem, val))
                    oplist.append((fn, inc, k, tuple(w)))

            def body(e, oplist=oplist):
                for fn, inc, k, waits in oplist:
                    for sname, val in waits:
                        e.wait_ge(semh[sname], val)
                    ins = fn(e)
                    if inc is not None and ins is not None:
                        ins.then_inc(semh[inc], k)
            decs[eng](body)


def sap(buf, p0, npart, col, free=()):
    pitch = int(np.prod(buf.shape[1:]))
    dims = [[pitch, npart]] + [list(d) for d in free]
    return AP(buf, p0 * pitch + col, dims)


def build_program():
    import contextlib
    nc = bass.Bass("TRN2", target_bir_lowering=False)
    es = contextlib.ExitStack()

    din = {}

    def dram_in(name, shape, dt=F32):
        t = nc.dram_tensor(name, list(shape), dt, kind="ExternalInput")
        din[name] = t

    dram_in("xT", (C, TS + 1))
    for n in ("W_r", "W_k", "W_v", "W_o", "wvT", "woT"):
        dram_in(n, (C, C))
    dram_in("w_lora_a", (C, 64)); dram_in("w_lora_b", (64, C))
    dram_in("a_lora_a", (C, 64)); dram_in("a_lora_b", (64, C))
    dram_in("g_lora_a", (C, 128)); dram_in("g_lora_b", (128, C))
    for n in ("x_r", "x_w", "x_k", "x_v", "x_a", "k_k", "k_a",
              "w_lora_bias", "a_lora_bias", "g_lora_bias", "r_k", "bo"):
        dram_in("v_" + n, (128, FCH))
    dram_in("gnw_rep", (128, C)); dram_in("gnb_rep", (128, C))
    dram_in("w1aT", (C, HFF)); dram_in("w1bT", (C, HFF)); dram_in("w2T", (HFF, SLOTS))
    dram_in("gru_wihT", (C, GDIM)); dram_in("gru_whhT", (C, GDIM))
    dram_in("slot_init", (SLOTS, C)); dram_in("v_slot_scale", (SLOTS, 1))
    dram_in("v_b1", (128, 4)); dram_in("b2_rep", (128, SLOTS))
    dram_in("bv_rep", (SLOTS, C))
    dram_in("bih_rep", (SLOTS, GDIM)); dram_in("bhh_rep", (SLOTS, GDIM))
    dram_in("tau", (1, 1))
    dram_in("identity128", (128, 128)); dram_in("ones_col", (128, 1))
    dram_in("ones_row", (1, 128)); dram_in("ones_rowf", (1, 128))
    dram_in("ones_f32", (128, TT))
    dram_in("onespat", (128, FCH * 16)); dram_in("sel2", (2, 128))
    dram_in("mask_trb", (128, 512)); dram_in("mask_w0", (128, 512))

    def dap(name, col, dims):
        return AP(din[name], col, dims)

    out_fm = nc.dram_tensor("out_fm", [C, SEG], F32, kind="ExternalOutput")
    dbg = {}

    def dbg_t(name, shape, dt=F32):
        if DBG and name not in dbg:
            dbg[name] = nc.dram_tensor("dbg_" + name, list(shape), dt,
                                       kind="ExternalOutput")
        return dbg.get(name)

    wsc = {n: nc.dram_tensor("wsc_" + n, [C, C], BF16, kind="Internal")
           for n in ("W_r", "W_k", "W_v")}
    cc_in = nc.dram_tensor("cc_in", [SLOTS, C + 16], F32, kind="Internal").ap().opt()
    cc_out = nc.dram_tensor("cc_out", [SLOTS, C + 16], F32, kind="Internal").ap().opt()
    x2_hbm = nc.dram_tensor("x2_hbm", [C, SEG], F32, kind="Internal")

    with es:
        T = KB(nc, es)

        def sbuf(name, shape, dt):
            return es.enter_context(nc.sbuf_tensor(name, list(shape), dt))

        k_id = sbuf("k_id", (128, 128), BF16)
        k_onescol = sbuf("k_onescol", (128, 1), BF16)
        k_onescolf = sbuf("k_onescolf", (128, 1), F32)
        k_onesrow = sbuf("k_onesrow", (1, 128), BF16)
        k_onesrowf = sbuf("k_onesrowf", (1, 128), F32)
        k_onesf32 = sbuf("k_onesf32", (128, TT), F32)
        k_onespat = sbuf("k_onespat", (128, FCH * 16), BF16)
        k_sel2 = sbuf("k_sel2", (2, 128), BF16)
        k_mtrb = sbuf("k_mtrb", (128, 512), BF16)
        k_mw0 = sbuf("k_mw0", (128, 512), BF16)
        vec = {}
        for n in ("x_r", "x_w", "x_k", "x_v", "x_a", "k_k", "k_a",
                  "w_lora_bias", "a_lora_bias", "g_lora_bias", "r_k", "bo"):
            vec[n] = sbuf("vec_" + n, (128, FCH), F32)
        v_omka = sbuf("v_omka", (128, FCH), F32)
        v_negwb = sbuf("v_negwb", (128, FCH), F32)
        gw_tm = sbuf("gw_tm", (128, C), BF16)
        gb_tm = sbuf("gb_tm", (128, C), BF16)
        k_consts = sbuf("k_consts", (128, 4), F32)
        wo_in = sbuf("wo_in", (128, FCH * SEG), BF16)

        ps = [es.enter_context(nc.psum_tensor("ps%d" % i, [128, 512], F32))
              for i in range(8)]

        dep_c = []
        dep_c.append(T.dma_sy(lambda e: e.dma_start(k_onesf32[:], din["ones_f32"].ap())))
        dep_c.append(T.dma_sy(lambda e: e.dma_start(k_onesrowf[:], din["ones_rowf"].ap())))
        dep_c.append(T.dma_sy(lambda e: e.dma_start(k_onescolf[:],
                                                    AP(din["ones_col"], 0, [[1, 128], [1, 1]]))))
        vdeps = {}
        for n in vec:
            vdeps[n] = T.dma_sy(lambda e, n=n: e.dma_start(vec[n][:], din["v_" + n].ap()))
        for name, buf in (("identity128", k_id), ("ones_col", k_onescol),
                          ("ones_row", k_onesrow), ("onespat", k_onespat),
                          ("sel2", k_sel2), ("mask_trb", k_mtrb), ("mask_w0", k_mw0)):
            dep_c.append(T.dma_ge(lambda e, b=buf, nm=name: e.dma_start(b[:], din[nm].ap())))
        T.ve(lambda e: e.memset(k_consts[:, 0:1], EPS))
        T.ve(lambda e: e.memset(k_consts[:, 1:2], -1.0))
        T.ve(lambda e: e.memset(k_consts[:, 2:3], -0.5))
        nc.const_aps.register(F32, EPS, k_consts[:, 0:1])
        nc.const_aps.register(F32, -1.0, k_consts[:, 1:2])
        nc.const_aps.register(F32, -0.5, k_consts[:, 2:3])
        m_omka = T.ve(lambda e: e.tensor_scalar(v_omka[:], vec["k_a"][:], -1.0, 1.0,
                                                ALU.mult, ALU.add), deps=(vdeps["k_a"],))
        m_negwb = T.ve(lambda e: e.tensor_scalar_mul(v_negwb[:], vec["w_lora_bias"][:],
                                                     -1.0), deps=(vdeps["w_lora_bias"],))
        T.barrier()

        p1 = contextlib.ExitStack()
        with p1:
            def sb1(name, shape, dt):
                return p1.enter_context(nc.sbuf_tensor(name, list(shape), dt))

            w_wla = sb1("w_wla", (128, FCH * 64), BF16)
            w_wlb = sb1("w_wlb", (64, C), BF16)
            w_ala = sb1("w_ala", (128, FCH * 64), BF16)
            w_alb = sb1("w_alb", (64, C), BF16)
            w_gla = sb1("w_gla", (128, FCH * 128), BF16)
            w_glb = sb1("w_glb", (128, C), BF16)
            w_st = sb1("w_st", (128, 2 * C), BF16)

            z_f32 = sb1("z_f32", (128, FCH * 64), F32)
            z_bf = sb1("z_bf", (128, FCH * 64), BF16)
            g_carry = sb1("g_carry", (128, FCH), F32)

            NW1 = TT + 1
            xf = sb1("xf", (128, FCH * NW1), F32)
            ln_f = sb1("ln_f", (128, FCH * NW1), F32)
            cb_f = sb1("cb_f", (128, 2 * NW1), F32)
            dd_b = sb1("dd_b", (128, FCH * TT), BF16)
            lrp = sb1("lrp", (128, FCH * TT), BF16)
            r_b = sb1("r_b", (128, FCH * TT), BF16)
            v_b = sb1("v_b", (128, FCH * TT), BF16)
            ah_f = sb1("ah_f", (128, FCH * TT), F32)
            kk_f = sb1("kk_f", (128, FCH * TT), F32)
            sp_f = sb1("sp_f", (128, FCH * TT), F32)
            g_f = sb1("g_f", (128, FCH * TT), F32)
            gbase = sb1("gbase", (128, FCH * 8), F32)
            clc_f = sb1("clc_f", (128, FCH * 8), F32)
            d1_f = sb1("d1_f", (128, FCH * TT), F32)
            bk_b = sb1("bk_b", (128, 2 * FCH * TT), BF16)
            ar_b = sb1("ar_b", (128, 2 * FCH * TT), BF16)
            bh_b = sb1("bh_b", (128, FCH * TT), BF16)
            kh_b = sb1("kh_b", (128, FCH * TT), BF16)
            gate_b = sb1("gate_b", (128, FCH * TT), BF16)
            bonus_b = sb1("bonus_b", (128, FCH * TT), BF16)
            scr_b = sb1("scr_b", (128, FCH * NW1), BF16)
            mid_b = sb1("mid_b", (128, TT), BF16)
            row_f = sb1("row_f", (1, 4 * NW1), F32)
            row_c0 = sb1("row_c0", (1, NW1), F32)
            row_c1 = sb1("row_c1", (1, NW1), F32)
            nrm_b = sb1("nrm_b", (2, FCH * TT), BF16)
            stm = sb1("stm", (128, 3 * 1024), BF16)
            strb = sb1("strb", (128, 1024), BF16)
            smrk = sb1("smrk", (128, 1024), BF16)
            sw0 = sb1("sw0", (128, 512), BF16)
            ssq = sb1("ssq", (128, 7 * 512), BF16)
            sxp_f = [sb1("sxpf0", (128, 1024), F32), sb1("sxpf1", (128, 1024), F32)]
            sxp_b = sb1("sxp_b", (128, 1024), BF16)
            o_tm = xf
            gnst = sb1("gnst", (128, 6 * 16), F32)

            def fmS(buf, ch, col, n, width=TT):
                return sap(buf, 0, 128, ch * width + col, [[1, n]])

            def fm3(buf, col, n, width=TT, nch=FCH, base=0):
                return sap(buf, 0, 128, base + col, [[width, nch], [1, n]])

            def ldw_small(dst, nm, mcols):
                return T.dma_ge(lambda e, d=dst, nm=nm, m=mcols: e.dma_start(
                    sap(d, 0, 128, 0, [[m, FCH], [1, m]]),
                    dap(nm, 0, [[m, 128], [128 * m, FCH], [1, m]])))

            ldw_small(w_wla, "w_lora_a", 64)
            ldw_small(w_ala, "a_lora_a", 64)
            ldw_small(w_gla, "g_lora_a", 128)
            T.dma_ge(lambda e: e.dma_start(w_wlb[:], din["w_lora_b"].ap()))
            T.dma_ge(lambda e: e.dma_start(w_alb[:], din["a_lora_b"].ap()))
            T.dma_ge(lambda e: e.dma_start(w_glb[:], din["g_lora_b"].ap()))
            d_lora = ("geQ", T.semn["geQ"])
            d_wla = d_ala = d_gla = d_wlb = d_alb = d_glb = d_lora

            T.ve(lambda e: e.memset(z_f32[:], 0.0))
            m_zb0 = T.ve(lambda e: e.memset(z_bf[:], 0.0))
            m_zi = T.ve(lambda e: e.memset(g_carry[:], 0.0))

            wl_marks = [None, None]
            
            wc_marks = [None, None]
            ws_marks = [None, None]
            blocks = [(nm, k) for nm in ("W_k", "W_r", "W_v") for k in range(FCH)]
            for i, (nm, k) in enumerate(blocks):
                h = i % 2
                ldep = (wc_marks[h],) if wc_marks[h] else ()
                wl_marks[h] = T.dma_sy(
                    lambda e, nm=nm, k=k, h=h: e.dma_start(
                        sap(xf, 0, 128, h * 1024, [[1, C]]),
                        dap(nm, k * 128 * C, [[C, 128], [1, C]])),
                    deps=ldep, sem=("wA", "wB")[h])
                cdep = (wl_marks[h],) + ((ws_marks[h],) if ws_marks[h] else ())
                wc_marks[h] = T.se(lambda e, h=h: e.copy(
                    sap(w_st, 0, 128, h * 1024, [[1, C]]),
                    sap(xf, 0, 128, h * 1024, [[1, C]])), deps=cdep)
                ws_marks[h] = T.dma_sc(
                    lambda e, nm=nm, k=k, h=h: e.dma_start(
                        AP(wsc[nm], k * 128 * C, [[C, 128], [1, C]]),
                        sap(w_st, 0, 128, h * 1024, [[1, C]])),
                    deps=(wc_marks[h],), sem=("oA", "oB")[h])
            m_wsc = (("oA", T.semn["oA"]), ("oB", T.semn["oB"]),
                     ("wA", T.semn["wA"]), ("wB", T.semn["wB"]))
            d_gnw = T.dma_sy(lambda e: e.dma_start(
                sap(xf, 0, 128, 0, [[1, C]]), din["gnw_rep"].ap()),
                deps=(wc_marks[0],), sem="wA")
            d_gnb = T.dma_sc(lambda e: e.dma_start(
                sap(xf, 0, 128, 1024, [[1, C]]), din["gnb_rep"].ap()),
                deps=(wc_marks[1],), sem="wB")
            T.se(lambda e: e.copy(gw_tm[:], sap(xf, 0, 128, 0, [[1, C]])),
                 deps=(d_gnw, ws_marks[0]))
            m_gnbc = T.se(lambda e: e.copy(gb_tm[:], sap(xf, 0, 128, 1024, [[1, C]])),
                          deps=(d_gnb, ws_marks[1]))

            scan_t0 = 0
            prev_zb = m_zb0
            prev_gn_add = None
            for ti, TTi in enumerate(TILES[:KMAXT]):
                is_warm = (ti == 0)
                ncht = TTi // L
                NW = TTi + 1

                m_xf = T.dma_sc(lambda e, t0=scan_t0, n=NW: e.dma_start(
                    fm3(xf, 0, n, width=NW1),
                    dap("xT", t0, [[TS + 1, 128], [128 * (TS + 1), FCH], [1, n]])),
                    deps=(m_gnbc,) if ti == 0 else ())

                for ch in range(FCH):
                    mm = T.te(lambda e, ch=ch, n=NW: e.matmul(
                        ps[4][0:1, 0:n], k_onescolf[:], fmS(xf, ch, 0, n, width=NW1),
                        start=(ch == 0), stop=(ch == FCH - 1)),
                        deps=(m_xf,) if ch == 0 else ())
                m_sq = T.se(lambda e, n=NW: e.activation(
                    sap(scr_b, 0, 128, 0, [[1, FCH * n]]) if n == NW1 else
                    fm3(scr_b, 0, n, width=NW1),
                    sap(xf, 0, 128, 0, [[1, FCH * n]]) if n == NW1 else
                    fm3(xf, 0, n, width=NW1),
                    AF.Square), deps=(m_xf,))
                for ch in range(FCH):
                    mm = T.te(lambda e, ch=ch, n=NW: e.matmul(
                        ps[5][0:1, 0:n], k_onescol[:], fmS(scr_b, ch, 0, n, width=NW1),
                        start=(ch == 0), stop=(ch == FCH - 1)),
                        deps=(m_sq,) if ch == 0 else ())
                m_stat = mm
                T.ve(lambda e, n=NW: e.tensor_scalar_mul(AP(row_f, 0 * NW1, [[4 * NW1, 1], [1, n]]),
                                                         ps[4][0:1, 0:n], 1.0 / C),
                     deps=(m_stat,))
                T.ve(lambda e, n=NW: e.tensor_mul(AP(row_f, 1 * NW1, [[4 * NW1, 1], [1, n]]), AP(row_f, 0 * NW1, [[4 * NW1, 1], [1, n]]),
                                                  AP(row_f, 0 * NW1, [[4 * NW1, 1], [1, n]])))
                m_var = T.ve(lambda e, n=NW: e.scalar_tensor_tensor(
                    AP(row_f, 2 * NW1, [[4 * NW1, 1], [1, n]]), ps[5][0:1, 0:n], 1.0 / C, AP(row_f, 1 * NW1, [[4 * NW1, 1], [1, n]]),
                    ALU.mult, ALU.subtract))
                m_sd = T.se(lambda e, n=NW: e.activation(AP(row_f, 3 * NW1, [[4 * NW1, 1], [1, n]]),
                                                         AP(row_f, 2 * NW1, [[4 * NW1, 1], [1, n]]), AF.Sqrt,
                                                         bias=EPS), deps=(m_var,))
                T.ve(lambda e, n=NW: e.reciprocal(AP(row_f, 2 * NW1, [[4 * NW1, 1], [1, n]]), AP(row_f, 3 * NW1, [[4 * NW1, 1], [1, n]])),
                     deps=(m_sd,))
                T.ve(lambda e, n=NW: e.tensor_copy(row_c1[0:1, 0:n], AP(row_f, 2 * NW1, [[4 * NW1, 1], [1, n]])))
                m_rows = T.ve(lambda e, n=NW: e.scalar_tensor_tensor(
                    row_c0[0:1, 0:n], AP(row_f, 0 * NW1, [[4 * NW1, 1], [1, n]]), -1.0, AP(row_f, 2 * NW1, [[4 * NW1, 1], [1, n]]),
                    ALU.mult, ALU.mult))
                T.te(lambda e, n=NW: e.matmul(ps[6][:, 0:n], k_onesrowf[0:1, :],
                                              row_c0[0:1, 0:n], start=True, stop=True),
                     deps=(m_rows,))
                m_bc = T.te(lambda e, n=NW: e.matmul(ps[7][:, 0:n], k_onesrowf[0:1, :],
                                                     row_c1[0:1, 0:n],
                                                     start=True, stop=True))
                T.se(lambda e, n=NW: e.copy(sap(cb_f, 0, 128, 0, [[1, n]]),
                                            ps[6][:, 0:n]), deps=(m_bc,))
                m_cb = T.se(lambda e, n=NW: e.copy(sap(cb_f, 0, 128, NW1, [[1, n]]),
                                                   ps[7][:, 0:n]))
                T.ve(lambda e, n=NW: e.tensor_mul(
                    fm3(ln_f, 0, n, width=NW1), fm3(xf, 0, n, width=NW1),
                    sap(cb_f, 0, 128, NW1, [[0, FCH], [1, n]])), deps=(m_cb,))
                m_ln = T.ve(lambda e, n=NW: e.tensor_add(
                    fm3(ln_f, 0, n, width=NW1), fm3(ln_f, 0, n, width=NW1),
                    sap(cb_f, 0, 128, 0, [[0, FCH], [1, n]])))
                m_dd = T.ve(lambda e, n=TTi: e.tensor_sub(
                    fm3(dd_b, 0, n), fm3(ln_f, 0, n, width=NW1),
                    fm3(ln_f, 1, n, width=NW1)))

                def lerp(nm, extra=()):
                    mk = None
                    for ch in range(FCH):
                        mk = T.ve(lambda e, ch=ch, n=TTi, nm=nm: e.scalar_tensor_tensor(
                            fmS(lrp, ch, 0, n), fmS(dd_b, ch, 0, n),
                            sap(vec[nm], 0, 128, ch, [[1, 1]]),
                            sap(ln_f, 0, 128, ch * NW1 + 1, [[1, n]]),
                            ALU.mult, ALU.add),
                            deps=((m_dd, vdeps[nm]) + tuple(extra)) if ch == 0 else ())
                    return mk

                def proj(wname, m_lrp, evac_fn, extra=()):
                    d_w = [None, None]
                    d_w[0] = T.dma_sy(lambda e, nm=wname: e.dma_start(
                        sap(w_st, 0, 128, 0, [[1, C]]),
                        AP(wsc[nm], 0, [[C, 128], [1, C]])),
                        deps=(m_lrp,) + m_wsc, sem="wA")
                    mmk = None
                    for kch in range(FCH):
                        if kch + 1 < FCH:
                            dmaf = T.dma_sy if (kch + 1) % 2 == 0 else T.dma_sc
                            d_w[(kch + 1) % 2] = dmaf(
                                lambda e, nm=wname, k=kch + 1: e.dma_start(
                                    sap(w_st, 0, 128, (k % 2) * C, [[1, C]]),
                                    AP(wsc[nm], k * 128 * C, [[C, 128], [1, C]])),
                                deps=((mmk,) if kch >= 1 else m_wsc),
                                sem=("wA", "wB")[(kch + 1) % 2])
                        for mch in range(FCH):
                            bank = ps[mch]
                            dep = ()
                            if mch == 0:
                                dep = (d_w[kch % 2], m_lrp)
                                if kch == 0:
                                    dep = dep + tuple(extra)
                            mmk = T.te(lambda e, m=mch, k=kch, b=bank, n=TTi:
                                       e.matmul(
                                b[:, 0:n],
                                sap(w_st, 0, 128, (k % 2) * C + m * 128, [[1, 128]]),
                                fmS(lrp, k, 0, n),
                                start=(k == 0), stop=(k == FCH - 1)), deps=dep)
                    marks = []
                    for mch in range(FCH):
                        marks.append(evac_fn(mch, ps[mch],
                                             (mmk,) if mch == 0 else ()))
                    return marks

                if KSTAGE >= 12:
                    m_lrp = lerp("x_w")
                    mmk = None
                    for kch in range(FCH):
                        dep = (m_lrp, d_wla) if kch == 0 else ()
                        mmk = T.te(lambda e, k=kch, n=TTi: e.matmul(
                            ps[6][0:64, 0:n], sap(w_wla, 0, 128, k * 64, [[1, 64]]),
                            fmS(lrp, k, 0, n), start=(k == 0), stop=(k == FCH - 1)), deps=dep)
                    m_wmid = T.se(lambda e, n=TTi: e.activation(mid_b[0:64, 0:n],
                                                                ps[6][0:64, 0:n], AF.Tanh),
                                  deps=(mmk,))
                    ev_w = None
                    evl = []
                    for mch in range(FCH):
                        bank = ps[6 + mch % 2]
                        dep = (m_wmid, d_wlb, m_negwb) if mch == 0 else ()
                        if mch >= 2:
                            dep = dep + (evl[mch - 2],)
                        mmk = T.te(lambda e, m=mch, b=bank, n=TTi: e.matmul(
                            b[:, 0:n], sap(w_wlb, 0, 64, m * 128, [[1, 128]]),
                            mid_b[0:64, 0:n], start=True, stop=True), deps=dep)
                        ev_x = T.se(lambda e, m=mch, b=bank, n=TTi: e.activation(
                            fmS(g_f, m, 0, n), b[:, 0:n], AF.Exp,
                            bias=sap(v_negwb, 0, 128, m, [[1, 1]]), scale=-1.0),
                            deps=(mmk,))
                        evl.append(ev_x)
                        ev_w = T.se(lambda e, m=mch, n=TTi: e.activation(
                            fmS(sp_f, m, 0, n), fmS(g_f, m, 0, n),
                            AF.Ln, bias=1.0))
                    m_w = T.ve(lambda e, n=TTi: e.tensor_scalar(
                        fm3(sp_f, 0, n), fm3(sp_f, 0, n), -1.0, -0.5, ALU.mult, ALU.add),
                        deps=(ev_w,))


                if KSTAGE >= 13:
                    mG = None
                    for ch in range(FCH):
                        dep = (m_w, m_zi) if ch == 0 else ()
                        mG = T.ve(lambda e, ch=ch, n=TTi: e.tensor_tensor_scan(
                            fmS(g_f, ch, 0, n), k_onesf32[:, 0:n], fmS(sp_f, ch, 0, n),
                            sap(g_carry, 0, 128, ch, [[1, 1]]), ALU.mult, ALU.add), deps=dep)
                    T.ve(lambda e: e.tensor_sub(
                        sap(gbase, 0, 128, 0, [[8, FCH], [1, 1]]),
                        sap(g_f, 0, 128, 0, [[TT, FCH], [1, 1]]),
                        sap(sp_f, 0, 128, 0, [[TT, FCH], [1, 1]])))
                    if ncht > 1:
                        m_gb = T.ve(lambda e, nj=ncht: e.tensor_copy(
                            sap(gbase, 0, 128, 1, [[8, FCH], [1, nj - 1]]),
                            sap(g_f, 0, 128, L - 1, [[TT, FCH], [L, nj - 1]])))
                    else:
                        m_gb = mG
                    m_gc = T.ve(lambda e, n=TTi: e.tensor_copy(
                        g_carry[:, 0:FCH],
                        sap(g_f, 0, 128, n - 1, [[TT, FCH], [1, 1]])))
                    m_a1 = T.ve(lambda e, nj=ncht: e.tensor_sub(
                        sap(sp_f, 0, 128, 0, [[TT, FCH], [L, nj], [1, L]]),
                        sap(g_f, 0, 128, 0, [[TT, FCH], [L, nj], [1, L]]),
                        sap(gbase, 0, 128, 0, [[8, FCH], [1, nj], [0, L]])))
                    m_d1 = T.se(lambda e, n=TTi: e.activation(fm3(d1_f, 0, n),
                                                              fm3(sp_f, 0, n), AF.Exp),
                                deps=(m_a1,))
                    m_clc = T.se(lambda e, nj=ncht: e.activation(
                        sap(clc_f, 0, 128, 0, [[8, FCH], [1, nj]]),
                        sap(sp_f, 0, 128, L - 1, [[TT, FCH], [L, nj]]), AF.Exp))
                    m_d3 = T.ve(lambda e, n=TTi: e.reciprocal(fm3(sp_f, 0, n),
                                                              fm3(d1_f, 0, n)),
                                deps=(m_d1, m_clc))


                if KSTAGE >= 14:
                    m_lrp = lerp("x_a", extra=(m_w,))
                    mmk = None
                    for kch in range(FCH):
                        dep = (m_lrp, d_ala) if kch == 0 else ()
                        mmk = T.te(lambda e, k=kch, n=TTi: e.matmul(
                            ps[6][0:64, 0:n], sap(w_ala, 0, 128, k * 64, [[1, 64]]),
                            fmS(lrp, k, 0, n), start=(k == 0), stop=(k == FCH - 1)), deps=dep)
                    m_amid = T.se(lambda e, n=TTi: e.copy(mid_b[0:64, 0:n], ps[6][0:64, 0:n]),
                                  deps=(mmk, m_d1))
                    ev_a = None
                    evl = []
                    for mch in range(FCH):
                        bank = ps[6 + mch % 2]
                        dep = (m_amid, d_alb) if mch == 0 else ()
                        if mch >= 2:
                            dep = dep + (evl[mch - 2],)
                        mmk = T.te(lambda e, m=mch, b=bank, n=TTi: e.matmul(
                            b[:, 0:n], sap(w_alb, 0, 64, m * 128, [[1, 128]]),
                            mid_b[0:64, 0:n], start=True, stop=True), deps=dep)
                        ev_a = T.se(lambda e, m=mch, b=bank, n=TTi: e.activation(
                            fmS(ah_f, m, 0, n), b[:, 0:n], AF.Sigmoid,
                            bias=sap(vec["a_lora_bias"], 0, 128, m, [[1, 1]])), deps=(mmk,))
                        evl.append(ev_a)
                    m_ah = ev_a

                    m_kms = None
                    for ch in range(FCH):
                        dep = (m_ah, m_omka) if ch == 0 else ()
                        m_kms = T.ve(lambda e, ch=ch, n=TTi: e.tensor_scalar(
                            fmS(scr_b, ch, 0, n, width=NW1), fmS(ah_f, ch, 0, n),
                            sap(vec["k_a"], 0, 128, ch, [[1, 1]]),
                            sap(v_omka, 0, 128, ch, [[1, 1]]), ALU.mult, ALU.add), deps=dep)


                if KSTAGE >= 15:
                    m_lrp = lerp("x_k", extra=(m_kms,))

                    def evac_k(mch, bank, dep0):
                        dep = dep0 + ((m_kms, vdeps["k_k"], m_d3) if mch == 0 else ())
                        T.ve(lambda e, m=mch, b=bank, n=TTi: e.tensor_scalar_mul(
                            fmS(kk_f, m, 0, n), b[:, 0:n],
                            sap(vec["k_k"], 0, 128, m, [[1, 1]])), deps=dep)
                        return T.ve(lambda e, m=mch, b=bank, n=TTi: e.tensor_mul(
                            fmS(g_f, m, 0, n), b[:, 0:n],
                            fmS(scr_b, m, 0, n, width=NW1)))
                    m_kev = proj("W_k", m_lrp, evac_k, extra=(m_ah,))
                    m_k = m_kev[-1]

                    m_kb2 = T.ve(lambda e, n=TTi: e.tensor_mul(
                        fm3(g_f, 0, n), fm3(g_f, 0, n), fm3(sp_f, 0, n)),
                        deps=(m_k, m_d3))
                    m_bkF = T.se(lambda e, n=TTi: e.copy(
                        fm3(bk_b, 0, n, base=FCH * TT), fm3(g_f, 0, n)), deps=(m_kb2,))
                    m_kh = T.ve(lambda e, nj=ncht: e.tensor_mul(
                        sap(kh_b, 0, 128, 0, [[TT, FCH], [L, nj], [1, L]]),
                        sap(g_f, 0, 128, 0, [[TT, FCH], [L, nj], [1, L]]),
                        sap(d1_f, 0, 128, L - 1, [[TT, FCH], [L, nj], [0, L]])),
                        deps=(m_kb2,))


                if KSTAGE >= 16:
                    m_ksq = T.se(lambda e, n=TTi: e.activation(fm3(scr_b, 0, n, width=NW1),
                                                               fm3(kk_f, 0, n), AF.Square),
                                 deps=(m_k, m_kh))
                    m_nrm = None
                    for ch in range(FCH):
                        mm = T.te(lambda e, ch=ch, n=TTi: e.matmul(
                            ps[6][0:2, 0:n],
                            sap(k_onespat, 0, 128, 0, [[1, 2]]),
                            fmS(scr_b, ch, 0, n, width=NW1),
                            start=True, stop=True),
                            deps=(m_ksq,) if ch == 0 else ((m_nrm,) if m_nrm else ()))
                        m_nrm = T.se(lambda e, ch=ch, n=TTi: e.activation(
                            sap(nrm_b, 0, 2, ch * TT, [[1, n]]),
                            ps[6][0:2, 0:n], AF.Sqrt), deps=(mm,))
                    T.ve(lambda e, n=TTi: e.tensor_scalar_max(
                        sap(nrm_b, 0, 2, 0, [[TT, FCH], [1, n]]),
                        sap(nrm_b, 0, 2, 0, [[TT, FCH], [1, n]]), 1e-12), deps=(m_nrm,))
                    T.ve(lambda e, n=TTi: e.reciprocal(
                        sap(nrm_b, 0, 2, 0, [[TT, FCH], [1, n]]),
                        sap(nrm_b, 0, 2, 0, [[TT, FCH], [1, n]])))
                    m_ninv = T.ve(lambda e, n=TTi: e.tensor_scalar_mul(
                        sap(nrm_b, 0, 2, 0, [[TT, FCH], [1, n]]),
                        sap(nrm_b, 0, 2, 0, [[TT, FCH], [1, n]]), -1.0))
                    mkk = None
                    for ch in range(FCH):
                        dep = (m_ninv,) if ch == 0 else (mkk,)
                        mmb = T.te(lambda e, ch=ch, n=TTi: e.matmul(
                            ps[7][:, 0:n], k_sel2[:],
                            sap(nrm_b, 0, 2, ch * TT, [[1, n]]),
                            start=True, stop=True), deps=dep)
                        mkk = T.ve(lambda e, ch=ch, n=TTi: e.tensor_mul(
                            fmS(kk_f, ch, 0, n), fmS(kk_f, ch, 0, n), ps[7][:, 0:n]),
                            deps=(mmb,))
                    m_kkneg = mkk

                    FT = FCH * TT
                    m_ab1 = T.ve(lambda e, nj=ncht: e.tensor_mul(
                        sap(ar_b, 0, 128, 1, [[TT, FCH], [L, nj], [1, L - 1]]),
                        sap(kk_f, 0, 128, 1, [[TT, FCH], [L, nj], [1, L - 1]]),
                        sap(d1_f, 0, 128, 0, [[TT, FCH], [L, nj], [1, L - 1]])),
                        deps=(m_kkneg,))
                    m_ab2 = T.ve(lambda e, nj=ncht: e.tensor_copy(
                        sap(ar_b, 0, 128, 0, [[TT, FCH], [L, nj], [1, 1]]),
                        sap(kk_f, 0, 128, 0, [[TT, FCH], [L, nj], [1, 1]])))
                    m_bb1 = T.ve(lambda e, n=TTi: e.tensor_mul(
                        fm3(g_f, 0, n), fm3(kk_f, 0, n), fm3(ah_f, 0, n)),
                        deps=(m_kkneg, m_bkF))
                    m_bb2 = T.ve(lambda e, n=TTi: e.scalar_tensor_tensor(
                        fm3(g_f, 0, n), fm3(g_f, 0, n), -1.0,
                        fm3(sp_f, 0, n), ALU.mult, ALU.mult))
                    m_bk1 = T.se(lambda e, n=TTi: e.copy(
                        fm3(bk_b, 0, n), fm3(g_f, 0, n)), deps=(m_bb2,))
                    m_bh = T.ve(lambda e, nj=ncht: e.tensor_mul(
                        sap(bh_b, 0, 128, 0, [[TT, FCH], [L, nj], [1, L]]),
                        sap(g_f, 0, 128, 0, [[TT, FCH], [L, nj], [1, L]]),
                        sap(d1_f, 0, 128, L - 1, [[TT, FCH], [L, nj], [0, L]])),
                        deps=(m_bb2,))


                if KSTAGE >= 17:
                    m_lrp = lerp("x_r", extra=(m_bh,))

                    r_se = []

                    def evac_r(mch, bank, dep0):
                        dep = dep0 + ((m_bh, m_bk1, vdeps["r_k"]) if mch == 0 else ())
                        T.ve(lambda e, m=mch, b=bank, n=TTi: e.tensor_mul(
                            sap(ar_b, 0, 128, FT + m * TT, [[1, n]]), b[:, 0:n],
                            fmS(d1_f, m, 0, n)), deps=dep)
                        mvst = T.ve(lambda e, m=mch, b=bank, n=TTi: e.scalar_tensor_tensor(
                            fmS(g_f, m, 0, n), b[:, 0:n],
                            sap(vec["r_k"], 0, 128, m, [[1, 1]]),
                            sap(bk_b, 0, 128, FT + m * TT, [[1, n]]),
                            ALU.mult, ALU.mult))
                        r_se.append(T.se(lambda e, m=mch, b=bank, n=TTi: e.copy(
                            fmS(r_b, m, 0, n), b[:, 0:n]), deps=(mvst,)))
                        return mvst
                    m_rev = proj("W_r", m_lrp, evac_r, extra=(m_kkneg, m_nrm))
                    m_r = m_rev[-1]

                    if KSTAGE >= 18:
                        mmk = None
                        for kch in range(FCH):
                            dep = (m_r, r_se[-1], d_gla) if kch == 0 else ()
                            mmk = T.te(lambda e, k=kch, n=TTi: e.matmul(
                                ps[4][:, 0:n], sap(w_gla, 0, 128, k * 128, [[1, 128]]),
                                fmS(r_b, k, 0, n), start=(k == 0), stop=(k == FCH - 1)),
                                deps=dep)
                        m_gmid = T.se(lambda e, n=TTi: e.activation(mid_b[:, 0:n], ps[4][:, 0:n],
                                                                    AF.Sigmoid), deps=(mmk,))
                        ev_g = None
                        evl = []
                        for mch in range(FCH):
                            bank = ps[4 + mch % 2]
                            dep = (m_gmid, d_glb) if mch == 0 else ()
                            if mch >= 2:
                                dep = dep + (evl[mch - 2],)
                            mmk = T.te(lambda e, m=mch, b=bank, n=TTi: e.matmul(
                                b[:, 0:n], sap(w_glb, 0, 128, m * 128, [[1, 128]]),
                                mid_b[:, 0:n], start=True, stop=True), deps=dep)
                            ev_g = T.se(lambda e, m=mch, b=bank, n=TTi: e.activation(
                                fmS(gate_b, m, 0, n), b[:, 0:n], AF.Identity,
                                bias=sap(vec["g_lora_bias"], 0, 128, m, [[1, 1]])), deps=(mmk,))
                            evl.append(ev_g)
                        m_gate = ev_g


                if KSTAGE >= 18 and KSEL & 2:
                    m_b1 = T.ve(lambda e, n=TTi: e.tensor_mul(
                        fm3(g_f, 0, n), fm3(g_f, 0, n), fm3(d1_f, 0, n)),
                        deps=(m_r,))
                    m_bcast = T.se(lambda e, n=TTi: e.activation(
                        fm3(scr_b, 0, n, width=NW1), fm3(g_f, 0, n), AF.Identity),
                        deps=(m_b1, m_gate))
                    m_bs = None
                    if KSEL & 4:
                        for ch in range(FCH):
                            mm = T.te(lambda e, ch=ch, n=TTi: e.matmul(
                                ps[6][0:2, 0:n], sap(k_onespat, 0, 128, 0, [[1, 2]]),
                                fmS(scr_b, ch, 0, n, width=NW1), start=True, stop=True),
                                deps=(m_bcast,) if ch == 0 else ((m_bs,) if m_bs else ()))
                            m_bs = T.se(lambda e, ch=ch, n=TTi: e.copy(
                                sap(nrm_b, 0, 2, ch * TT, [[1, n]]), ps[6][0:2, 0:n]),
                                deps=(mm,))
                    m_bse = None
                    if KSEL & 8:
                        for ch in range(FCH):
                            dep = (m_bs,) if ch == 0 else ()
                            mmb = T.te(lambda e, ch=ch, n=TTi: e.matmul(
                                ps[7][:, 0:n], k_sel2[:],
                                sap(nrm_b, 0, 2, ch * TT, [[1, n]]),
                                start=True, stop=True),
                                deps=dep + ((m_bse,) if m_bse else ()))
                            m_bse = T.se(lambda e, ch=ch, n=TTi: e.copy(
                                fmS(scr_b, ch, 0, n, width=NW1), ps[7][:, 0:n]),
                                deps=(mmb,))
                    if m_bse is None:
                        m_bse = m_bcast

                    if KSTAGE >= 19:
                        m_lrp = lerp("x_v", extra=(m_b1,))

                        v_se = []

                        def evac_v(mch, bank, dep0):
                            dep = dep0 + ((m_bse, m_gate) if mch == 0 else ())
                            mvb = T.ve(lambda e, m=mch, b=bank, n=TTi: e.tensor_mul(
                                fmS(bonus_b, m, 0, n), b[:, 0:n],
                                fmS(scr_b, m, 0, n, width=NW1)), deps=dep)
                            v_se.append(T.se(lambda e, m=mch, b=bank, n=TTi: e.copy(
                                fmS(v_b, m, 0, n), b[:, 0:n]), deps=(mvb,)))
                            return mvb
                        m_vev = proj("W_v", m_lrp, evac_v, extra=(m_bse,))
                        m_v = m_vev[-1]
                        m_bonus = m_v
                        m_vars = (m_ab1, m_ab2, m_bh, m_kh, m_bkF, m_bk1,
                                  m_clc, m_gc, m_v, m_r, m_gate,
                                  r_se[-1], v_se[-1])

                if not (KSEL & 1):
                    m_gate = m_ln
                if not (KSEL & 2):
                    m_bse = m_b1 = m_ln
                prev_ev = (m_bonus,) if KSTAGE >= 19 else (m_ln,)
                prev_xb = prev_ev[0]
                if KSTAGE < 19:
                    m_vars = (m_ln,)
                for j in range(ncht if KSTAGE >= 20 else 0):
                    cglob = scan_t0 // L + j
                    band = 32 * (cglob % 4)
                    colc = j * L

                    def hsl(buf, hh, col, n, base=0):
                        return sap(buf, (hh % 2) * 64, 64,
                                   base + (hh // 2) * TT + col, [[1, n]])

                    ev = None
                    for t6, src in enumerate((v_b, bh_b, kh_b)):
                        m_s0 = None
                        for p in range(FCH):
                            bnk = ps[6 + (0 if p < 4 else 1)]
                            dep = ()
                            if m_s0 is None:
                                dep = (ev,) if ev else (m_vars + (prev_xb,))
                            m_s0 = T.te(lambda e, p=p, b=bnk, s=src, bd=band,
                                        cc=colc: e.matmul(
                                b[bd:bd + 32, (p % 4) * 128:(p % 4) * 128 + 128],
                                sap(s, 0, 128, p * TT + cc, [[1, L]]), k_id[:],
                                start=True, stop=True, tile_position=(0, bd)),
                                deps=dep)
                        T.se(lambda e, t6=t6, bd=band: e.copy(
                            sap(stm, bd, 32, t6 * 1024, [[1, 512]]),
                            ps[6][bd:bd + 32, :]), deps=(m_s0,))
                        ev = T.se(lambda e, t6=t6, bd=band: e.copy(
                            sap(stm, bd, 32, t6 * 1024 + 512, [[1, 512]]),
                            ps[7][bd:bd + 32, :]))
                    m_s1 = ev

                    if KCH >= 2:
                        m_s2 = None
                        for hh in range(16):
                            bnk = ps[0] if hh < 8 else ps[1]
                            dep = ((m_s1,) + prev_ev) if hh == 0 else ()
                            m_s2 = T.te(lambda e, hh=hh, b=bnk, bd=band, cc=colc: e.matmul(
                                b[bd:bd + 32, (hh % 8) * 64:(hh % 8) * 64 + 64],
                                hsl(bk_b, hh, cc, L),
                                sap(ar_b, (hh % 2) * 64, 64, (hh // 2) * TT + cc,
                                    [[FCH * TT, 2], [1, L]]),
                                start=True, stop=True,
                                tile_position=((hh % 2) * 64, bd)), deps=dep)
                        for hh in range(16):
                            bnk = ps[2] if hh < 8 else ps[3]
                            m_s2 = T.te(lambda e, hh=hh, b=bnk, bd=band, cc=colc: e.matmul(
                                b[bd:bd + 32, (hh % 8) * 64:(hh % 8) * 64 + 64],
                                hsl(bk_b, hh, cc, L, base=FCH * TT),
                                sap(ar_b, (hh % 2) * 64, 64, (hh // 2) * TT + cc,
                                    [[FCH * TT, 2], [1, L]]),
                                start=True, stop=True,
                                tile_position=((hh % 2) * 64, bd)))
                        for hh in range(16):
                            m_s2 = T.te(lambda e, hh=hh, bd=band, cc=colc: e.matmul(
                                ps[4][bd:bd + 32, hh * 32:hh * 32 + 32],
                                hsl(ar_b, hh, cc, L), hsl(bk_b, hh, cc, L),
                                start=True, stop=True,
                                tile_position=((hh % 2) * 64, bd)))
                    else:
                        m_s2 = m_s1

                    if KCH >= 3:
                        T.ve(lambda e, bd=band: e.tensor_mul(
                            sap(strb, bd, 32, 0, [[1, 512]]), ps[0][bd:bd + 32, :],
                            k_mtrb[bd:bd + 32, :]), deps=(m_s2,))
                        T.ve(lambda e, bd=band: e.tensor_mul(
                            sap(strb, bd, 32, 512, [[1, 512]]), ps[1][bd:bd + 32, :],
                            k_mtrb[bd:bd + 32, :]))
                        T.ve(lambda e, bd=band: e.tensor_mul(
                            sap(smrk, bd, 32, 0, [[1, 512]]), ps[2][bd:bd + 32, :],
                            k_mtrb[bd:bd + 32, :]))
                        T.ve(lambda e, bd=band: e.tensor_mul(
                            sap(smrk, bd, 32, 512, [[1, 512]]), ps[3][bd:bd + 32, :],
                            k_mtrb[bd:bd + 32, :]))
                        m_s3 = T.ve(lambda e, bd=band: e.tensor_mul(
                            sap(sw0, bd, 32, 0, [[1, 512]]), ps[4][bd:bd + 32, :],
                            k_mw0[bd:bd + 32, :]))

                        def ztile(k, hh, n=32):
                            if k == 0:
                                return sap(strb, band, 32, hh * 64, [[1, n]])
                            return sap(ssq, band, 32, (k - 1) * 512 + hh * 32, [[1, n]])

                        def wtile(k, hh, n=32):
                            if k == 0:
                                return sap(sw0, band, 32, hh * 32, [[1, n]])
                            return sap(ssq, band, 32, (3 + k) * 512 + hh * 32, [[1, n]])

                    else:
                        m_s3 = m_s1

                    if KCH >= 5:
                        m_g1 = None
                        for hh in range(16):
                            bnk = ps[2] if hh < 8 else ps[3]
                            dep = (m_s3, prev_zb) if hh == 0 else ()
                            m_g1 = T.te(lambda e, hh=hh, b=bnk, bd=band, cc=colc: e.matmul(
                                b[bd:bd + 32, (hh % 8) * 64:(hh % 8) * 64 + 64],
                                hsl(ar_b, hh, cc, L),
                                sap(z_bf, (hh % 2) * 64, 64, (hh // 2) * 64, [[1, 64]]),
                                start=True, stop=True,
                                tile_position=((hh % 2) * 64, bd)), deps=dep)
                        T.se(lambda e, bd=band: e.copy(
                            sap(sxp_f[0], bd, 32, 0, [[1, 512]]),
                            ps[2][bd:bd + 32, :]), deps=(m_g1,))
                        m_g1e = T.se(lambda e, bd=band: e.copy(
                            sap(sxp_f[0], bd, 32, 512, [[1, 512]]), ps[3][bd:bd + 32, :]))

                    if KCH >= 4:
                        m_prev = m_s3
                        for lv in range(4):
                            m_mm = None
                            for hh in range(16):
                                dep = (m_prev,) if hh == 0 else ()
                                m_mm = T.te(lambda e, hh=hh, bd=band, wt=wtile(lv, hh),
                                            zt=ztile(lv, hh): e.matmul(
                                    ps[5][bd:bd + 32, hh * 32:hh * 32 + 32],
                                    wt, zt,
                                    start=True, stop=True, tile_position=(bd, bd)),
                                    deps=dep)
                            if lv < 3:
                                for hh in range(16):
                                    m_mm = T.te(lambda e, hh=hh, bd=band, wt=wtile(lv, hh),
                                                zt=ztile(lv, hh): e.matmul(
                                        ps[6][bd:bd + 32, hh * 32:hh * 32 + 32],
                                        zt, wt,
                                        start=True, stop=True, tile_position=(bd, bd)))
                            e1 = T.se(lambda e, lv=lv, bd=band: e.copy(
                                sap(ssq, bd, 32, lv * 512, [[1, 512]]),
                                ps[5][bd:bd + 32, :]), deps=(m_mm,))
                            if lv < 3:
                                m_prev = T.se(lambda e, lv=lv, bd=band: e.copy(
                                    sap(ssq, bd, 32, (4 + lv) * 512, [[1, 512]]),
                                    ps[6][bd:bd + 32, :]))
                            else:
                                m_prev = e1
                        m_s4 = m_prev

                    else:
                        m_s4 = m_s3

                    if (KCH >= 10) and not is_warm:
                        m_oa = None
                        for hh in range(16):
                            bnk = ps[2] if hh < 8 else ps[3]
                            m_oa = T.te(lambda e, hh=hh, b=bnk, bd=band, cc=colc:
                                        e.matmul(
                                b[bd:bd + 32, (hh % 8) * 64:(hh % 8) * 64 + 64],
                                hsl(ar_b, hh, cc, L, base=FCH * TT),
                                sap(z_bf, (hh % 2) * 64, 64, (hh // 2) * 64, [[1, 64]]),
                                start=True, stop=True,
                                tile_position=((hh % 2) * 64, bd)),
                                deps=(m_g1e, prev_zb) + prev_ev if hh == 0 else ())
                        tg0 = (j * L) // 128
                        T.se(lambda e, tg=tg0, bd=band: e.copy(
                            sap(o_tm, bd, 32, tg * C, [[1, 512]]),
                            ps[2][bd:bd + 32, :]), deps=(m_oa,))
                        m_oae = T.se(lambda e, tg=tg0, bd=band: e.copy(
                            sap(o_tm, bd, 32, tg * C + 512, [[1, 512]]),
                            ps[3][bd:bd + 32, :]))

                    if KCH >= 5:
                        pass
                    else:
                        m_g1e = m_s4

                    if KCH >= 6:
                        m_mv = None
                        for hh in range(16):
                            bnk = ps[6] if hh < 8 else ps[7]
                            dep = (m_g1e, m_s4) if hh == 0 else ()
                            m_mv = T.te(lambda e, hh=hh, b=bnk, bd=band: e.matmul(
                                b[bd:bd + 32, (hh % 8) * 64:(hh % 8) * 64 + 64],
                                sap(smrk, bd, 32, hh * 64, [[1, 32]]),
                                sap(stm, bd, 32, (hh // 2) * 128 + (hh % 2) * 64, [[1, 64]]),
                                start=True, stop=True, tile_position=(bd, bd)), deps=dep)
                    else:
                        m_mv = m_g1e

                    if KCH >= 7:
                        T.ve(lambda e, bd=band: e.tensor_add(
                            sap(sxp_f[0], bd, 32, 0, [[1, 512]]),
                            sap(sxp_f[0], bd, 32, 0, [[1, 512]]),
                            ps[6][bd:bd + 32, :]), deps=(m_mv, m_g1e))
                        m_x = T.ve(lambda e, bd=band: e.tensor_add(
                            sap(sxp_f[0], bd, 32, 512, [[1, 512]]),
                            sap(sxp_f[0], bd, 32, 512, [[1, 512]]),
                            ps[7][bd:bd + 32, :]))
                        m_xb = T.se(lambda e, bd=band: e.copy(
                            sap(sxp_b, bd, 32, 0, [[1, 1024]]),
                            sap(sxp_f[0], bd, 32, 0, [[1, 1024]])), deps=(m_x,))
                    else:
                        m_xb = m_x = m_mv

                    if KCH >= 8:
                        for lv in range(5):
                            srcf = sxp_f[lv % 2]
                            dstf = sxp_f[(lv + 1) % 2]
                            m_mm = None
                            for hh in range(16):
                                bnk = ps[6] if hh < 8 else ps[7]
                                dep = ((m_xb, m_s4) if lv == 0 else (m_xb,)) if hh == 0 else ()
                                m_mm = T.te(lambda e, hh=hh, b=bnk, bd=band,
                                            zt=ztile(lv, hh): e.matmul(
                                    b[bd:bd + 32, (hh % 8) * 64:(hh % 8) * 64 + 64],
                                    zt, sap(sxp_b, bd, 32, hh * 64, [[1, 64]]),
                                    start=True, stop=True, tile_position=(bd, bd)),
                                    deps=dep)
                            T.ve(lambda e, d=dstf, s=srcf, bd=band: e.tensor_add(
                                sap(d, bd, 32, 0, [[1, 512]]), ps[6][bd:bd + 32, :],
                                sap(s, bd, 32, 0, [[1, 512]])), deps=(m_mm,))
                            m_x = T.ve(lambda e, d=dstf, s=srcf, bd=band: e.tensor_add(
                                sap(d, bd, 32, 512, [[1, 512]]), ps[7][bd:bd + 32, :],
                                sap(s, bd, 32, 512, [[1, 512]])))
                            m_xb = T.se(lambda e, d=dstf, bd=band: e.copy(
                                sap(sxp_b, bd, 32, 0, [[1, 1024]]),
                                sap(d, bd, 32, 0, [[1, 1024]])), deps=(m_x,))
                        sP = sxp_b
                        m_p = m_xb
                        prev_xb = m_xb

                    else:
                        sP = sxp_b
                        m_p = m_xb
                        prev_xb = m_xb

                    if KCH >= 9:
                        m_f = None
                        for hh in range(16):
                            dep = (m_p,) if hh == 0 else ()
                            T.te(lambda e, hh=hh, s=sP, bd=band: e.matmul(
                                ps[0][(hh % 2) * 64:(hh % 2) * 64 + 64,
                                      (hh // 2) * 64:(hh // 2) * 64 + 64],
                                sap(stm, bd, 32, 1024 + (hh // 2) * 128 + (hh % 2) * 64,
                                    [[1, 64]]),
                                sap(s, bd, 32, hh * 64, [[1, 64]]),
                                start=True, stop=False,
                                tile_position=(bd, (hh % 2) * 64)), deps=dep)
                            m_f = T.te(lambda e, hh=hh, bd=band: e.matmul(
                                ps[0][(hh % 2) * 64:(hh % 2) * 64 + 64,
                                      (hh // 2) * 64:(hh // 2) * 64 + 64],
                                sap(stm, bd, 32, 2048 + (hh // 2) * 128 + (hh % 2) * 64,
                                    [[1, 64]]),
                                sap(stm, bd, 32, (hh // 2) * 128 + (hh % 2) * 64, [[1, 64]]),
                                start=False, stop=True,
                                tile_position=(bd, (hh % 2) * 64)))

                    else:
                        m_f = m_p

                    if KCH >= 10:
                        if not is_warm:
                            m_ob = None
                            for hh in range(16):
                                bnk = ps[4] if hh < 8 else ps[5]
                                dep = (m_p,) if hh == 0 else ()
                                T.te(lambda e, hh=hh, b=bnk, s=sP, bd=band: e.matmul(
                                    b[bd:bd + 32, (hh % 8) * 64:(hh % 8) * 64 + 64],
                                    sap(strb, bd, 32, hh * 64 + 32, [[1, 32]]),
                                    sap(s, bd, 32, hh * 64, [[1, 64]]),
                                    start=True, stop=False, tile_position=(bd, bd)),
                                    deps=dep)
                                m_ob = T.te(lambda e, hh=hh, b=bnk, bd=band: e.matmul(
                                    b[bd:bd + 32, (hh % 8) * 64:(hh % 8) * 64 + 64],
                                    sap(smrk, bd, 32, hh * 64 + 32, [[1, 32]]),
                                    sap(stm, bd, 32, (hh // 2) * 128 + (hh % 2) * 64,
                                        [[1, 64]]),
                                    start=False, stop=True, tile_position=(bd, bd)))
                            tg = (j * L) // 128
                            T.ve(lambda e, tg=tg, bd=band: e.tensor_add(
                                sap(o_tm, bd, 32, tg * C, [[1, 512]]),
                                sap(o_tm, bd, 32, tg * C, [[1, 512]]),
                                ps[4][bd:bd + 32, :]),
                                deps=(m_oae, m_ob))
                            m_oe = T.ve(lambda e, tg=tg, bd=band: e.tensor_add(
                                sap(o_tm, bd, 32, tg * C + 512, [[1, 512]]),
                                sap(o_tm, bd, 32, tg * C + 512, [[1, 512]]),
                                ps[5][bd:bd + 32, :]))
                        else:
                            m_oe = m_s3

                    else:
                        m_oe = m_s3

                    if KCH >= 11:
                        mz = None
                        for p in range(FCH):
                            dep = (m_f, m_clc) if p == 0 else ()
                            mz = T.ve(lambda e, p=p, j=j: e.scalar_tensor_tensor(
                                sap(z_f32, 0, 128, p * 64, [[1, 64]]),
                                sap(z_f32, 0, 128, p * 64, [[1, 64]]),
                                sap(clc_f, 0, 128, p * 8 + j, [[1, 1]]),
                                ps[0][:, p * 64:p * 64 + 64], ALU.mult, ALU.add), deps=dep)
                        prev_zb = T.ve(lambda e: e.tensor_copy(z_bf[:], z_f32[:]), deps=(mz,))
                        prev_ev = (prev_zb, m_oe)
                    else:
                        mz = m_f
                        prev_zb = prev_zb

                    if (not is_warm) and (j % 4 == 3):
                        tg = (j * L) // 128
                        m1 = T.ve(lambda e, tg=tg: e.reduce_sum(
                            sap(gnst, 0, 128, 0, [[1, 16]]),
                            sap(o_tm, 0, 128, tg * C, [[64, 16], [1, 64]]),
                            axis=AX.X), deps=prev_ev)
                        T.ve(lambda e: e.tensor_scalar_mul(
                            sap(gnst, 0, 128, 0, [[1, 16]]),
                            sap(gnst, 0, 128, 0, [[1, 16]]), 1.0 / 64))
                        m_sqg = T.se(lambda e, tg=tg: e.activation(
                            sap(scr_b, 0, 128, 0, [[1, C]]),
                            sap(o_tm, 0, 128, tg * C, [[1, C]]), AF.Square),
                            deps=(m_oe,))
                        T.ve(lambda e: e.reduce_sum(
                            sap(gnst, 0, 128, 16, [[1, 16]]),
                            sap(scr_b, 0, 128, 0, [[64, 16], [1, 64]]), axis=AX.X),
                            deps=(m_sqg,))
                        T.ve(lambda e: e.tensor_mul(sap(gnst, 0, 128, 32, [[1, 16]]),
                                                    sap(gnst, 0, 128, 0, [[1, 16]]),
                                                    sap(gnst, 0, 128, 0, [[1, 16]])))
                        m_vr = T.ve(lambda e: e.scalar_tensor_tensor(
                            sap(gnst, 0, 128, 48, [[1, 16]]),
                            sap(gnst, 0, 128, 16, [[1, 16]]), 1.0 / 64,
                            sap(gnst, 0, 128, 32, [[1, 16]]), ALU.mult, ALU.subtract))
                        m_sd2 = T.se(lambda e: e.activation(
                            sap(gnst, 0, 128, 64, [[1, 16]]),
                            sap(gnst, 0, 128, 48, [[1, 16]]), AF.Sqrt, bias=EPS),
                            deps=(m_vr,))
                        m_rs2 = T.ve(lambda e: e.reciprocal(
                            sap(gnst, 0, 128, 80, [[1, 16]]),
                            sap(gnst, 0, 128, 64, [[1, 16]])), deps=(m_sd2,))
                        T.ve(lambda e, tg=tg: e.tensor_sub(
                            sap(o_tm, 0, 128, tg * C, [[64, 16], [1, 64]]),
                            sap(o_tm, 0, 128, tg * C, [[64, 16], [1, 64]]),
                            sap(gnst, 0, 128, 0, [[1, 16], [0, 64]])), deps=(m_rs2,))
                        T.ve(lambda e, tg=tg: e.tensor_mul(
                            sap(o_tm, 0, 128, tg * C, [[64, 16], [1, 64]]),
                            sap(o_tm, 0, 128, tg * C, [[64, 16], [1, 64]]),
                            sap(gnst, 0, 128, 80, [[1, 16], [0, 64]])))
                        T.ve(lambda e, tg=tg: e.tensor_mul(
                            sap(o_tm, 0, 128, tg * C, [[1, C]]),
                            sap(o_tm, 0, 128, tg * C, [[1, C]]), gw_tm[:]),
                            deps=(m_gnbc,))
                        m_gn = T.ve(lambda e, tg=tg: e.scalar_tensor_tensor(
                            sap(scr_b, 0, 128, 0, [[1, C]]),
                            sap(o_tm, 0, 128, tg * C, [[1, C]]), 1.0, gb_tm[:],
                            ALU.mult, ALU.add))
                        m_tp = None
                        for ch in range(FCH):
                            dept = (m_gn,)
                            if ch == 0 and prev_gn_add is not None:
                                dept = dept + (prev_gn_add,)
                            m_tp = T.dma_sc(lambda e, ch=ch, tg=tg: e.dma_start(
                                sap(scr_b, 0, 128, 1024 + ch * 128, [[1, 128]]),
                                sap(scr_b, 0, 128, ch * 128, [[1, 128]]),
                                transpose=True), deps=dept)
                        real0 = scan_t0 - WARM + tg * 128
                        T.ve(lambda e, tg=tg, r0=real0: e.tensor_add(
                            sap(wo_in, 0, 128, r0, [[SEG, FCH], [1, 128]]),
                            sap(scr_b, 0, 128, 1024, [[128, FCH], [1, 128]]),
                            sap(bonus_b, 0, 128, tg * 128, [[TT, FCH], [1, 128]])),
                            deps=(m_tp, m_bonus))
                        prev_gn_add = T.ve(lambda e, tg=tg, r0=real0: e.tensor_mul(
                            sap(wo_in, 0, 128, r0, [[SEG, FCH], [1, 128]]),
                            sap(wo_in, 0, 128, r0, [[SEG, FCH], [1, 128]]),
                            sap(gate_b, 0, 128, tg * 128, [[TT, FCH], [1, 128]])),
                            deps=(m_gate,))

                if DBG and ti == 1:
                    T.barrier()
                    for nm_, buf_ in (("ln", ln_f), ("r_b", r_b),
                                      ("v_b", v_b), ("ah_f", ah_f), ("sp_f", sp_f),
                                      ("kk_f", kk_f),
                                      ("gate_b", gate_b), ("bonus_b", bonus_b),
                                      ("d1_f", d1_f),
                                      ("bk_b", bk_b), ("ar_b", ar_b), ("bh_b", bh_b),
                                      ("kh_b", kh_b),
                                      ("o_tm", o_tm), ("strb", strb),
                                      ("stm", stm), ("sw0", sw0)):
                        t_ = dbg_t("t1_" + nm_, tuple(buf_.shape),
                                   F32 if buf_.dtype == F32 else F32)
                        if buf_.dtype == F32:
                            T.dma_sy(lambda e, t_=t_, b_=buf_: e.dma_start(t_.ap(), b_[:]),
                                     deps=(prev_zb, prev_ev[0]))
                        else:
                            T.dma_ge(lambda e, t_=t_, b_=buf_: e.dma_start(t_.ap(), b_[:]),
                                     deps=(prev_zb, prev_ev[0]))
                T.barrier()
                scan_t0 += TTi
            if DBG:
                t = dbg_t("wo_in", (128, FCH * SEG))
                T.dma_ge(lambda e: e.dma_start(t.ap(), wo_in[:]), deps=(prev_zb,))
                t2 = dbg_t("z", (128, FCH * 64))
                T.dma_sy(lambda e: e.dma_start(t2.ap(), z_f32[:]), deps=(prev_zb,))
            T.barrier()

        if KSKIP2:
            dmy = []
            for m in range(FCH):
                dmy.append(T.dma_out(lambda e, m=m: e.dma_start(
                    AP(out_fm, m * 128 * SEG, [[SEG, 128], [1, SEG]]),
                    sap(xf, 0, 128, 0, [[1, SEG]]))))
            T._rec("ge", lambda e: e.nop(), None, 0,
                   (("outQ", T.semn["outQ"]),))
            with nc.allow_low_precision(reason="debug"):
                with nc.Block() as block:
                    T.finalize(block)
            return nc
        p2 = contextlib.ExitStack()
        with p2:
            def sb2(name, shape, dt):
                return p2.enter_context(nc.sbuf_tensor(name, list(shape), dt))

            w_big = sb2("w_big", (128, FCH * C), BF16)
            xf_t = sb2("xf_t", (128, FCH * 512), F32)
            w_2 = sb2("w_2", (128, 4 * SLOTS), BF16)
            w_gs = sb2("w_gs", (128, FCH * 512), BF16)
            s_slots = sb2("s_slots", (SLOTS, C), F32)
            s_slotsb = sb2("s_slotsb", (SLOTS, C), BF16)
            s_alpha = sb2("s_alpha", (128, 16 * SLOTS), BF16)
            s_alpsm = sb2("s_alpsm", (SLOTS, SEG), BF16)
            s_si = sb2("s_si", (SLOTS, C + 16), F32)
            s_lgt = sb2("s_lgt", (128, SLOTS), F32)
            s_m8 = sb2("s_m8", (128, 12), F32)
            s_rep = sb2("s_rep", (128, SLOTS), F32)
            s_msk = sb2("s_msk", (128, SLOTS), F32)
            s_el = sb2("s_el", (128, SLOTS), F32)
            s_meanb = sb2("s_meanb", (1, C), BF16)
            s_fm64 = sb2("s_fm64", (128, FCH * 64), BF16)
            s_fm64b = sb2("s_fm64b", (128, FCH * 64), BF16)
            s_tau = sb2("s_tau", (1, 4), F32)
            s_taub = sb2("s_taub", (128, 1), F32)
            s_b2bc = sb2("s_b2bc", (128, SLOTS), F32)
            s_ofin = sb2("s_ofin", (128, 2 * 512), F32)
            biash = sb2("biash", (128, 4), F32)
            y_tm = wo_in

            p2t = contextlib.ExitStack()
            with p2t:
                def sb2t(name, shape, dt):
                    return p2t.enter_context(nc.sbuf_tensor(name, list(shape), dt))

                y_fm = sb2t("y_fm", (128, FCH * SEG), BF16)
                x2t_f = sb2t("x2t_f", (128, FCH * 512), F32)
                w_1a = sb2t("w_1a", (128, FCH * HFF), BF16)
                w_1b = sb2t("w_1b", (128, FCH * HFF), BF16)
                s_hfm = sb2t("s_hfm", (128, 4 * SEG), BF16)
                row2_f = sb2t("row2_f", (1, 4 * 512), F32)
                row2_c0 = sb2t("row2_c0", (1, 512), BF16)
                row2_c1 = sb2t("row2_c1", (1, 512), BF16)
                cb2 = sb2t("cb2", (128, 2 * 512), BF16)

                wo_ld = [None, None]
                wo_cs = [None, None]
                for kch in range(FCH):
                    h = kch % 2
                    wo_ld[h] = T.dma_sy(lambda e, k=kch, h=h: e.dma_start(
                        sap(xf_t, 0, 128, h * 1024, [[1, C]]),
                        dap("W_o", k * 128 * C, [[C, 128], [1, C]])),
                        deps=(wo_cs[h],) if wo_cs[h] else (), sem=("wA", "wB")[h])
                    wo_cs[h] = T.se(lambda e, k=kch, h=h: e.copy(
                        sap(w_big, 0, 128, k * C, [[1, C]]),
                        sap(xf_t, 0, 128, h * 1024, [[1, C]])),
                        deps=(wo_ld[h],))
                d_WO = wo_cs[1]
                d_WO2 = wo_cs[0]

                m_yg = []
                prev_stat2 = None
                prev_x2h = None
                for grp in range(4):
                    m_xf = T.dma_sy(lambda e, g=grp: e.dma_start(
                        sap(xf_t, 0, 128, 0, [[512, FCH], [1, 512]]),
                        dap("xT", WARM + 1 + g * 512,
                            [[TS + 1, 128], [128 * (TS + 1), FCH], [1, 512]])),
                        deps=tuple(m_yg[-1:]) + ((d_WO, d_WO2) if grp == 0 else ()) +
                        ((prev_stat2,) if prev_stat2 else ()))
                    mmk = None
                    for kch in range(FCH):
                        for mch in range(FCH):
                            bank = ps[mch]
                            dep = (d_WO, d_WO2) if (kch == 0 and mch == 0) else ()
                            if kch == 0 and mch == 0 and grp > 0:
                                dep = dep + tuple(m_yg[-1:])
                            mmk = T.te(lambda e, m=mch, k=kch, b=bank, g=grp: e.matmul(
                                b[:, :], sap(w_big, 0, 128, k * C + m * 128, [[1, 128]]),
                                sap(wo_in, 0, 128, k * SEG + g * 512, [[1, 512]]),
                                start=(k == 0), stop=(k == FCH - 1)), deps=dep)
                    m_x2 = None
                    for mch in range(FCH):
                        m_x2 = T.ve(lambda e, m=mch, g=grp: e.tensor_add(
                            sap(x2t_f, 0, 128, m * 512, [[1, 512]]),
                            ps[m][:, :],
                            sap(xf_t, 0, 128, m * 512, [[1, 512]])),
                            deps=(mmk, m_xf) + ((prev_x2h,) if prev_x2h else ()))
                    m_x2h = T.dma_sy(lambda e, g=grp: e.dma_start(
                        AP(x2_hbm, g * 512, [[SEG, 128], [128 * SEG, FCH], [1, 512]]),
                        sap(x2t_f, 0, 128, 0, [[512, FCH], [1, 512]])), deps=(m_x2,),
                        sem="x2s")
                    prev_x2h = m_x2h
                    m_yb = T.ve(lambda e, g=grp: e.tensor_copy(
                        sap(y_fm, 0, 128, g * 512, [[SEG, FCH], [1, 512]]),
                        sap(x2t_f, 0, 128, 0, [[512, FCH], [1, 512]])), deps=(m_x2,))
                    for ch in range(FCH):
                        mm = T.te(lambda e, ch=ch, g=grp: e.matmul(
                            ps[4][0:1, :], k_onescol[:],
                            sap(y_fm, 0, 128, ch * SEG + g * 512, [[1, 512]]),
                            start=(ch == 0), stop=(ch == FCH - 1)),
                            deps=(m_yb,) if ch == 0 else ())
                    m_sq = T.se(lambda e, g=grp: e.activation(
                        sap(xf_t, 0, 128, 0, [[512, FCH], [1, 512]]),
                        sap(y_fm, 0, 128, g * 512, [[SEG, FCH], [1, 512]]), AF.Square),
                        deps=(m_yb, m_x2h))
                    for ch in range(FCH):
                        mm = T.te(lambda e, ch=ch: e.matmul(
                            ps[5][0:1, :], k_onescolf[:],
                            sap(xf_t, 0, 128, ch * 512, [[1, 512]]),
                            start=(ch == 0), stop=(ch == FCH - 1)),
                            deps=(m_sq,) if ch == 0 else ())
                    m_stat = mm
                    prev_stat2 = m_stat
                    T.ve(lambda e: e.tensor_scalar_mul(AP(row2_f, 0 * 512, [[4 * 512, 1], [1, 512]]), ps[4][0:1, :],
                                                       1.0 / C), deps=(m_stat,))
                    T.ve(lambda e: e.tensor_mul(AP(row2_f, 1 * 512, [[4 * 512, 1], [1, 512]]), AP(row2_f, 0 * 512, [[4 * 512, 1], [1, 512]]),
                                                AP(row2_f, 0 * 512, [[4 * 512, 1], [1, 512]])))
                    m_vr = T.ve(lambda e: e.scalar_tensor_tensor(
                        AP(row2_f, 2 * 512, [[4 * 512, 1], [1, 512]]), ps[5][0:1, :], 1.0 / C, AP(row2_f, 1 * 512, [[4 * 512, 1], [1, 512]]),
                        ALU.mult, ALU.subtract))
                    m_sd = T.se(lambda e: e.activation(AP(row2_f, 3 * 512, [[4 * 512, 1], [1, 512]]), AP(row2_f, 2 * 512, [[4 * 512, 1], [1, 512]]),
                                                       AF.Sqrt, bias=EPS), deps=(m_vr,))
                    T.ve(lambda e: e.reciprocal(AP(row2_f, 2 * 512, [[4 * 512, 1], [1, 512]]), AP(row2_f, 3 * 512, [[4 * 512, 1], [1, 512]])),
                         deps=(m_sd,))
                    T.ve(lambda e: e.tensor_copy(row2_c1[0:1, :], AP(row2_f, 2 * 512, [[4 * 512, 1], [1, 512]])))
                    m_rw = T.ve(lambda e: e.scalar_tensor_tensor(
                        row2_c0[0:1, :], AP(row2_f, 0 * 512, [[4 * 512, 1], [1, 512]]), -1.0, AP(row2_f, 2 * 512, [[4 * 512, 1], [1, 512]]),
                        ALU.mult, ALU.mult))
                    T.te(lambda e: e.matmul(ps[6][:, :], k_onesrow[0:1, :],
                                            row2_c0[0:1, :], start=True, stop=True),
                         deps=(m_rw,))
                    m_bc = T.te(lambda e: e.matmul(ps[7][:, :], k_onesrow[0:1, :],
                                                   row2_c1[0:1, :], start=True, stop=True))
                    T.se(lambda e: e.copy(sap(cb2, 0, 128, 0, [[1, 512]]), ps[6][:, :]),
                         deps=(m_bc,))
                    m_cb = T.se(lambda e: e.copy(sap(cb2, 0, 128, 512, [[1, 512]]),
                                                 ps[7][:, :]))
                    T.ve(lambda e, g=grp: e.tensor_mul(
                        sap(y_fm, 0, 128, g * 512, [[SEG, FCH], [1, 512]]),
                        sap(y_fm, 0, 128, g * 512, [[SEG, FCH], [1, 512]]),
                        sap(cb2, 0, 128, 512, [[0, FCH], [1, 512]])), deps=(m_cb,))
                    m_y = T.ve(lambda e, g=grp: e.tensor_add(
                        sap(y_fm, 0, 128, g * 512, [[SEG, FCH], [1, 512]]),
                        sap(y_fm, 0, 128, g * 512, [[SEG, FCH], [1, 512]]),
                        sap(cb2, 0, 128, 0, [[0, FCH], [1, 512]])))
                    m_yg.append(m_y)
                m_yall = m_yg[-1]
                T.barrier()

                m_yt = None
                for tg in range(16):
                    for ch in range(FCH):
                        m_yt = T.dma_sc(lambda e, tg=tg, ch=ch: e.dma_start(
                            sap(y_tm, 0, 128, tg * C + ch * 128, [[1, 128]]),
                            sap(y_fm, 0, 128, ch * SEG + tg * 128, [[1, 128]]),
                            transpose=True), deps=(m_yall,))

                m_w1x = None
                for wnm, wdst in (("w1aT", w_1a), ("w1bT", w_1b)):
                    dep = (prev_x2h, prev_stat2, m_yall) if m_w1x is None \
                        else (m_w1x,)
                    d_l = T.dma_sy(lambda e, wnm=wnm: e.dma_start(
                        sap(xf_t, 0, 128, 0, [[HFF, FCH], [1, HFF]]),
                        dap(wnm, 0, [[HFF, 128], [128 * HFF, FCH], [1, HFF]])),
                        deps=dep, sem="wA")
                    m_w1x = T.se(lambda e, wdst=wdst: e.copy(
                        sap(wdst, 0, 128, 0, [[1, FCH * HFF]]),
                        sap(xf_t, 0, 128, 0, [[1, FCH * HFF]])),
                        deps=(d_l,))
                d_w1a = d_w1b = m_w1x
                d_w2 = T.dma_ge(lambda e: e.dma_start(
                    sap(w_2, 0, 128, 0, [[SLOTS, 4], [1, SLOTS]]),
                    dap("w2T", 0, [[SLOTS, 128], [128 * SLOTS, 4], [1, SLOTS]])))
                T.dma_sy(lambda e: e.dma_start(s_slots[:], din["slot_init"].ap()))
                T.dma_sy(lambda e: e.dma_start(
                    sap(s_si, 0, SLOTS, 0, [[1, 1]]), din["v_slot_scale"].ap()))
                T.dma_sy(lambda e: e.dma_start(
                    AP(s_tau, 0, [[4, 1], [1, 1]]), din["tau"].ap()))
                T.dma_sy(lambda e: e.dma_start(biash[:, 0:4], din["v_b1"].ap()))
                T.dma_sy(lambda e: e.dma_start(s_b2bc[:], din["b2_rep"].ap()))
                d_slot = d_scale = d_tau = d_b1 = d_b2r = ("syQ", T.semn["syQ"])

                m_S = T.ve(lambda e: e.tensor_scalar_mul(
                    s_slots[:], s_slots[:], sap(s_si, 0, SLOTS, 0, [[1, 1]])),
                    deps=(d_slot, d_scale))
                m_Sb = T.ve(lambda e: e.tensor_copy(s_slotsb[:], s_slots[:]))
                T.te(lambda e: e.matmul(ps[4][0:1, :], k_onescol[0:64, :],
                                        s_slotsb[:, 0:512], start=True, stop=True),
                     deps=(m_Sb,))
                m_sm = T.te(lambda e: e.matmul(ps[5][0:1, :], k_onescol[0:64, :],
                                               s_slotsb[:, 512:1024],
                                               start=True, stop=True))
                T.ve(lambda e: e.tensor_scalar_mul(
                    AP(s_meanb, 0, [[C, 1], [1, 512]]), ps[4][0:1, :], 1.0 / 64),
                    deps=(m_sm,))
                m_smb = T.ve(lambda e: e.tensor_scalar_mul(
                    AP(s_meanb, 512, [[C, 1], [1, 512]]), ps[5][0:1, :], 1.0 / 64))
                for ch in range(FCH):
                    mm = T.te(lambda e, ch=ch: e.matmul(
                        ps[6][:, ch:ch + 1],
                        AP(s_meanb, ch * 128, [[C, 1], [1, 128]]),
                        k_onesrow[0:1, 0:1], start=True, stop=True),
                        deps=(m_smb,) if ch == 0 else ())
                m_mfm = T.ve(lambda e: e.tensor_copy(sap(s_fm64, 0, 128, 0, [[1, FCH]]),
                                                     ps[6][:, 0:FCH]), deps=(mm,))
                mmk = None
                for mh in range(4):
                    for k in range(FCH):
                        dep = (m_mfm, d_w1b) if (k == 0 and mh == 0) else ()
                        mmk = T.te(lambda e, mh=mh, k=k: e.matmul(
                            ps[7][:, mh:mh + 1],
                            sap(w_1b, 0, 128, k * HFF + mh * 128, [[1, 128]]),
                            sap(s_fm64, 0, 128, k, [[1, 1]]),
                            start=(k == 0), stop=(k == FCH - 1)), deps=dep)
                m_bh = T.ve(lambda e: e.tensor_add(biash[:, 0:4], biash[:, 0:4],
                                                   ps[7][:, 0:4]), deps=(mmk, d_b1))

                m_h = None
                for grp in range(4):
                    for mh in range(4):
                        bank = ps[mh % 2]
                        mmk = None
                        for k in range(FCH):
                            dep = ()
                            if k == 0:
                                dep = (m_yall, d_w1a, m_bh)
                                if m_h is not None:
                                    dep = dep + (m_h,)
                            mmk = T.te(lambda e, g=grp, mh=mh, k=k, b=bank: e.matmul(
                                b[:, :],
                                sap(w_1a, 0, 128, k * HFF + mh * 128, [[1, 128]]),
                                sap(y_fm, 0, 128, k * SEG + g * 512, [[1, 512]]),
                                start=(k == 0), stop=(k == FCH - 1)), deps=dep)
                        m_h = T.se(lambda e, g=grp, mh=mh, b=bank: e.activation(
                            sap(s_hfm, 0, 128, mh * SEG + g * 512, [[1, 512]]),
                            b[:, :], AF.Gelu, bias=sap(biash, 0, 128, mh, [[1, 1]])),
                            deps=(mmk,))

                m_t1 = T.se(lambda e: e.activation(AP(s_tau, 1, [[4, 1], [1, 1]]),
                                                   AP(s_tau, 0, [[4, 1], [1, 1]]),
                                                   AF.Abs), deps=(d_tau,))
                T.ve(lambda e: e.tensor_scalar_add(AP(s_tau, 1, [[4, 1], [1, 1]]),
                                                   AP(s_tau, 1, [[4, 1], [1, 1]]),
                                                   0.1), deps=(m_t1,))
                m_t2 = T.ve(lambda e: e.reciprocal(AP(s_tau, 2, [[4, 1], [1, 1]]),
                                                   AP(s_tau, 1, [[4, 1], [1, 1]])))
                m_tmm = T.te(lambda e: e.matmul(ps[6][:, 0:1], k_onesrowf[0:1, :],
                                                AP(s_tau, 2, [[4, 1], [1, 1]]),
                                                start=True, stop=True), deps=(m_t2,))
                m_tb = T.ve(lambda e: e.tensor_copy(s_taub[:], ps[6][:, 0:1]),
                            deps=(m_tmm,))
                m_b2bc = T.ve(lambda e: e.tensor_scalar_mul(
                    s_b2bc[:], s_b2bc[:], sap(s_taub, 0, 128, 0, [[1, 1]])),
                    deps=(m_tb, d_b2r))

                m_sia = None
                prev_lg = None
                prev_ae = None
                m_aes = None
                for tg in range(16):
                    mmk = None
                    for k in range(4):
                        dep = (m_h, d_w2) if (k == 0 and tg == 0) else ()
                        if k == 0 and prev_lg is not None:
                            dep = dep + (prev_lg,)
                        mmk = T.te(lambda e, tg=tg, k=k: e.matmul(
                            ps[2][:, 0:SLOTS],
                            sap(s_hfm, 0, 128, k * SEG + tg * 128, [[1, 128]]),
                            sap(w_2, 0, 128, k * SLOTS, [[1, SLOTS]]),
                            start=(k == 0), stop=(k == 3)), deps=dep)
                    m_lg = T.ve(lambda e: e.scalar_tensor_tensor(
                        s_lgt[:], ps[2][:, 0:SLOTS], sap(s_taub, 0, 128, 0, [[1, 1]]),
                        s_b2bc[:], ALU.mult, ALU.add), deps=(mmk, m_tb, m_b2bc))
                    prev_lg = m_lg
                    m_m8 = T.ve(lambda e: e.max(out=s_m8[:, 0:8], in_=s_lgt[:]),
                                deps=(m_lg,))
                    T.ve(lambda e: e.tensor_scalar_mul(s_m8[:, 8:9], s_m8[:, 0:1],
                                                       -1.0))
                    m_rep = T.ve(lambda e: e.match_replace(out=s_rep[:],
                                                           in_to_replace=s_m8[:, 0:8],
                                                           in_values=s_lgt[:],
                                                           imm_value=-1e30))
                    m_e8 = T.se(lambda e: e.activation(s_m8[:, 0:8], s_m8[:, 0:8],
                                                       AF.Exp, bias=s_m8[:, 8:9]),
                                deps=(m_rep,))
                    m_den = T.ve(lambda e: e.reduce_sum(s_m8[:, 9:10], s_m8[:, 0:8],
                                                        axis=AX.X), deps=(m_e8,))
                    m_rd = T.ve(lambda e: e.reciprocal(s_m8[:, 10:11], s_m8[:, 9:10]))
                    m_el = T.se(lambda e: e.activation(s_el[:], s_lgt[:], AF.Exp,
                                                       bias=s_m8[:, 8:9]), deps=(m_rd,))
                    m_msk = T.ve(lambda e: e.tensor_tensor(s_msk[:], s_lgt[:], s_rep[:],
                                                           ALU.not_equal), deps=(m_el,))
                    T.ve(lambda e: e.tensor_mul(s_el[:], s_el[:], s_msk[:]))
                    m_al = T.ve(lambda e, tg=tg: e.tensor_scalar_mul(
                        sap(s_alpha, 0, 128, tg * SLOTS, [[1, SLOTS]]), s_el[:],
                        sap(s_m8, 0, 128, 10, [[1, 1]])))
                    m_at = T.te(lambda e, tg=tg: e.matmul(
                        ps[3][0:64, 0:128],
                        sap(s_alpha, 0, 128, tg * SLOTS, [[1, SLOTS]]), k_id[:],
                        start=True, stop=True),
                        deps=(m_al,) + ((prev_ae,) if prev_ae else ()))
                    m_ae = T.se(lambda e, tg=tg: e.copy(
                        sap(s_alpsm, 0, SLOTS, tg * 128, [[1, 128]]),
                        ps[3][0:64, 0:128]), deps=(m_at,))
                    prev_ae = m_ae
                    m_aes = m_ae
                    dep0 = (m_al, m_yt) if tg == 0 else (m_al,)
                    T.te(lambda e, tg=tg: e.matmul(
                        ps[0][0:64, :],
                        sap(s_alpha, 0, 128, tg * SLOTS, [[1, SLOTS]]),
                        sap(y_tm, 0, 128, tg * C, [[1, 512]]),
                        start=(tg == 0), stop=(tg == 15)), deps=dep0)
                    T.te(lambda e, tg=tg: e.matmul(
                        ps[1][0:64, :],
                        sap(s_alpha, 0, 128, tg * SLOTS, [[1, SLOTS]]),
                        sap(y_tm, 0, 128, tg * C + 512, [[1, 512]]),
                        start=(tg == 0), stop=(tg == 15)))
                    m_sia = T.te(lambda e, tg=tg: e.matmul(
                        ps[4][0:64, 0:1],
                        sap(s_alpha, 0, 128, tg * SLOTS, [[1, SLOTS]]),
                        k_onescol[:], start=(tg == 0), stop=(tg == 15)))
                T.se(lambda e: e.copy(sap(s_si, 0, SLOTS, 0, [[1, 512]]),
                                      ps[0][0:64, :]), deps=(m_sia,))
                T.se(lambda e: e.copy(sap(s_si, 0, SLOTS, 512, [[1, 512]]),
                                      ps[1][0:64, :]))
                m_sie = T.se(lambda e: e.copy(sap(s_si, 0, SLOTS, C, [[1, 1]]),
                                              ps[4][0:64, 0:1]))
                T.barrier()
                m_x2all = ("x2s", T.semn["x2s"])

            m_z16 = T.ve(lambda e: e.memset(s_si[:, C + 1:C + 16], 0.0))
            m_ccw = T.dma_ge(lambda e: e.dma_start(cc_in, s_si[:, 0:C + 16]),
                             deps=(m_z16,))
            m_cc = T._rec("ge", lambda e: e.collective_compute(
                "AllReduce", ALU.add, ins=[cc_in], outs=[cc_out],
                replica_groups=[list(range(NCORES))]), "cc", 1, (m_ccw,))
            m_ccr = T.dma_ge(lambda e: e.dma_start(s_si[:, 0:C + 16], cc_out),
                             deps=(m_cc,))

            s_gx = sb2("s_gx", (SLOTS, GDIM), F32)
            s_gh = sb2("s_gh", (SLOTS, GDIM), F32)
            s_gate2 = sb2("s_gate2", (SLOTS, 2 * C), F32)
            s_gtmp = sb2("s_gtmp", (SLOTS, C), F32)
            s_snew = sb2("s_snew", (SLOTS, C), F32)
            s_snb = sb2("s_snb", (SLOTS, C), BF16)
            s_gib = sb2("s_gib", (SLOTS, C), BF16)
            s_sv = sb2("s_sv", (SLOTS, C), F32)
            s_svb = sb2("s_svb", (SLOTS, C), BF16)
            s_svw = sb2("s_svw", (SLOTS, C), BF16)
            s_bc = sb2("s_bc", (SLOTS, GDIM), F32)

            T.ve(lambda e: e.tensor_scalar_add(sap(s_si, 0, SLOTS, C + 1, [[1, 1]]),
                                               sap(s_si, 0, SLOTS, C, [[1, 1]]),
                                               1e-8), deps=(m_ccr,))
            m_rw2 = T.ve(lambda e: e.reciprocal(sap(s_si, 0, SLOTS, C + 2, [[1, 1]]),
                                                sap(s_si, 0, SLOTS, C + 1, [[1, 1]])))
            m_gib = T.ve(lambda e: e.tensor_scalar_mul(
                s_gib[:], sap(s_si, 0, SLOTS, 0, [[1, C]]),
                sap(s_si, 0, SLOTS, C + 2, [[1, 1]])), deps=(m_rw2,))

            prev_gwc = None
            m_gfm = None
            for ch in range(FCH):
                mm = T.te(lambda e, ch=ch: e.matmul(
                    ps[4][:, 0:64], s_gib[:, ch * 128:(ch + 1) * 128],
                    k_id[0:64, 0:64], start=True, stop=True),
                    deps=(m_gib,) if ch == 0 else (m_gfm,))
                m_gfm = T.se(lambda e, ch=ch: e.copy(
                    sap(s_fm64, 0, 128, ch * 64, [[1, 64]]), ps[4][:, 0:64]),
                    deps=(mm,))
            m_hfm = None
            for ch in range(FCH):
                mm = T.te(lambda e, ch=ch: e.matmul(
                    ps[5][:, 0:64], s_slotsb[:, ch * 128:(ch + 1) * 128],
                    k_id[0:64, 0:64], start=True, stop=True),
                    deps=(m_gib,) if ch == 0 else (m_hfm,))
                m_hfm = T.se(lambda e, ch=ch: e.copy(
                    sap(s_fm64b, 0, 128, ch * 64, [[1, 64]]), ps[5][:, 0:64]),
                    deps=(mm,))

            d_bih = T.dma_sy(lambda e: e.dma_start(s_bc[:], din["bih_rep"].ap()))
            m_gx = None
            for n6 in range(6):
                d_wl = T.dma_sy(lambda e, n6=n6: e.dma_start(
                    sap(xf_t, 0, 128, 0, [[512, FCH], [1, 512]]),
                    dap("gru_wihT", n6 * 512,
                        [[GDIM, 128], [128 * GDIM, FCH], [1, 512]])),
                    deps=((m_gx,) if m_gx else ()) + ((prev_gwc,) if prev_gwc else ()),
                    sem="wA")
                d_ws = prev_gwc = T.se(lambda e: e.copy(
                    sap(w_gs, 0, 128, 0, [[1, FCH * 512]]),
                    sap(xf_t, 0, 128, 0, [[1, FCH * 512]])), deps=(d_wl,))
                mmk = None
                for k in range(FCH):
                    dep = (d_ws, m_gfm) if k == 0 else ()
                    mmk = T.te(lambda e, k=k: e.matmul(
                        ps[0][0:64, :], sap(s_fm64, 0, 128, k * 64, [[1, 64]]),
                        sap(w_gs, 0, 128, k * 512, [[1, 512]]),
                        start=(k == 0), stop=(k == FCH - 1)), deps=dep)
                m_gx = T.ve(lambda e, n6=n6: e.tensor_add(
                    sap(s_gx, 0, SLOTS, n6 * 512, [[1, 512]]), ps[0][0:64, :],
                    sap(s_bc, 0, SLOTS, n6 * 512, [[1, 512]])), deps=(mmk, d_bih))
            d_bhh = T.dma_sy(lambda e: e.dma_start(s_bc[:], din["bhh_rep"].ap()),
                             deps=(m_gx,))
            m_gh = None
            for n6 in range(6):
                d_wl = T.dma_sy(lambda e, n6=n6: e.dma_start(
                    sap(xf_t, 0, 128, 0, [[512, FCH], [1, 512]]),
                    dap("gru_whhT", n6 * 512,
                        [[GDIM, 128], [128 * GDIM, FCH], [1, 512]])),
                    deps=((m_gh,) if m_gh else (m_gx,)) + ((prev_gwc,) if prev_gwc else ()),
                    sem="wA")
                d_ws = prev_gwc = T.se(lambda e: e.copy(
                    sap(w_gs, 0, 128, 0, [[1, FCH * 512]]),
                    sap(xf_t, 0, 128, 0, [[1, FCH * 512]])), deps=(d_wl,))
                mmk = None
                for k in range(FCH):
                    dep = (d_ws, m_hfm) if k == 0 else ()
                    mmk = T.te(lambda e, k=k: e.matmul(
                        ps[0][0:64, :], sap(s_fm64b, 0, 128, k * 64, [[1, 64]]),
                        sap(w_gs, 0, 128, k * 512, [[1, 512]]),
                        start=(k == 0), stop=(k == FCH - 1)), deps=dep)
                m_gh = T.ve(lambda e, n6=n6: e.tensor_add(
                    sap(s_gh, 0, SLOTS, n6 * 512, [[1, 512]]), ps[0][0:64, :],
                    sap(s_bc, 0, SLOTS, n6 * 512, [[1, 512]])), deps=(mmk, d_bhh))

            m_rz0 = T.ve(lambda e: e.tensor_add(s_gate2[:, 0:2048], s_gx[:, 0:2048],
                                                s_gh[:, 0:2048]), deps=(m_gh, m_gx))
            m_rz = T.se(lambda e: e.activation(s_gate2[:, 0:2048],
                                               s_gate2[:, 0:2048], AF.Sigmoid),
                        deps=(m_rz0,))
            T.ve(lambda e: e.tensor_mul(s_gtmp[:], s_gate2[:, 0:1024],
                                        s_gh[:, 2048:3072]), deps=(m_rz,))
            m_n0 = T.ve(lambda e: e.tensor_add(s_gtmp[:], s_gtmp[:],
                                               s_gx[:, 2048:3072]))
            m_n = T.se(lambda e: e.activation(s_gtmp[:], s_gtmp[:], AF.Tanh),
                       deps=(m_n0,))
            T.ve(lambda e: e.tensor_sub(s_snew[:], s_slots[:], s_gtmp[:]),
                 deps=(m_n, m_S))
            T.ve(lambda e: e.tensor_mul(s_snew[:], s_snew[:], s_gate2[:, 1024:2048]))
            m_sn = T.ve(lambda e: e.tensor_add(s_snew[:], s_snew[:], s_gtmp[:]))
            m_snb = T.ve(lambda e: e.tensor_copy(s_snb[:], s_snew[:]))
            m_sfm = None
            for ch in range(FCH):
                mm = T.te(lambda e, ch=ch: e.matmul(
                    ps[4][:, 0:64], s_snb[:, ch * 128:(ch + 1) * 128],
                    k_id[0:64, 0:64], start=True, stop=True),
                    deps=(m_snb,) if ch == 0 else (m_sfm,))
                m_sfm = T.se(lambda e, ch=ch: e.copy(
                    sap(s_fm64, 0, 128, ch * 64, [[1, 64]]), ps[4][:, 0:64]),
                    deps=(mm,))

            wv_cs = None
            for kch in range(FCH):
                d_l = T.dma_sy(lambda e, k=kch: e.dma_start(
                    sap(xf_t, 0, 128, (k % 2) * 1024, [[1, C]]),
                    dap("wvT", k * 128 * C, [[C, 128], [1, C]])),
                    deps=((prev_gwc,) if kch == 0 else ()) +
                    ((wv_cs,) if kch >= 2 else ()), sem=("wA", "wB")[kch % 2])
                wv_cs = T.se(lambda e, k=kch: e.copy(
                    sap(w_big, 0, 128, k * C, [[1, C]]),
                    sap(xf_t, 0, 128, (k % 2) * 1024, [[1, C]])), deps=(d_l,))
            d_wv = wv_cs
            d_bv = T.dma_sy(lambda e: e.dma_start(
                sap(s_bc, 0, SLOTS, 0, [[1, C]]), din["bv_rep"].ap()),
                deps=(m_gh,))
            m_sv = None
            for n2 in range(2):
                mmk = None
                for k in range(FCH):
                    dep = (d_wv, m_sfm) if (k == 0 and n2 == 0) else ()
                    if k == 0 and m_sv is not None:
                        dep = dep + (m_sv,)
                    mmk = T.te(lambda e, k=k, n2=n2: e.matmul(
                        ps[0][0:64, :], sap(s_fm64, 0, 128, k * 64, [[1, 64]]),
                        sap(w_big, 0, 128, k * C + n2 * 512, [[1, 512]]),
                        start=(k == 0), stop=(k == FCH - 1)), deps=dep)
                m_sv = T.ve(lambda e, n2=n2: e.tensor_add(
                    sap(s_sv, 0, SLOTS, n2 * 512, [[1, 512]]), ps[0][0:64, :],
                    sap(s_bc, 0, SLOTS, n2 * 512, [[1, 512]])), deps=(mmk, d_bv))
            m_svb = T.ve(lambda e: e.tensor_copy(s_svb[:], s_sv[:]), deps=(m_sv,))
            m_svfm = None
            for ch in range(FCH):
                mm = T.te(lambda e, ch=ch: e.matmul(
                    ps[4][:, 0:64], s_svb[:, ch * 128:(ch + 1) * 128],
                    k_id[0:64, 0:64], start=True, stop=True),
                    deps=(m_svb,) if ch == 0 else (m_svfm,))
                m_svfm = T.se(lambda e, ch=ch: e.copy(
                    sap(s_fm64b, 0, 128, ch * 64, [[1, 64]]), ps[4][:, 0:64]),
                    deps=(mm,))
            wo2_cs = None
            for kch in range(FCH):
                d_l = T.dma_sy(lambda e, k=kch: e.dma_start(
                    sap(xf_t, 0, 128, (k % 2) * 1024, [[1, C]]),
                    dap("woT", k * 128 * C, [[C, 128], [1, C]])),
                    deps=((m_sv,) if kch == 0 else ()) +
                    ((wo2_cs,) if kch >= 2 else ()), sem=("wA", "wB")[kch % 2])
                wo2_cs = T.se(lambda e, k=kch: e.copy(
                    sap(w_big, 0, 128, k * C, [[1, C]]),
                    sap(xf_t, 0, 128, (k % 2) * 1024, [[1, C]])), deps=(d_l,))
            d_wo2 = wo2_cs
            m_svw = None
            for n2 in range(2):
                mmk = None
                for k in range(FCH):
                    dep = (d_wo2, m_svfm) if (k == 0 and n2 == 0) else ()
                    if k == 0 and m_svw is not None:
                        dep = dep + (m_svw,)
                    mmk = T.te(lambda e, k=k, n2=n2: e.matmul(
                        ps[0][0:64, :], sap(s_fm64b, 0, 128, k * 64, [[1, 64]]),
                        sap(w_big, 0, 128, k * C + n2 * 512, [[1, 512]]),
                        start=(k == 0), stop=(k == FCH - 1)), deps=dep)
                m_svw = T.se(lambda e, n2=n2: e.copy(
                    sap(s_svw, 0, SLOTS, n2 * 512, [[1, 512]]), ps[0][0:64, :]),
                    deps=(mmk,))

            m_fin = None
            out_dmas = []
            finl = []
            for grp in range(4):
                m_xf2 = T.dma_sy(lambda e, g=grp: e.dma_start(
                    sap(xf_t, 0, 128, 0, [[512, FCH], [1, 512]]),
                    AP(x2_hbm, g * 512, [[SEG, 128], [128 * SEG, FCH], [1, 512]])),
                    deps=((m_fin,) if m_fin else (m_x2all, wo2_cs)))
                for mch in range(FCH):
                    bank = ps[1 + mch % 4]
                    slot = (grp * FCH + mch) % 2
                    mdep = (m_svw, m_aes) if (mch == 0 and grp == 0) else ()
                    if len(finl) >= 4:
                        mdep = mdep + (finl[-4],)
                    mmk = T.te(lambda e, m=mch, g=grp, b=bank: e.matmul(
                        b[:, :], s_svw[:, m * 128:(m + 1) * 128],
                        sap(s_alpsm, 0, SLOTS, g * 512, [[1, 512]]),
                        start=True, stop=True),
                        deps=mdep)
                    findep = [mmk, m_xf2, vdeps["bo"]]
                    if len(out_dmas) >= 2:
                        findep.append(out_dmas[-2])
                    m_fin = T.ve(lambda e, m=mch, b=bank, sl=slot:
                                 e.scalar_tensor_tensor(
                        sap(s_ofin, 0, 128, sl * 512, [[1, 512]]),
                        b[:, :], sap(vec["bo"], 0, 128, m, [[1, 1]]),
                        sap(xf_t, 0, 128, m * 512, [[1, 512]]), ALU.add, ALU.add),
                        deps=tuple(findep))
                    finl.append(m_fin)
                    out_dmas.append(T.dma_out(lambda e, m=mch, g=grp, sl=slot:
                                              e.dma_start(
                        AP(out_fm, m * 128 * SEG + g * 512, [[SEG, 128], [1, 512]]),
                        sap(s_ofin, 0, 128, sl * 512, [[1, 512]])), deps=(m_fin,),
                        sem=("oA", "oB")[slot]))
            T._rec("ge", lambda e: e.nop(), None, 0,
                   (("oA", T.semn["oA"]), ("oB", T.semn["oB"])))

        with nc.allow_low_precision(reason="bf16 elementwise validated against problem tolerance"):
            with nc.Block() as block:
                T.finalize(block)

    return nc


def kernel(**inputs):
    per_core = _host_prepare(inputs)
    nc = build_program()
    res = run_bass_kernel_spmd(nc, per_core, core_ids=list(range(NCORES)))
    outs = [r["out_fm"] for r in res.results]
    full = np.concatenate([o.T for o in outs], axis=0)
    return np.ascontiguousarray(full[None], dtype=np.float32)
